# version 09
# baseline (speedup 1.0000x reference)
import numpy as np
import ml_dtypes
import concourse.bass as bass
import concourse.mybir as mybir
from concourse.bass_utils import run_bass_kernel_spmd

F32 = mybir.dt.float32
BF16 = mybir.dt.bfloat16
U8 = mybir.dt.uint8
AF = mybir.ActivationFunctionType
ALU = mybir.AluOpType
AX = mybir.AxisListType
BFD = ml_dtypes.bfloat16

T_SEQ = 2048
DM = 1024
NH = 16
NEG_BIG = -1.0e30
NSL = 4


class _Rec:
    __slots__ = ("eng", "dma", "thunk", "deps", "marked", "value", "waits",
                 "seq", "tag", "lane", "lane_pred")

    def __init__(self, eng, dma, thunk, seq, tag):
        self.eng = eng
        self.dma = dma
        self.thunk = thunk
        self.deps = []
        self.marked = False
        self.value = None
        self.waits = None
        self.seq = seq
        self.tag = tag
        self.lane = None
        self.lane_pred = None


class _Buf:
    __slots__ = ("name", "writers", "readers")

    def __init__(self, name):
        self.name = name
        self.writers = []
        self.readers = []

    def absorb(self, others):
        for o in others:
            self.writers = self.writers + o.writers + o.readers


class _Sched:
    ENGINES = ("tensor", "vector", "scalar", "gpsimd", "sync")
    NLANE = 8

    def __init__(self):
        self.recs = {e: [] for e in self.ENGINES}
        self.nseq = 0
        self.dma_count = {e: 0 for e in self.ENGINES}
        self.lane_last = {}

    def op(self, eng, thunk, reads=(), writes=(), pwrites=(), dma=False, tag=""):
        r = _Rec(eng, dma, thunk, self.nseq, tag)
        self.nseq += 1
        if dma:
            lane = self.dma_count[eng] % self.NLANE
            self.dma_count[eng] += 1
            r.lane = lane
            r.lane_pred = self.lane_last.get((eng, lane))
            self.lane_last[(eng, lane)] = r
        deps = {}
        for b in reads:
            for d in b.writers:
                deps[id(d)] = d
        for b in tuple(writes) + tuple(pwrites):
            for d in b.writers:
                deps[id(d)] = d
            for d in b.readers:
                deps[id(d)] = d
        if dma and r.lane_pred is not None:
            deps[id(r.lane_pred)] = r.lane_pred
        for d in deps.values():
            if d is r:
                continue
            if (not d.dma) and (not dma) and d.eng == eng == "tensor":
                continue
            assert d.seq < r.seq, (d.tag, r.tag)
            r.deps.append(d)
        for b in reads:
            b.readers.append(r)
        for b in writes:
            b.writers = [r]
            b.readers = []
        for b in pwrites:
            b.writers.append(r)
        self.recs[eng].append(r)
        return r

    def finalize(self):
        for lst in self.recs.values():
            for r in lst:
                for d in r.deps:
                    d.marked = True
                if r.dma:
                    r.marked = True
        for lst in self.recs.values():
            cnt = {}
            for r in lst:
                if r.marked:
                    key = ("d", r.lane) if r.dma else "c"
                    cnt[key] = cnt.get(key, 0) + 1
                    r.value = cnt[key] * (16 if r.dma else 1)
        for lst in self.recs.values():
            hw = {}
            for r in lst:
                ws = []
                for d in r.deps:
                    key = (d.eng, d.lane) if d.dma else (d.eng, None)
                    assert d.value is not None
                    if hw.get(key, 0) < d.value:
                        ws.append((key, d.value))
                        hw[key] = d.value
                ws.sort(key=lambda kv: kv[1])
                r.waits = ws

    def fin(self, bufs, tag="fin"):
        r = self.op("sync", lambda e: e.nop(), reads=tuple(bufs), tag=tag)
        have = {id(d) for d in r.deps}
        for rec in self.lane_last.values():
            if id(rec) not in have:
                r.deps.append(rec)
                have.add(id(rec))
        return r

    def run(self, eng_name, eng, sems):
        for r in self.recs[eng_name]:
            for key, val in r.waits:
                eng.wait_ge(sems[key], val)
            ins = r.thunk(eng)
            if r.marked:
                assert ins is not None, r.tag
                key = (eng_name, r.lane) if r.dma else (eng_name, None)
                ins.then_inc(sems[key], 16 if r.dma else 1)


def _host_constants():
    c = {}
    c["ident_f32"] = np.eye(128, dtype=np.float32)
    i_of_r = (np.arange(128) % 64) // 2
    freqs = np.exp(-np.log(10000.0) * i_of_r / 32.0)
    t = np.arange(T_SEQ, dtype=np.float64)
    ang = freqs[:, None] * t[None, :]
    cos = np.cos(ang)
    sin = np.sin(ang)
    sgn = np.where(np.arange(128) % 2 == 0, -1.0, 1.0)[:, None]
    c["Cq"] = (0.125 * cos).astype(BFD)
    c["Sq"] = (0.125 * sgn * sin).astype(BFD)
    tq = np.arange(128)[:, None]
    cw = np.arange(384)[None, :]
    valid = (cw > tq + 6) & (cw <= tq + 256)
    c["maskT"] = np.where(valid, 0.0, NEG_BIG).astype(np.float32)
    c["neg1"] = np.full((128, 1), -1.0, np.float32)
    c["onesc2"] = np.full((1, 128), 2.0, np.float32)
    return c


_CONST_SPECS = [
    ("ident_f32", (128, 128), F32),
    ("Cq", (128, 2048), BF16), ("Sq", (128, 2048), BF16),
    ("maskT", (128, 384), F32),
    ("neg1", (128, 1), F32), ("onesc2", (1, 128), F32),
]


def build_nc(debug=()):
    nc = bass.Bass("TRN2", target_bir_lowering=False, debug=False)
    sch = _Sched()

    dx = nc.dram_tensor("x", [T_SEQ, DM], F32, kind="ExternalInput")
    dwqk = nc.dram_tensor("wqk", [16, 128, 1024], F32, kind="ExternalInput")
    dwv = nc.dram_tensor("wv", [8, 128, 1024], F32, kind="ExternalInput")
    dwo = nc.dram_tensor("wo", [8, 128, 1024], F32, kind="ExternalInput")
    dk0a = nc.dram_tensor("k0a", [8, 128, 4096], U8, kind="ExternalInput")
    dk0b = nc.dram_tensor("k0b", [8, 128, 4096], U8, kind="ExternalInput")
    dk1 = nc.dram_tensor("k1", [32, 128, 4096], U8, kind="ExternalInput")
    dk2 = nc.dram_tensor("k2", [32, 128, 1024], U8, kind="ExternalInput")
    dsc = [nc.dram_tensor(f"s{j}", [1, 1], F32, kind="ExternalInput")
           for j in range(3)]
    dconst = {name: nc.dram_tensor("c_" + name, list(shp), dt, kind="ExternalInput")
              for name, shp, dt in _CONST_SPECS}
    dout = nc.dram_tensor("out", [T_SEQ, DM], F32, kind="ExternalOutput")
    dwqkb = nc.dram_tensor("wqkb", [16, 128, 1024], BF16, kind="Internal")
    dwvb = nc.dram_tensor("wvb", [8, 128, 1024], BF16, kind="Internal")
    dwob = nc.dram_tensor("wob", [8, 128, 1024], BF16, kind="Internal")
    dw0s = nc.dram_tensor("w0s", [8, 128, 4096], BF16, kind="Internal")
    dw1s = nc.dram_tensor("w1s", [32, 128, 4096], BF16, kind="Internal")
    dw2s = nc.dram_tensor("w2s", [32, 128, 1024], BF16, kind="Internal")

    ddbg = {}
    for name, shp, dt in [
        ("xn_f", [128, 8192], BF16), ("q_f", [128, 8192], BF16),
        ("k_f", [128, 8192], BF16), ("v_t", [128, 8192], BF16),
        ("s_t", [128, 4096], BF16), ("t1", [128, 16384], BF16),
        ("t2", [128, 16384], BF16), ("af", [128, 6144], BF16),
    ]:
        if name in debug:
            ddbg[name] = nc.dram_tensor("dbg_" + name, shp, dt,
                                        kind="ExternalOutput")

    alloc_log = []

    def sba(name, shape, dtype):
        t = nc.alloc_sbuf_tensor(name, list(shape), dtype)
        alloc_log.append((name, int(np.prod(shape[1:])) * mybir.dt.size(dtype)))
        return t

    sb_idf = sba("ident_f32", [128, 128], F32)
    sb_mask = sba("maskT", [128, 384], F32)
    sb_neg1 = sba("neg1", [128, 1], F32)
    sb_oc2 = sba("onesc2", [1, 128], F32)
    sb_sca = sba("scalars", [1, 4], F32)
    sb_sb2 = sba("s012b2", [128, 4], F32)
    sb_lnst = sba("lnst", [128, 8], F32)
    sb_mng = sba("mng", [128, NSL], F32)
    sb_mng8 = sba("mng8", [128, NSL], F32)
    sb_smx = sba("smx", [128, NSL], F32)
    sb_rec = sba("rec", [128, NSL], F32)
    sb_Cq = sba("Cq", [128, 1024], BF16)
    sb_Sq = sba("Sq", [128, 1024], BF16)
    sb_Pe = sba("Pe", [128, 2 * 384], F32)
    sb_PeT = sba("PeT", [128, 2 * 384], BF16)
    sb_xn = sba("xn_f", [128, 8192], BF16)
    sb_q = sba("q_f", [128, 8192], BF16)
    sb_k = sba("k_f", [128, 8192], BF16)
    sb_v = sba("v_t", [128, 8192], BF16)
    sb_t1 = sba("t1", [128, 16384], BF16)
    sb_t2 = sba("t2", [128, 16384], BF16)
    sb_af = sba("af", [128, 3 * 2048], BF16)
    sb_s = sba("s_t", [128, 4096], BF16)
    sb_slab = [sba(f"slab{i}", [128, 2048], BF16) for i in range(3)]
    sb_wqkr = [sba(f"wqkr{i}", [128, 1024], BF16) for i in range(2)]
    sb_wor = [sba(f"wor{i}", [128, 1024], BF16) for i in range(2)]
    sb_sqd = sba("sqdump", [128, 1024], F32)
    sb_gb = sba("gb", [128, 512], F32)
    sb_pb = sba("pb", [128, 512], F32)
    sb_ob = [sba(f"outb{i}", [128, 1024], F32) for i in range(4)]

    used = sum(v for _, v in alloc_log)
    assert used <= 212832, (used, sorted(alloc_log, key=lambda kv: -kv[1])[:8])

    a_t1 = nc.lookup_mloc(sb_t1).addr
    sb_xb = [nc.alloc_sbuf_tensor_at(f"xbuf{i}", [128, 1024], F32,
                                     offset=a_t1 + i * 4096) for i in range(2)]
    a_t2 = nc.lookup_mloc(sb_t2).addr
    sb_cvF = [nc.alloc_sbuf_tensor_at(f"cvF{i}", [128, 1024], F32,
                                      offset=a_t2 + i * 4096) for i in range(2)]
    sb_cvB = [nc.alloc_sbuf_tensor_at(f"cvB{i}", [128, 1024], BF16,
                                      offset=a_t2 + 8192 + i * 2048)
              for i in range(2)]
    sb_cvA = [nc.alloc_sbuf_tensor_at(f"cvA{i}", [128, 2048], BF16,
                                      offset=a_t2 + i * 4096) for i in range(2)]
    sb_cvC = [nc.alloc_sbuf_tensor_at(f"cvC{i}", [128, 2048], BF16,
                                      offset=a_t2 + 8192 + i * 4096)
              for i in range(2)]
    a_sq = nc.lookup_mloc(sb_sqd).addr
    sb_swp = nc.alloc_sbuf_tensor_at("swp", [128, 512], F32, offset=a_sq)
    sb_tr1 = nc.alloc_sbuf_tensor_at("tr1", [128, 512], F32, offset=a_sq + 2048)
    a_gb = nc.lookup_mloc(sb_gb).addr
    sb_cvW = [nc.alloc_sbuf_tensor_at(f"cvW{i}", [128, 1024], BF16,
                                      offset=a_gb + i * 2048) for i in range(2)]

    psf = [nc.alloc_psum_tensor(f"psf{i}", [128, 512], F32) for i in range(4)]
    pa = [nc.alloc_psum_tensor(f"pa{i}", [128, 512], F32) for i in range(NSL)]

    B = _Buf
    b_const = B("const")
    b_cs = B("cs_tables")
    b_sca = B("sca")
    b_sb2 = B("sb2")
    b_lnst = B("lnst")
    b_sqd = B("sqd")
    b_swp = B("swp")
    b_tr1 = B("tr1")
    b_xb = [B(f"xb{i}") for i in range(2)]
    b_xnf = [[B(f"xnf{c}_{n}") for n in range(2)] for c in range(8)]
    b_q = [[B(f"q{c}_{n}") for n in range(2)] for c in range(8)]
    b_k = [[B(f"k{c}_{n}") for n in range(2)] for c in range(8)]
    b_v = [B(f"v{t}") for t in range(8)]
    b_t1 = [B(f"t1_{m}") for m in range(32)]
    b_t2 = [B(f"t2_{m}") for m in range(32)]
    b_af = [B(f"af{s}") for s in range(3)]
    b_s = [B(f"s{m}") for m in range(8)]
    b_slab = [B(f"slab{i}") for i in range(3)]
    b_wqkr = [B(f"wqkr{i}") for i in range(2)]
    b_wor = [B(f"wor{i}") for i in range(2)]
    b_gb = B("gb")
    b_pb = B("pb")
    b_ob = [B(f"ob{i}") for i in range(4)]
    b_psf = [B(f"psf{i}") for i in range(4)]
    b_pa = [B(f"paS{i}") for i in range(NSL)]
    b_pav = [B(f"paV{i}") for i in range(NSL)]
    b_Pe = [B(f"Pe{i}") for i in range(2)]
    b_PeT = [B(f"PeT{i}") for i in range(2)]
    b_mng = [B(f"mng{i}") for i in range(NSL)]
    b_mng8 = [B(f"mng8{i}") for i in range(NSL)]
    b_smx = [B(f"smx{i}") for i in range(NSL)]
    b_rec = [B(f"rec{i}") for i in range(NSL)]
    b_cvF = [B(f"cvF{i}") for i in range(2)]
    b_cvB = [B(f"cvB{i}") for i in range(2)]
    b_cvA = [B(f"cvA{i}") for i in range(2)]
    b_cvC = [B(f"cvC{i}") for i in range(2)]
    b_cvW = [B(f"cvW{i}") for i in range(2)]
    b_wqkb = [B(f"wqkb{m}") for m in range(16)]
    b_wvb = [B(f"wvb{c}") for c in range(8)]
    b_wob = [B(f"wob{c}") for c in range(8)]
    b_w0s = [[B(f"w0s{c}_{h}") for h in range(2)] for c in range(8)]
    b_w1s = [[B(f"w1s{c}_{h}") for h in range(2)] for c in range(32)]
    b_w2s = [B(f"w2s{c}") for c in range(32)]
    b_out = B("out_dram")
    b_dbg = B("dbg")

    def dma(q, dst, src, reads=(), writes=(), pwrites=(), tag=""):
        return sch.op(q, lambda e, d=dst, s=src: e.dma_start(d, s),
                      reads=reads, writes=writes, pwrites=pwrites, dma=True,
                      tag=tag)

    def mm(out, lhsT, rhs, start, stop, reads=(), writes=(), pwrites=(), tag=""):
        return sch.op("tensor",
                      lambda e, o=out, l=lhsT, r=rhs, st=start, sp=stop:
                      e.matmul(o, l, r, start=st, stop=sp),
                      reads=reads, writes=writes, pwrites=pwrites, tag=tag)

    def transp(out, inp, ident, reads=(), writes=(), pwrites=(), tag=""):
        return sch.op("tensor",
                      lambda e, o=out, i=inp, d=ident: e.transpose(o, i, d),
                      reads=reads, writes=writes, pwrites=pwrites, tag=tag)

    bank_tog = [0]

    def next_banks(n):
        bank_tog[0] ^= 1
        base = bank_tog[0] * 2
        return list(range(base, base + n))

    slab_i = [0]

    def next_slab():
        slab_i[0] = (slab_i[0] + 1) % 3
        return sb_slab[slab_i[0]], b_slab[slab_i[0]]

    sb_konst = dict(ident_f32=sb_idf, maskT=sb_mask, neg1=sb_neg1,
                    onesc2=sb_oc2)
    dma("sync", sb_xb[0][:], dx.ap()[0:128, :], writes=(b_xb[0],), tag="ldx0")
    dma("sync", sb_xb[1][:], dx.ap()[128:256, :], writes=(b_xb[1],), tag="ldx1")
    for name in ("ident_f32", "maskT", "neg1", "onesc2"):
        dma("sync", sb_konst[name][:], dconst[name].ap(), pwrites=(b_const,),
            tag=f"ld_{name}")
    for tb, dst in (("Cq", sb_Cq), ("Sq", sb_Sq)):
        dma("sync", dst[:], dconst[tb].ap()[:, 0:1024], pwrites=(b_cs,),
            tag=f"ld_{tb}0")
    for j in range(3):
        dma("sync", sb_sca[0:1, j:j + 1], dsc[j].ap(), pwrites=(b_sca,),
            tag=f"ld_s{j}")
    for j in range(3):
        mm(psf[0][:, j:j + 1], sb_oc2[:, :], sb_sca[0:1, j:j + 1], True, True,
           reads=(b_const, b_sca),
           writes=(b_psf[0],) if j == 0 else (), pwrites=(b_psf[0],) if j else (),
           tag=f"bc_s{j}")
    sch.op("vector", lambda e: e.tensor_copy(sb_sb2[:, 0:3], psf[0][:, 0:3]),
           reads=(b_psf[0],), writes=(b_sb2,), tag="ev_s012")

    conv_steps = []

    def conv_cast(dsrc, ddst, n, bufs, tagp):
        for m in range(n):
            i = m % 2

            def s1(m=m, i=i):
                dma("sync", sb_cvF[i][:], dsrc.ap()[m], writes=(b_cvF[i],),
                    tag=f"{tagp}l{m}")

            def s2(m=m, i=i):
                sch.op("scalar", lambda e, i=i: e.copy(sb_cvB[i][:], sb_cvF[i][:]),
                       reads=(b_cvF[i],), writes=(b_cvB[i],), tag=f"{tagp}c{m}")

            def s3(m=m, i=i):
                dma("sync", ddst.ap()[m], sb_cvB[i][:], reads=(b_cvB[i],),
                    writes=(bufs[m],), tag=f"{tagp}s{m}")
            conv_steps.extend([s1, s2, s3])

    def conv_w0(kc, h):
        i = (kc * 2 + h) % 2

        def s1(kc=kc, h=h, i=i):
            dma("gpsimd", sb_cvA[i][:], dk0a.ap()[kc, :, h * 2048:(h + 1) * 2048],
                writes=(b_cvA[i],), tag=f"w0a{kc}_{h}")

        def s1b(kc=kc, h=h, i=i):
            dma("gpsimd", sb_cvC[i][:], dk0b.ap()[kc, :, h * 2048:(h + 1) * 2048],
                writes=(b_cvC[i],), tag=f"w0b{kc}_{h}")

        def s2(i=i):
            sch.op("vector", lambda e, i=i: e.tensor_sub(
                sb_cvA[i][:], sb_cvA[i][:], sb_cvC[i][:]),
                   reads=(b_cvA[i], b_cvC[i]), writes=(b_cvA[i],),
                   tag=f"w0sub{i}")

        def s3(kc=kc, h=h, i=i):
            dma("sync", dw0s.ap()[kc, :, h * 2048:(h + 1) * 2048], sb_cvA[i][:],
                reads=(b_cvA[i],), writes=(b_w0s[kc][h],), tag=f"w0w{kc}_{h}")
        conv_steps.extend([s1, s1b, s2, s3])

    def conv_w1(kc, h):
        i = (kc * 2 + h) % 2

        def s1(kc=kc, h=h, i=i):
            dma("gpsimd", sb_cvA[i][:], dk1.ap()[kc, :, h * 2048:(h + 1) * 2048],
                writes=(b_cvA[i],), tag=f"w1l{kc}_{h}")

        def s2(i=i):
            sch.op("vector", lambda e, i=i: e.tensor_scalar(
                sb_cvA[i][:], sb_cvA[i][:], 2.0, -1.0, op0=ALU.mult, op1=ALU.add),
                   reads=(b_cvA[i],), writes=(b_cvA[i],), tag=f"w1a{i}")

        def s3(kc=kc, h=h, i=i):
            dma("sync", dw1s.ap()[kc, :, h * 2048:(h + 1) * 2048], sb_cvA[i][:],
                reads=(b_cvA[i],), writes=(b_w1s[kc][h],), tag=f"w1w{kc}_{h}")
        conv_steps.extend([s1, s2, s3])

    def conv_w2(kc):
        i = kc % 2

        def s1(kc=kc, i=i):
            dma("gpsimd", sb_cvW[i][:], dk2.ap()[kc], writes=(b_cvW[i],),
                tag=f"w2l{kc}")

        def s2(i=i):
            sch.op("vector", lambda e, i=i: e.tensor_scalar(
                sb_cvW[i][:], sb_cvW[i][:], 2.0, -1.0, op0=ALU.mult, op1=ALU.add),
                   reads=(b_cvW[i],), writes=(b_cvW[i],), tag=f"w2a{i}")

        def s3(kc=kc, i=i):
            dma("sync", dw2s.ap()[kc], sb_cvW[i][:], reads=(b_cvW[i],),
                writes=(b_w2s[kc],), tag=f"w2w{kc}")
        conv_steps.extend([s1, s2, s3])

    def _cv_to_AC():
        b_cvA[0].absorb([b_cvF[0]])
        b_cvA[1].absorb([b_cvF[1]])
        b_cvC[0].absorb([b_cvB[0], b_cvB[1]])
        b_cvC[1].absorb([b_cvB[0], b_cvB[1]])

    def _cv_to_FB():
        b_cvF[0].absorb([b_cvA[0]])
        b_cvF[1].absorb([b_cvA[1]])
        b_cvB[0].absorb([b_cvC[0]])
        b_cvB[1].absorb([b_cvC[0]])

    conv_cast(dwqk, dwqkb, 16, b_wqkb, "cq")
    conv_cast(dwv, dwvb, 8, b_wvb, "cv")
    conv_steps.append(_cv_to_AC)
    for kc in range(8):
        for h in range(2):
            conv_w0(kc, h)
    for kc in range(32):
        conv_w1(kc, 0)
    for kc in range(32):
        conv_w1(kc, 1)
    conv_steps.append(_cv_to_FB)
    conv_cast(dwo, dwob, 8, b_wob, "co")
    for kc in range(32):
        conv_w2(kc)
    conv_iter = iter(conv_steps)

    def emit_conv(k):
        for _ in range(k):
            step = next(conv_iter, None)
            if step is None:
                return
            step()

    xs = sb_lnst[:, 0:1]
    mu = sb_lnst[:, 1:2]
    vs = sb_lnst[:, 2:3]
    v1 = sb_lnst[:, 3:4]
    ln1 = sb_lnst[:, 4:5]
    inv = sb_lnst[:, 5:6]
    xnf_r = sb_xn[:].rearrange("p (c n) -> p c n", c=8)

    def ln_tile(seg, tl):
        xb = sb_xb[tl % 2]
        bxb = b_xb[tl % 2]
        gt = seg * 8 + tl
        if gt >= 2:
            dma("sync", xb[:], dx.ap()[gt * 128:(gt + 1) * 128, :], writes=(bxb,),
                tag=f"ldx{gt}")
        sch.op("vector", lambda e, xb=xb: e.reduce_sum(xs, xb[:], axis=AX.X),
               reads=(bxb,), writes=(b_lnst,), tag=f"ln_sum{gt}")
        sch.op("vector", lambda e: e.tensor_scalar_mul(mu, xs, 1.0 / DM),
               reads=(b_lnst,), pwrites=(b_lnst,), tag=f"ln_mu{gt}")
        sch.op("vector", lambda e, xb=xb: e.tensor_scalar(
            xb[:], xb[:], mu, None, op0=ALU.subtract),
               reads=(b_lnst,), writes=(bxb,), tag=f"ln_sub{gt}")
        sch.op("scalar", lambda e, xb=xb: e.activation(
            sb_sqd[:], xb[:], AF.Square, accum_out=vs),
               reads=(bxb,), writes=(b_sqd,), pwrites=(b_lnst,), tag=f"ln_sq{gt}")
        sch.op("vector", lambda e: e.tensor_scalar(
            v1, vs, 1.0 / DM, 1e-5, op0=ALU.mult, op1=ALU.add),
               reads=(b_lnst,), pwrites=(b_lnst,), tag=f"ln_v1{gt}")
        sch.op("scalar", lambda e: e.activation(ln1, v1, AF.Ln),
               reads=(b_lnst,), pwrites=(b_lnst,), tag=f"ln_ln{gt}")
        sch.op("scalar", lambda e: e.activation(inv, ln1, AF.Exp, scale=-0.5),
               reads=(b_lnst,), pwrites=(b_lnst,), tag=f"ln_inv{gt}")
        sch.op("vector", lambda e, xb=xb: e.tensor_scalar_mul(xb[:], xb[:], inv),
               reads=(b_lnst,), writes=(bxb,), tag=f"ln_fin{gt}")
        banks = next_banks(2)
        for g in range(2):
            bk, bb = psf[banks[g]], b_psf[banks[g]]
            for qq in range(4):
                cchunk = g * 4 + qq
                transp(bk[:, qq * 128:(qq + 1) * 128],
                       xb[:, cchunk * 128:(cchunk + 1) * 128], sb_idf[:],
                       reads=(bxb, b_const),
                       writes=(bb,) if qq == 0 else (),
                       pwrites=(bb,) if qq else (),
                       tag=f"tp{gt}_{cchunk}")
            dst = xnf_r[:, g * 4:(g + 1) * 4, tl * 128:(tl + 1) * 128]
            src = bk[:].rearrange("p (c n) -> p c n", c=4)
            sch.op("scalar", lambda e, d=dst, s=src: e.copy(d, s),
                   reads=(bb,),
                   pwrites=tuple(b_xnf[g * 4 + c][tl // 4] for c in range(4)),
                   tag=f"tpev{gt}_{g}")

    swmask = []
    for i in range(16):
        swmask += [2 * i + 1, 2 * i]
    ip_done = set()

    def ip_round(seg, m, nt):
        ip_done.add((seg, m, nt))
        wr, bwr = sb_wqkr[m % 2], b_wqkr[m % 2]
        dma("sync", wr[:], dwqkb.ap()[m], reads=(b_wqkb[m],), writes=(bwr,),
            tag=f"ldwqk{seg}_{m}_{nt}")
        nb = next_banks(1)[0]
        bk, bb = psf[nb], b_psf[nb]
        for kc in range(8):
            mm(bk[:, 0:512], wr[:, kc * 128:(kc + 1) * 128],
               sb_xn[:, kc * 1024 + nt * 512: kc * 1024 + nt * 512 + 512],
               kc == 0, kc == 7,
               reads=(bwr, b_xnf[kc][nt]),
               writes=(bb,) if kc == 0 else (), pwrites=(bb,) if kc else (),
               tag=f"qk{seg}_{m}_{nt}_{kc}")
        isq = m < 8
        c = m if isq else m - 8
        dstt = sb_q if isq else sb_k
        bdst = b_q[c][nt] if isq else b_k[c][nt]
        cs = slice(nt * 512, nt * 512 + 512)
        sch.op("vector", lambda e, bk=bk: e.stream_shuffle(
            sb_swp[:], bk[:, 0:512], swmask),
               reads=(bb,), writes=(b_swp,), tag=f"rp_sh{seg}_{m}_{nt}")
        sch.op("vector", lambda e, bk=bk, cs=cs: e.tensor_mul(
            sb_tr1[:], bk[:, 0:512], sb_Cq[:, cs]),
               reads=(bb, b_cs), writes=(b_tr1,), tag=f"rp_c{seg}_{m}_{nt}")
        sch.op("vector", lambda e, cs=cs: e.tensor_mul(
            sb_swp[:], sb_swp[:], sb_Sq[:, cs]),
               reads=(b_swp, b_cs), writes=(b_swp,), tag=f"rp_s{seg}_{m}_{nt}")
        dst = dstt[:, c * 1024 + nt * 512: c * 1024 + nt * 512 + 512]
        sch.op("vector", lambda e, d=dst: e.tensor_add(d, sb_tr1[:], sb_swp[:]),
               reads=(b_tr1, b_swp), writes=(bdst,), tag=f"rp_o{seg}_{m}_{nt}")

    v_done = set()

    def v_round(seg, pr, og):
        v_done.add((seg, pr, og))
        slabs = []
        for q4 in range(2):
            slab, bslab = next_slab()
            src = dwvb.ap()[q4 * 4:(q4 + 1) * 4, :, og * 512:(og + 1) * 512]
            src = src.rearrange("c p f -> p c f")
            dst = slab[:, 0:2048].rearrange("p (c f) -> p c f", c=4)
            dma("sync", dst, src,
                reads=tuple(b_wvb[q4 * 4:(q4 + 1) * 4]), writes=(bslab,),
                tag=f"vld{seg}_{pr}_{og}_{q4}")
            slabs.append((slab, bslab))
        banks = next_banks(2)
        for ti in range(2):
            tt = pr * 2 + ti
            bk, bb = psf[banks[ti]], b_psf[banks[ti]]
            for kc in range(8):
                slab, bslab = slabs[kc // 4]
                mm(bk[:, 0:512],
                   sb_xn[:, kc * 1024 + tt * 128: kc * 1024 + tt * 128 + 128],
                   slab[:, (kc % 4) * 512:(kc % 4) * 512 + 512],
                   kc == 0, kc == 7,
                   reads=(b_xnf[kc][tt // 4], bslab),
                   writes=(bb,) if kc == 0 else (), pwrites=(bb,) if kc else (),
                   tag=f"v{seg}_{tt}_{og}_{kc}")
            dst = sb_v[:, tt * 1024 + og * 512: tt * 1024 + og * 512 + 512]
            sch.op("scalar", lambda e, d=dst, bk=bk: e.copy(d, bk[:, 0:512]),
                   reads=(bb,), pwrites=(b_v[tt],), tag=f"vev{seg}_{tt}_{og}")

    def f0_round(seg, nt, r):
        slab, bslab = next_slab()
        src = dw0s.ap()[:, :, r * 256:(r + 1) * 256].rearrange("c p f -> p c f")
        dst = slab[:, 0:2048].rearrange("p (c f) -> p c f", c=8)
        dma("sync", dst, src, reads=tuple(b_w0s[kc][r // 8] for kc in range(8)),
            writes=(bslab,), tag=f"f0ld{seg}_{nt}_{r}")
        banks = next_banks(2)
        for kc in range(8):
            for widx in range(2):
                bb = b_psf[banks[widx]]
                mm(psf[banks[widx]][:, 0:512],
                   slab[:, kc * 256 + widx * 128: kc * 256 + widx * 128 + 128],
                   sb_xn[:, kc * 1024 + nt * 512: kc * 1024 + nt * 512 + 512],
                   kc == 0, kc == 7,
                   reads=(bslab, b_xnf[kc][nt]),
                   writes=(bb,) if kc == 0 else (), pwrites=(bb,) if kc else (),
                   tag=f"f0mm{seg}_{nt}_{r}_{kc}_{widx}")
        for widx in range(2):
            m1 = 2 * r + widx
            bk = banks[widx]
            dst2 = sb_t1[:, m1 * 512:(m1 + 1) * 512]
            sch.op("scalar", lambda e, d=dst2, bk=bk: e.activation(
                d, psf[bk][:, 0:512], AF.Relu, scale=sb_sb2[:, 0:1]),
                   reads=(b_psf[bk], b_sb2), writes=(b_t1[m1],),
                   tag=f"f0ev{seg}_{nt}_{m1}")

    def f1_round(seg, nt, r):
        banks = next_banks(2)
        for g4 in range(4):
            slab, bslab = next_slab()
            src = dw1s.ap()[g4 * 8:(g4 + 1) * 8, :, r * 256:(r + 1) * 256]
            src = src.rearrange("c p f -> p c f")
            dst = slab[:, 0:2048].rearrange("p (c f) -> p c f", c=8)
            dma("sync", dst, src,
                reads=tuple(b_w1s[g4 * 8 + j][r // 8] for j in range(8)),
                writes=(bslab,), tag=f"f1ld{seg}_{nt}_{r}_{g4}")
            for kci in range(8):
                kc = g4 * 8 + kci
                for widx in range(2):
                    bb = b_psf[banks[widx]]
                    mm(psf[banks[widx]][:, 0:512],
                       slab[:, kci * 256 + widx * 128: kci * 256 + widx * 128 + 128],
                       sb_t1[:, kc * 512:(kc + 1) * 512],
                       kc == 0, kc == 31,
                       reads=(bslab, b_t1[kc]),
                       writes=(bb,) if kc == 0 else (), pwrites=(bb,) if kc else (),
                       tag=f"f1mm{seg}_{nt}_{r}_{kc}_{widx}")
        for widx in range(2):
            m2 = 2 * r + widx
            bk = banks[widx]
            dst = sb_t2[:, m2 * 512:(m2 + 1) * 512]
            sch.op("scalar", lambda e, d=dst, bk=bk: e.activation(
                d, psf[bk][:, 0:512], AF.Relu, scale=sb_sb2[:, 1:2]),
                   reads=(b_psf[bk], b_sb2), writes=(b_t2[m2],),
                   tag=f"f1ev{seg}_{nt}_{m2}")

    def af_slot(seg, hp):
        return (seg * 4 + hp) % 3

    def c_round(seg, nt, r):
        banks = next_banks(2)
        for widx in range(2):
            mo = 2 * r + widx
            wr, bwr = sb_wor[mo % 2], b_wor[mo % 2]
            dma("sync", wr[:], dwob.ap()[mo], reads=(b_wob[mo],), writes=(bwr,),
                tag=f"ldwo{seg}_{nt}_{mo}")
            bk, bb = psf[banks[widx]], b_psf[banks[widx]]
            for g in range(2):
                sl = af_slot(seg, nt * 2 + g)
                for kc in range(8):
                    mm(bk[:, g * 256: g * 256 + 256],
                       wr[:, kc * 128:(kc + 1) * 128],
                       sb_af[:, sl * 2048 + kc * 256: sl * 2048 + kc * 256 + 256],
                       kc == 0 and g == 0, kc == 7 and g == 1,
                       reads=(bwr, b_af[sl]),
                       writes=(bb,) if (kc == 0 and g == 0) else (),
                       pwrites=(bb,) if (kc or g) else (),
                       tag=f"op{seg}_{nt}_{mo}_{g}_{kc}")
            dst = sb_s[:, mo * 512:(mo + 1) * 512]
            sch.op("scalar", lambda e, d=dst, bk=bk: e.copy(d, bk[:, 0:512]),
                   reads=(bb,), writes=(b_s[mo],), tag=f"opev{seg}_{nt}_{mo}")

    def ob_loads(seg, nt):
        for j in range(4):
            gt = seg * 8 + nt * 4 + j
            dma("sync", sb_ob[j][:], dx.ap()[gt * 128:(gt + 1) * 128, :],
                writes=(b_ob[j],), tag=f"ldxo{seg}_{nt}_{j}")

    def f2_round(seg, nt, r):
        banks = next_banks(2)
        for g4 in range(4):
            slab, bslab = next_slab()
            src = dw2s.ap()[g4 * 8:(g4 + 1) * 8, :, r * 256:(r + 1) * 256]
            src = src.rearrange("c p f -> p c f")
            dst = slab[:, 0:2048].rearrange("p (c f) -> p c f", c=8)
            dma("sync", dst, src,
                reads=tuple(b_w2s[g4 * 8 + j] for j in range(8)),
                writes=(bslab,), tag=f"f2ld{seg}_{nt}_{r}_{g4}")
            for kci in range(8):
                kc = g4 * 8 + kci
                for widx in range(2):
                    bb = b_psf[banks[widx]]
                    mm(psf[banks[widx]][:, 0:512],
                       slab[:, kci * 256 + widx * 128: kci * 256 + widx * 128 + 128],
                       sb_t2[:, kc * 512:(kc + 1) * 512],
                       kc == 0, kc == 31,
                       reads=(bslab, b_t2[kc]),
                       writes=(bb,) if kc == 0 else (), pwrites=(bb,) if kc else (),
                       tag=f"f2mm{seg}_{nt}_{r}_{kc}_{widx}")
        for widx in range(2):
            m3 = 2 * r + widx
            bk = banks[widx]
            sch.op("scalar", lambda e, bk=bk: e.activation(
                sb_gb[:], psf[bk][:, 0:512], AF.Silu, scale=sb_sb2[:, 2:3]),
                   reads=(b_psf[bk], b_sb2), writes=(b_gb,),
                   tag=f"silu{seg}_{nt}_{m3}")
            ssl = sb_s[:, m3 * 512:(m3 + 1) * 512]
            sch.op("vector", lambda e, ssl=ssl: e.tensor_mul(
                sb_pb[:], sb_gb[:], ssl),
                   reads=(b_gb, b_s[m3]), writes=(b_pb,),
                   tag=f"prod{seg}_{nt}_{m3}")
            for j in range(4):
                transp(psf[bk][:, j * 128:(j + 1) * 128],
                       sb_pb[:, j * 128:(j + 1) * 128], sb_idf[:],
                       reads=(b_pb, b_const),
                       writes=(b_psf[bk],) if j == 0 else (),
                       pwrites=(b_psf[bk],) if j else (),
                       tag=f"ptr{seg}_{nt}_{m3}_{j}")
            for j in range(4):
                osl = sb_ob[j][:, m3 * 128:(m3 + 1) * 128]
                sch.op("vector",
                       lambda e, osl=osl, bk=bk, j=j: e.tensor_add(
                           osl, psf[bk][:, j * 128:(j + 1) * 128], osl),
                       reads=(b_psf[bk], b_ob[j]), pwrites=(b_ob[j],),
                       tag=f"radd{seg}_{nt}_{m3}_{j}")

    def ob_stores(seg, nt):
        for j in range(4):
            gt = seg * 8 + nt * 4 + j
            dma("sync", dout.ap()[gt * 128:(gt + 1) * 128, :], sb_ob[j][:],
                reads=(b_ob[j],), pwrites=(b_out,), tag=f"stout{seg}_{nt}_{j}")

    def runs_of(il, jj0):
        slots = [(il - 2 + jj) % 8 for jj in range(jj0, 3)]
        runs = []
        st = 0
        for j in range(1, len(slots)):
            if slots[j] != slots[j - 1] + 1:
                runs.append((jj0 + st, slots[st], j - st))
                st = j
        runs.append((jj0 + st, slots[st], len(slots) - st))
        return runs

    class Attn:
        def __init__(self, seg):
            self.seg = seg
            self.it = 0

        def pump(self, n):
            for _ in range(n):
                for lag, fn in STAGES:
                    u = self.it - lag
                    if 0 <= u < 128:
                        fn(self, u)
                self.it += 1

        def drain(self):
            if self.it < 128 + 5:
                self.pump(128 + 5 - self.it)

        def ctx(self, u):
            il, h = u // 16, u % 16
            gi = self.seg * 8 + il
            jj0 = max(0, 2 - gi)
            return il, h, gi, jj0

    def st_mm(a, u):
        il, h, gi, jj0 = a.ctx(u)
        c, off = h // 2, (h % 2) * 64
        s4 = u % NSL
        assert (a.seg, c, il // 4) in ip_done, ("q not ready", a.seg, u)
        bank, bb = pa[s4], b_pa[s4]
        runs = runs_of(il, jj0)
        for jj in range(jj0, 3):
            sl = (il - 2 + jj) % 8
            if il - 2 + jj >= 0:
                assert (a.seg, 8 + c, sl // 4) in ip_done, ("k", a.seg, u, sl)
            else:
                assert (a.seg, 8 + c, sl // 4) not in ip_done, \
                    ("k overwritten", a.seg, u, sl)
        first = True
        for jjs, sl0, ln in runs:
            kread = [b_k[c][0 if sl0 < 4 else 1]]
            if sl0 + ln - 1 >= 4 and sl0 < 4:
                kread.append(b_k[c][1])
            mm(bank[:, jjs * 128: (jjs + ln) * 128],
               sb_q[off:off + 64, c * 1024 + il * 128: c * 1024 + il * 128 + 128],
               sb_k[off:off + 64, c * 1024 + sl0 * 128: c * 1024 + (sl0 + ln) * 128],
               first, (jjs, sl0, ln) == runs[-1],
               reads=tuple([b_q[c][il // 4]] + kread),
               writes=(bb, b_pav[s4]) if first else (),
               pwrites=(bb,) if not first else (),
               tag=f"sq{a.seg}_{il}_{h}")
            first = False

    def st_msk(a, u):
        il, h, gi, jj0 = a.ctx(u)
        s4 = u % NSL
        bank, bb = pa[s4], b_pa[s4]
        sch.op("vector", lambda e, bank=bank, jj0=jj0: e.tensor_add(
            bank[:, jj0 * 128:384], bank[:, jj0 * 128:384],
            sb_mask[:, jj0 * 128:384]),
               reads=(bb, b_const), pwrites=(bb,), tag=f"msk{a.seg}_{il}_{h}")

    def st_max(a, u):
        il, h, gi, jj0 = a.ctx(u)
        s4 = u % NSL
        bank, bb = pa[s4], b_pa[s4]
        sch.op("vector", lambda e, bank=bank, jj0=jj0, s4=s4: e.reduce_max(
            sb_mng[:, s4:s4 + 1], bank[:, jj0 * 128:384], axis=AX.X,
            negate=True),
               reads=(bb,), writes=(b_mng[s4],), tag=f"mx{a.seg}_{il}_{h}")

    def st_m8(a, u):
        s4 = u % NSL
        sch.op("vector", lambda e, s4=s4: e.tensor_scalar_mul(
            sb_mng8[:, s4:s4 + 1], sb_mng[:, s4:s4 + 1], 8.0),
               reads=(b_mng[s4],), writes=(b_mng8[s4],), tag=f"m8_{a.seg}_{u}")

    def st_exp(a, u):
        il, h, gi, jj0 = a.ctx(u)
        s4, s2 = u % NSL, u % 2
        bank, bb = pa[s4], b_pa[s4]
        pes = sb_Pe[:, s2 * 384 + jj0 * 128: s2 * 384 + 384]
        sch.op("scalar", lambda e, pes=pes, bank=bank, jj0=jj0, s4=s4:
               e.activation(pes, bank[:, jj0 * 128:384], AF.Exp, scale=8.0,
                            bias=sb_mng8[:, s4:s4 + 1],
                            accum_out=sb_smx[:, s4:s4 + 1]),
               reads=(bb, b_mng8[s4]),
               writes=(b_Pe[s2], b_smx[s4]), tag=f"exp{a.seg}_{il}_{h}")

    def st_rcp(a, u):
        s4 = u % NSL
        sch.op("vector", lambda e, s4=s4: e.reciprocal(
            sb_rec[:, s4:s4 + 1], sb_smx[:, s4:s4 + 1]),
               reads=(b_smx[s4],), writes=(b_rec[s4],), tag=f"rcp{a.seg}_{u}")

    def st_pnorm(a, u):
        il, h, gi, jj0 = a.ctx(u)
        s4, s2 = u % NSL, u % 2
        pes = sb_Pe[:, s2 * 384 + jj0 * 128: s2 * 384 + 384]
        sch.op("vector", lambda e, pes=pes, s4=s4: e.tensor_scalar_mul(
            pes, pes, sb_rec[:, s4:s4 + 1]),
               reads=(b_rec[s4],), pwrites=(b_Pe[s2],),
               tag=f"pn{a.seg}_{il}_{h}")

    def st_tp(a, u):
        il, h, gi, jj0 = a.ctx(u)
        s4, s2 = u % NSL, u % 2
        bank, bb = pa[s4], b_pa[s4]
        for jj in range(jj0, 3):
            transp(bank[:, jj * 128:(jj + 1) * 128],
                   sb_Pe[:, s2 * 384 + jj * 128: s2 * 384 + jj * 128 + 128],
                   sb_idf[:],
                   reads=(b_Pe[s2], b_const),
                   writes=(bb,) if jj == jj0 else (),
                   pwrites=(bb,) if jj != jj0 else (),
                   tag=f"pt{a.seg}_{il}_{h}_{jj}")

    def st_ptev(a, u):
        il, h, gi, jj0 = a.ctx(u)
        s4, s2 = u % NSL, u % 2
        bank, bb = pa[s4], b_pa[s4]
        pets = sb_PeT[:, s2 * 384 + jj0 * 128: s2 * 384 + 384]
        sch.op("scalar", lambda e, pets=pets, bank=bank, jj0=jj0: e.copy(
            pets, bank[:, jj0 * 128:384]),
               reads=(bb,), writes=(b_PeT[s2],), tag=f"ptev{a.seg}_{il}_{h}")

    def st_av(a, u):
        il, h, gi, jj0 = a.ctx(u)
        s4, s2 = u % NSL, u % 2
        bank = pa[s4]
        po = (h % 2) * 64
        first = True
        for jj in range(jj0, 3):
            jt = (il - 2 + jj) % 8
            if il - 2 + jj >= 0:
                assert (a.seg, jt // 2, 0) in v_done and \
                    (a.seg, jt // 2, 1) in v_done, ("v", a.seg, u, jt)
            else:
                assert (a.seg, jt // 2, 0) not in v_done and \
                    (a.seg, jt // 2, 1) not in v_done, ("v ow", a.seg, u, jt)
            mm(bank[po:po + 64, 384:512],
               sb_v[:, jt * 1024 + h * 64: jt * 1024 + h * 64 + 64],
               sb_PeT[:, s2 * 384 + jj * 128: s2 * 384 + jj * 128 + 128],
               first, jj == 2,
               reads=(b_PeT[s2], b_v[jt]),
               writes=(b_pav[s4],) if first else (),
               pwrites=(b_pav[s4],) if not first else (),
               tag=f"av{a.seg}_{il}_{h}_{jj}")
            first = False

    def st_aev(a, u):
        il, h, gi, jj0 = a.ctx(u)
        s4 = u % NSL
        bank = pa[s4]
        po = (h % 2) * 64
        sl = af_slot(a.seg, il // 2)
        base = sl * 2048 + (h // 2) * 256 + (il % 2) * 128
        dst = sb_af[po:po + 64, base: base + 128]
        sch.op("scalar", lambda e, d=dst, bank=bank, po=po: e.copy(
            d, bank[po:po + 64, 384:512]),
               reads=(b_pav[s4],), pwrites=(b_af[sl],),
               tag=f"aev{a.seg}_{il}_{h}")

    STAGES = [(4, st_ptev), (4, st_av), (4, st_aev),
              (3, st_rcp), (3, st_pnorm), (3, st_tp),
              (2, st_exp), (1, st_msk), (1, st_max), (1, st_m8),
              (0, st_mm)]

    for tl in range(8):
        ln_tile(0, tl)
        emit_conv(6)
    for m1 in range(8):
        b_t1[m1].absorb([b_xb[0], b_xb[1]])
    b_swp.absorb([b_sqd])
    b_tr1.absorb([b_sqd])
    for nt in range(2):
        for m in range(16):
            ip_round(0, m, nt)
            emit_conv(3)
    for og in range(2):
        for pr in range(4):
            v_round(0, pr, og)
            emit_conv(4)
    at0 = Attn(0)
    for r in range(16):
        f0_round(0, 0, r)
        at0.pump(2)
        emit_conv(4)
    while at0.it < 72:
        at0.pump(1)
        emit_conv(4)
    for r in range(4):
        c_round(0, 0, r)
        at0.pump(2)
        emit_conv(4)
    while at0.it < 96:
        at0.pump(1)
        emit_conv(4)
    for m2 in range(32):
        b_t2[m2].absorb([b_cvF[0], b_cvF[1], b_cvB[0], b_cvB[1],
                         b_cvA[0], b_cvA[1], b_cvC[0], b_cvC[1]])
    for r in range(16):
        f1_round(0, 0, r)
        at0.pump(2)
        emit_conv(6)
    at0.drain()
    emit_conv(10 ** 9)
    b_gb.absorb([b_cvW[0], b_cvW[1]])
    b_pb.absorb([b_cvW[0], b_cvW[1]])
    if "xn_f" in ddbg:
        dma("sync", ddbg["xn_f"].ap(), sb_xn[:],
            reads=tuple(b for row in b_xnf for b in row), pwrites=(b_dbg,),
            tag="dbg_xn")
    if "q_f" in ddbg:
        dma("sync", ddbg["q_f"].ap(), sb_q[:],
            reads=tuple(b for row in b_q for b in row), pwrites=(b_dbg,),
            tag="dbg_q")
    if "k_f" in ddbg:
        dma("sync", ddbg["k_f"].ap(), sb_k[:],
            reads=tuple(b for row in b_k for b in row), pwrites=(b_dbg,),
            tag="dbg_k")
    if "v_t" in ddbg:
        dma("sync", ddbg["v_t"].ap(), sb_v[:], reads=tuple(b_v),
            pwrites=(b_dbg,), tag="dbg_v")
    if "s_t" in ddbg:
        dma("sync", ddbg["s_t"].ap(), sb_s[:], reads=tuple(b_s),
            pwrites=(b_dbg,), tag="dbg_s")
    if "t1" in ddbg:
        dma("sync", ddbg["t1"].ap(), sb_t1[:], reads=tuple(b_t1),
            pwrites=(b_dbg,), tag="dbg_t1")
    if "af" in ddbg:
        dma("sync", ddbg["af"].ap(), sb_af[:], reads=tuple(b_af),
            pwrites=(b_dbg,), tag="dbg_af")
    ob_loads(0, 0)
    for r in range(4):
        f2_round(0, 0, r)
    ob_stores(0, 0)
    if "t2" in ddbg:
        dma("sync", ddbg["t2"].ap(), sb_t2[:], reads=tuple(b_t2),
            pwrites=(b_dbg,), tag="dbg_t2")
    for r in range(16):
        f0_round(0, 1, r)
    for r in range(4):
        c_round(0, 1, r)
    for r in range(16):
        f1_round(0, 1, r)
    ob_loads(0, 1)
    for tb, dst in (("Cq", sb_Cq), ("Sq", sb_Sq)):
        dma("sync", dst[:], dconst[tb].ap()[:, 1024:2048], writes=(b_cs,),
            tag=f"ld_{tb}1")

    def _absorb_sqd():
        b_sqd.absorb([b_swp, b_tr1])

    def _absorb_rope():
        b_swp.absorb([b_sqd])
        b_tr1.absorb([b_sqd])

    def _absorb_xb():
        for i in range(2):
            b_xb[i].absorb([b_t1[c] for c in range(8)])
    tail = [_absorb_sqd, _absorb_xb]
    for tl in range(4):
        tail.append(lambda tl=tl: ln_tile(1, tl))
    tail.append(_absorb_rope)
    for m in range(16):
        tail.append(lambda m=m: ip_round(1, m, 0))
    tail.append(_absorb_sqd)
    for tl in range(4, 8):
        tail.append(lambda tl=tl: ln_tile(1, tl))
    tail.append(_absorb_rope)
    for m in range(8):
        tail.append(lambda m=m: ip_round(1, m, 1))
    for og in range(2):
        for pr in range(2):
            tail.append(lambda pr=pr, og=og: v_round(1, pr, og))
    ti = iter(tail)
    for r in range(4):
        f2_round(0, 1, r)
        for _ in range(4 if r == 0 else 12):
            step = next(ti, None)
            if step is not None:
                step()
    for step in ti:
        step()
    ob_stores(0, 1)
    for m1 in range(8):
        b_t1[m1].absorb([b_xb[0], b_xb[1]])
    at1 = Attn(1)
    deferred = []
    for m in range(8, 16):
        deferred.append(lambda m=m: ip_round(1, m, 1))
    for pr in range(2, 4):
        for og in range(2):
            deferred.append(lambda pr=pr, og=og: v_round(1, pr, og))
    di = iter(deferred)
    for r in range(16):
        f0_round(1, 0, r)
        at1.pump(2)
    for r in range(16):
        f1_round(1, 0, r)
        at1.pump(4)
        if at1.it >= 40:
            for _ in range(2):
                step = next(di, None)
                if step is not None:
                    step()
        if r >= 12:
            c_round(1, 0, r - 12)
    for step in di:
        step()
    ob_loads(1, 0)
    for r in range(4):
        f2_round(1, 0, r)
        at1.pump(2)
    ob_stores(1, 0)
    for r in range(16):
        f0_round(1, 1, r)
        at1.pump(1)
    for r in range(16):
        f1_round(1, 1, r)
        at1.pump(2)
    at1.drain()
    for r in range(4):
        c_round(1, 1, r)
    ob_loads(1, 1)
    for r in range(4):
        f2_round(1, 1, r)
    ob_stores(1, 1)

    sch.fin([b_out, b_dbg])
    return _emit(nc, sch)


def _emit(nc, sch):
    sch.finalize()
    sems = {}
    for eng in _Sched.ENGINES:
        sems[(eng, None)] = nc.alloc_semaphore(f"sem_{eng}")
        for ln in range(_Sched.NLANE):
            sems[(eng, ln)] = nc.alloc_semaphore(f"sem_{eng}_d{ln}")

    with nc.Block() as block:
        @block.tensor
        def _(e):
            sch.run("tensor", e, sems)

        @block.vector
        def _(e):
            sch.run("vector", e, sems)

        @block.scalar
        def _(e):
            sch.run("scalar", e, sems)

        @block.gpsimd
        def _(e):
            sch.run("gpsimd", e, sems)

        @block.sync
        def _(e):
            sch.run("sync", e, sems)

    return nc


def _prep(L0_attn_query, L0_attn_in_proj_w, L0_attn_out_proj_w,
          L0_ffn_k0, L0_ffn_k1, L0_ffn_k2, L0_ffn_s0, L0_ffn_s1, L0_ffn_s2):
    x = np.ascontiguousarray(np.asarray(L0_attn_query, np.float32))
    win = np.ascontiguousarray(np.asarray(L0_attn_in_proj_w, np.float32))
    wout = np.ascontiguousarray(np.asarray(L0_attn_out_proj_w, np.float32))
    wqk = np.ascontiguousarray(
        win[:2048].T.reshape(8, 128, 16, 128).transpose(2, 1, 0, 3)
        .reshape(16, 128, 1024))
    wv = np.ascontiguousarray(win[2048:].T.reshape(8, 128, 1024))
    wo = np.ascontiguousarray(
        wout.T.reshape(8, 128, 8, 128).transpose(2, 1, 0, 3)
        .reshape(8, 128, 1024))
    k0 = np.ascontiguousarray(np.asarray(L0_ffn_k0)).view(np.uint8)
    k0a = np.ascontiguousarray(k0[:1024].reshape(8, 128, 4096))
    k0b = np.ascontiguousarray(k0[1024:].reshape(8, 128, 4096))
    k1 = np.ascontiguousarray(np.asarray(L0_ffn_k1)).view(np.uint8)
    k1 = k1.reshape(32, 128, 4096)
    k2 = np.ascontiguousarray(np.asarray(L0_ffn_k2)).view(np.uint8)
    k2 = k2.reshape(32, 128, 1024)
    shared = dict(wqk=wqk, wv=wv, wo=wo, k0a=k0a, k0b=k0b, k1=k1, k2=k2,
                  s0=np.asarray(L0_ffn_s0, np.float32).reshape(1, 1),
                  s1=np.asarray(L0_ffn_s1, np.float32).reshape(1, 1),
                  s2=np.asarray(L0_ffn_s2, np.float32).reshape(1, 1))
    for name, arr in _host_constants().items():
        shared["c_" + name] = np.ascontiguousarray(arr)
    return x, shared


def kernel(L0_attn_query, L0_attn_in_proj_w, L0_attn_out_proj_w,
           L0_ffn_k0, L0_ffn_k1, L0_ffn_k2, L0_ffn_s0, L0_ffn_s1, L0_ffn_s2):
    x, shared = _prep(L0_attn_query, L0_attn_in_proj_w, L0_attn_out_proj_w,
                      L0_ffn_k0, L0_ffn_k1, L0_ffn_k2,
                      L0_ffn_s0, L0_ffn_s1, L0_ffn_s2)
    nc = build_nc()
    in_maps = [dict(shared, x=np.ascontiguousarray(x[c])) for c in range(8)]
    res = run_bass_kernel_spmd(nc, in_maps, core_ids=list(range(8)))
    return np.stack([np.asarray(r["out"], np.float32) for r in res.results],
                    axis=0)


# version 10
# speedup vs baseline: 1.0046x; 1.0046x over previous
import numpy as np
import ml_dtypes
import concourse.bass as bass
import concourse.mybir as mybir
from concourse.bass_utils import run_bass_kernel_spmd

F32 = mybir.dt.float32
BF16 = mybir.dt.bfloat16
U8 = mybir.dt.uint8
AF = mybir.ActivationFunctionType
ALU = mybir.AluOpType
AX = mybir.AxisListType
BFD = ml_dtypes.bfloat16

T_SEQ = 2048
DM = 1024
NH = 16
NEG_BIG = -1.0e30
NSL = 5


class _Rec:
    __slots__ = ("eng", "dma", "thunk", "deps", "marked", "value", "waits",
                 "seq", "tag", "lane", "lane_pred")

    def __init__(self, eng, dma, thunk, seq, tag):
        self.eng = eng
        self.dma = dma
        self.thunk = thunk
        self.deps = []
        self.marked = False
        self.value = None
        self.waits = None
        self.seq = seq
        self.tag = tag
        self.lane = None
        self.lane_pred = None


class _Buf:
    __slots__ = ("name", "writers", "readers")

    def __init__(self, name):
        self.name = name
        self.writers = []
        self.readers = []

    def absorb(self, others):
        for o in others:
            self.writers = self.writers + o.writers + o.readers


class _Sched:
    ENGINES = ("tensor", "vector", "scalar", "gpsimd", "sync")
    NLANE = 8

    def __init__(self):
        self.recs = {e: [] for e in self.ENGINES}
        self.nseq = 0
        self.dma_count = {e: 0 for e in self.ENGINES}
        self.lane_last = {}

    def op(self, eng, thunk, reads=(), writes=(), pwrites=(), dma=False, tag=""):
        r = _Rec(eng, dma, thunk, self.nseq, tag)
        self.nseq += 1
        if dma:
            lane = self.dma_count[eng] % self.NLANE
            self.dma_count[eng] += 1
            r.lane = lane
            r.lane_pred = self.lane_last.get((eng, lane))
            self.lane_last[(eng, lane)] = r
        deps = {}
        for b in reads:
            for d in b.writers:
                deps[id(d)] = d
        for b in tuple(writes) + tuple(pwrites):
            for d in b.writers:
                deps[id(d)] = d
            for d in b.readers:
                deps[id(d)] = d
        if dma and r.lane_pred is not None:
            deps[id(r.lane_pred)] = r.lane_pred
        for d in deps.values():
            if d is r:
                continue
            if (not d.dma) and (not dma) and d.eng == eng == "tensor":
                continue
            assert d.seq < r.seq, (d.tag, r.tag)
            r.deps.append(d)
        for b in reads:
            b.readers.append(r)
        for b in writes:
            b.writers = [r]
            b.readers = []
        for b in pwrites:
            b.writers.append(r)
        self.recs[eng].append(r)
        return r

    def finalize(self):
        for lst in self.recs.values():
            for r in lst:
                for d in r.deps:
                    d.marked = True
                if r.dma:
                    r.marked = True
        for lst in self.recs.values():
            cnt = {}
            for r in lst:
                if r.marked:
                    key = ("d", r.lane) if r.dma else "c"
                    cnt[key] = cnt.get(key, 0) + 1
                    r.value = cnt[key] * (16 if r.dma else 1)
        for lst in self.recs.values():
            hw = {}
            for r in lst:
                ws = []
                for d in r.deps:
                    key = (d.eng, d.lane) if d.dma else (d.eng, None)
                    assert d.value is not None
                    if hw.get(key, 0) < d.value:
                        ws.append((key, d.value))
                        hw[key] = d.value
                ws.sort(key=lambda kv: kv[1])
                r.waits = ws

    def fin(self, bufs, tag="fin"):
        r = self.op("sync", lambda e: e.nop(), reads=tuple(bufs), tag=tag)
        have = {id(d) for d in r.deps}
        for rec in self.lane_last.values():
            if id(rec) not in have:
                r.deps.append(rec)
                have.add(id(rec))
        return r

    def run(self, eng_name, eng, sems):
        for r in self.recs[eng_name]:
            for key, val in r.waits:
                eng.wait_ge(sems[key], val)
            ins = r.thunk(eng)
            if r.marked:
                assert ins is not None, r.tag
                key = (eng_name, r.lane) if r.dma else (eng_name, None)
                ins.then_inc(sems[key], 16 if r.dma else 1)


def _host_constants():
    c = {}
    c["ident_f32"] = np.eye(128, dtype=np.float32)
    i_of_r = (np.arange(128) % 64) // 2
    freqs = np.exp(-np.log(10000.0) * i_of_r / 32.0)
    t = np.arange(T_SEQ, dtype=np.float64)
    ang = freqs[:, None] * t[None, :]
    cos = np.cos(ang)
    sin = np.sin(ang)
    sgn = np.where(np.arange(128) % 2 == 0, -1.0, 1.0)[:, None]
    c["Cq"] = (0.125 * cos).astype(BFD)
    c["Sq"] = (0.125 * sgn * sin).astype(BFD)
    c["Ck"] = cos.astype(BFD)
    c["Sk"] = (sgn * sin).astype(BFD)
    tq = np.arange(128)[:, None]
    cw = np.arange(384)[None, :]
    valid = (cw > tq + 6) & (cw <= tq + 256)
    c["maskT"] = np.where(valid, 0.0, NEG_BIG).astype(np.float32)
    c["neg1"] = np.full((128, 1), -1.0, np.float32)
    c["onesc2"] = np.full((1, 128), 2.0, np.float32)
    return c


_CONST_SPECS = [
    ("ident_f32", (128, 128), F32),
    ("Cq", (128, 2048), BF16), ("Sq", (128, 2048), BF16),
    ("Ck", (128, 2048), BF16), ("Sk", (128, 2048), BF16),
    ("maskT", (128, 384), F32),
    ("neg1", (128, 1), F32), ("onesc2", (1, 128), F32),
]


def build_nc(debug=()):
    nc = bass.Bass("TRN2", target_bir_lowering=False, debug=False)
    sch = _Sched()

    dx = nc.dram_tensor("x", [T_SEQ, DM], F32, kind="ExternalInput")
    dwqk = nc.dram_tensor("wqk", [16, 128, 1024], F32, kind="ExternalInput")
    dwv = nc.dram_tensor("wv", [8, 128, 1024], F32, kind="ExternalInput")
    dwo = nc.dram_tensor("wo", [8, 128, 1024], F32, kind="ExternalInput")
    dk0a = nc.dram_tensor("k0a", [8, 128, 4096], U8, kind="ExternalInput")
    dk0b = nc.dram_tensor("k0b", [8, 128, 4096], U8, kind="ExternalInput")
    dk1 = nc.dram_tensor("k1", [32, 128, 4096], U8, kind="ExternalInput")
    dk2 = nc.dram_tensor("k2", [32, 128, 1024], U8, kind="ExternalInput")
    dsc = [nc.dram_tensor(f"s{j}", [1, 1], F32, kind="ExternalInput")
           for j in range(3)]
    dconst = {name: nc.dram_tensor("c_" + name, list(shp), dt, kind="ExternalInput")
              for name, shp, dt in _CONST_SPECS}
    dout = nc.dram_tensor("out", [T_SEQ, DM], F32, kind="ExternalOutput")
    dwqkb = nc.dram_tensor("wqkb", [16, 128, 1024], BF16, kind="Internal")
    dwvb = nc.dram_tensor("wvb", [8, 128, 1024], BF16, kind="Internal")
    dwob = nc.dram_tensor("wob", [8, 128, 1024], BF16, kind="Internal")
    dw0s = nc.dram_tensor("w0s", [8, 128, 4096], BF16, kind="Internal")
    dw1s = nc.dram_tensor("w1s", [32, 128, 4096], BF16, kind="Internal")
    dw2s = nc.dram_tensor("w2s", [32, 128, 1024], BF16, kind="Internal")

    ddbg = {}
    for name, shp, dt in [
        ("xn_f", [128, 8192], BF16), ("q_f", [128, 8192], BF16),
        ("k_f", [128, 8192], BF16), ("v_t", [128, 8192], BF16),
        ("s_t", [128, 4096], BF16), ("t1", [128, 16384], BF16),
        ("t2", [128, 16384], BF16), ("af", [128, 6144], BF16),
    ]:
        if name in debug:
            ddbg[name] = nc.dram_tensor("dbg_" + name, shp, dt,
                                        kind="ExternalOutput")

    alloc_log = []

    def sba(name, shape, dtype):
        t = nc.alloc_sbuf_tensor(name, list(shape), dtype)
        alloc_log.append((name, int(np.prod(shape[1:])) * mybir.dt.size(dtype)))
        return t

    sb_idf = sba("ident_f32", [128, 128], F32)
    sb_mask = sba("maskT", [128, 384], F32)
    sb_neg1 = sba("neg1", [128, 1], F32)
    sb_oc2 = sba("onesc2", [1, 128], F32)
    sb_sca = sba("scalars", [1, 4], F32)
    sb_sb2 = sba("s012b2", [128, 4], F32)
    sb_lnst = sba("lnst", [128, 8], F32)
    sb_mng = sba("mng", [128, NSL], F32)
    sb_mng8 = sba("mng8", [128, NSL], F32)
    sb_smx = sba("smx", [128, NSL], F32)
    sb_rec = sba("rec", [128, NSL], F32)
    sb_Cq = sba("Cq", [128, 1024], BF16)
    sb_Sq = sba("Sq", [128, 1024], BF16)
    sb_Ck = sba("Ck", [128, 1024], BF16)
    sb_Sk = sba("Sk", [128, 1024], BF16)
    sb_Pe = sba("Pe", [128, 2 * 384], F32)
    sb_PeT = sba("PeT", [128, 2 * 384], BF16)
    sb_xn = sba("xn_f", [128, 8192], BF16)
    sb_q = sba("q_f", [128, 8192], BF16)
    sb_k = sba("k_f", [128, 8192], BF16)
    sb_v = sba("v_t", [128, 8192], BF16)
    sb_t1 = sba("t1", [128, 16384], BF16)
    sb_t2 = sba("t2", [128, 16384], BF16)
    sb_af = sba("af", [128, 3 * 2048], BF16)
    sb_s = sba("s_t", [128, 4096], BF16)
    sb_slab = [sba(f"slab{i}", [128, 2048], BF16) for i in range(3)]
    sb_wqkr = [sba(f"wqkr{i}", [128, 1024], BF16) for i in range(2)]
    sb_wor = [sba(f"wor{i}", [128, 1024], BF16) for i in range(2)]
    sb_sqd = sba("sqdump", [128, 1024], F32)
    sb_gb = sba("gb", [128, 512], F32)
    sb_pb = sba("pb", [128, 512], F32)
    sb_ob = [sba(f"outb{i}", [128, 1024], F32) for i in range(4)]

    used = sum(v for _, v in alloc_log)
    assert used <= 212832, (used, sorted(alloc_log, key=lambda kv: -kv[1])[:8])

    a_t1 = nc.lookup_mloc(sb_t1).addr
    sb_xb = [nc.alloc_sbuf_tensor_at(f"xbuf{i}", [128, 1024], F32,
                                     offset=a_t1 + i * 4096) for i in range(2)]
    a_t2 = nc.lookup_mloc(sb_t2).addr
    sb_cvF = [nc.alloc_sbuf_tensor_at(f"cvF{i}", [128, 1024], F32,
                                      offset=a_t2 + i * 4096) for i in range(2)]
    sb_cvB = [nc.alloc_sbuf_tensor_at(f"cvB{i}", [128, 1024], BF16,
                                      offset=a_t2 + 8192 + i * 2048)
              for i in range(2)]
    sb_cvU = [nc.alloc_sbuf_tensor_at(f"cvU{i}", [128, 2048], U8,
                                      offset=a_t2 + i * 2048) for i in range(2)]
    sb_cvO = [nc.alloc_sbuf_tensor_at(f"cvO{i}", [128, 2048], BF16,
                                      offset=a_t2 + 4096 + i * 4096)
              for i in range(2)]
    sb_cvC = [nc.alloc_sbuf_tensor_at(f"cvC{i}", [128, 2048], U8,
                                      offset=a_t2 + 12288 + i * 2048)
              for i in range(2)]
    a_sq = nc.lookup_mloc(sb_sqd).addr
    sb_swp = nc.alloc_sbuf_tensor_at("swp", [128, 512], F32, offset=a_sq)
    sb_tr1 = nc.alloc_sbuf_tensor_at("tr1", [128, 512], F32, offset=a_sq + 2048)
    a_gb = nc.lookup_mloc(sb_gb).addr
    sb_cvWU = nc.alloc_sbuf_tensor_at("cvWU", [128, 1024], U8, offset=a_gb)
    a_pb = nc.lookup_mloc(sb_pb).addr
    sb_cvWO = nc.alloc_sbuf_tensor_at("cvWO", [128, 1024], BF16, offset=a_pb)

    psf = [nc.alloc_psum_tensor(f"psf{i}", [128, 512], F32) for i in range(3)]
    pa = [nc.alloc_psum_tensor(f"pa{i}", [128, 512], F32) for i in range(NSL)]

    B = _Buf
    b_const = B("const")
    b_cs = B("cs_tables")
    b_sca = B("sca")
    b_sb2 = B("sb2")
    b_lnst = B("lnst")
    b_sqd = B("sqd")
    b_swp = B("swp")
    b_tr1 = B("tr1")
    b_xb = [B(f"xb{i}") for i in range(2)]
    b_xnf = [[B(f"xnf{c}_{n}") for n in range(2)] for c in range(8)]
    b_q = [[B(f"q{c}_{n}") for n in range(2)] for c in range(8)]
    b_k = [[B(f"k{c}_{n}") for n in range(2)] for c in range(8)]
    b_v = [B(f"v{t}") for t in range(8)]
    b_t1 = [B(f"t1_{m}") for m in range(32)]
    b_t2 = [B(f"t2_{m}") for m in range(32)]
    b_af = [B(f"af{s}") for s in range(3)]
    b_s = [B(f"s{m}") for m in range(8)]
    b_slab = [B(f"slab{i}") for i in range(3)]
    b_wqkr = [B(f"wqkr{i}") for i in range(2)]
    b_wor = [B(f"wor{i}") for i in range(2)]
    b_gb = B("gb")
    b_pb = B("pb")
    b_ob = [B(f"ob{i}") for i in range(4)]
    b_psf = [B(f"psf{i}") for i in range(3)]
    b_pa = [B(f"paS{i}") for i in range(NSL)]
    b_pav = [B(f"paV{i}") for i in range(NSL)]
    b_Pe = [B(f"Pe{i}") for i in range(2)]
    b_PeT = [B(f"PeT{i}") for i in range(2)]
    b_mng = [B(f"mng{i}") for i in range(NSL)]
    b_mng8 = [B(f"mng8{i}") for i in range(NSL)]
    b_smx = [B(f"smx{i}") for i in range(NSL)]
    b_rec = [B(f"rec{i}") for i in range(NSL)]
    b_cvF = [B(f"cvF{i}") for i in range(2)]
    b_cvB = [B(f"cvB{i}") for i in range(2)]
    b_cvU = [B(f"cvU{i}") for i in range(2)]
    b_cvO = [B(f"cvO{i}") for i in range(2)]
    b_cvC = [B(f"cvC{i}") for i in range(2)]
    b_cvWU = B("cvWU")
    b_cvWO = B("cvWO")
    b_wqkb = [B(f"wqkb{m}") for m in range(16)]
    b_wvb = [B(f"wvb{c}") for c in range(8)]
    b_wob = [B(f"wob{c}") for c in range(8)]
    b_w0s = [[B(f"w0s{c}_{h}") for h in range(2)] for c in range(8)]
    b_w1s = [[B(f"w1s{c}_{h}") for h in range(2)] for c in range(32)]
    b_w2s = [B(f"w2s{c}") for c in range(32)]
    b_out = B("out_dram")
    b_dbg = B("dbg")

    def dma(q, dst, src, reads=(), writes=(), pwrites=(), tag=""):
        return sch.op(q, lambda e, d=dst, s=src: e.dma_start(d, s),
                      reads=reads, writes=writes, pwrites=pwrites, dma=True,
                      tag=tag)

    def mm(out, lhsT, rhs, start, stop, reads=(), writes=(), pwrites=(), tag=""):
        return sch.op("tensor",
                      lambda e, o=out, l=lhsT, r=rhs, st=start, sp=stop:
                      e.matmul(o, l, r, start=st, stop=sp),
                      reads=reads, writes=writes, pwrites=pwrites, tag=tag)

    def transp(out, inp, ident, reads=(), writes=(), pwrites=(), tag=""):
        return sch.op("tensor",
                      lambda e, o=out, i=inp, d=ident: e.transpose(o, i, d),
                      reads=reads, writes=writes, pwrites=pwrites, tag=tag)

    bank_tog = [0]

    def next_bank():
        bank_tog[0] = (bank_tog[0] + 1) % 3
        return bank_tog[0]

    slab_i = [0]

    def next_slab():
        slab_i[0] = (slab_i[0] + 1) % 3
        return sb_slab[slab_i[0]], b_slab[slab_i[0]]

    sb_konst = dict(ident_f32=sb_idf, maskT=sb_mask, neg1=sb_neg1,
                    onesc2=sb_oc2)
    dma("sync", sb_xb[0][:], dx.ap()[0:128, :], writes=(b_xb[0],), tag="ldx0")
    dma("sync", sb_xb[1][:], dx.ap()[128:256, :], writes=(b_xb[1],), tag="ldx1")
    for name in ("ident_f32", "maskT", "neg1", "onesc2"):
        dma("sync", sb_konst[name][:], dconst[name].ap(), pwrites=(b_const,),
            tag=f"ld_{name}")
    for tb, dst in (("Cq", sb_Cq), ("Sq", sb_Sq), ("Ck", sb_Ck), ("Sk", sb_Sk)):
        dma("sync", dst[:], dconst[tb].ap()[:, 0:1024], pwrites=(b_cs,),
            tag=f"ld_{tb}0")
    for j in range(3):
        dma("sync", sb_sca[0:1, j:j + 1], dsc[j].ap(), pwrites=(b_sca,),
            tag=f"ld_s{j}")
    for j in range(3):
        mm(psf[0][:, j:j + 1], sb_oc2[:, :], sb_sca[0:1, j:j + 1], True, True,
           reads=(b_const, b_sca),
           writes=(b_psf[0],) if j == 0 else (), pwrites=(b_psf[0],) if j else (),
           tag=f"bc_s{j}")
    sch.op("vector", lambda e: e.tensor_copy(sb_sb2[:, 0:3], psf[0][:, 0:3]),
           reads=(b_psf[0],), writes=(b_sb2,), tag="ev_s012")

    conv_steps = []

    def conv_cast(dsrc, ddst, n, bufs, tagp):
        for m in range(n):
            i = m % 2

            def s1(m=m, i=i):
                dma("sync", sb_cvF[i][:], dsrc.ap()[m], writes=(b_cvF[i],),
                    tag=f"{tagp}l{m}")

            def s2(m=m, i=i):
                sch.op("scalar", lambda e, i=i: e.copy(sb_cvB[i][:], sb_cvF[i][:]),
                       reads=(b_cvF[i],), writes=(b_cvB[i],), tag=f"{tagp}c{m}")

            def s3(m=m, i=i):
                dma("sync", ddst.ap()[m], sb_cvB[i][:], reads=(b_cvB[i],),
                    writes=(bufs[m],), tag=f"{tagp}s{m}")
            conv_steps.extend([s1, s2, s3])

    def conv_w0(kc, h):
        i = (kc * 2 + h) % 2

        def s1(kc=kc, h=h, i=i):
            dma("sync", sb_cvU[i][:], dk0a.ap()[kc, :, h * 2048:(h + 1) * 2048],
                writes=(b_cvU[i],), tag=f"w0a{kc}_{h}")

        def s1b(kc=kc, h=h, i=i):
            dma("sync", sb_cvC[i][:], dk0b.ap()[kc, :, h * 2048:(h + 1) * 2048],
                writes=(b_cvC[i],), tag=f"w0b{kc}_{h}")

        def s2(i=i):
            sch.op("vector", lambda e, i=i: e.tensor_sub(
                sb_cvO[i][:], sb_cvU[i][:], sb_cvC[i][:]),
                   reads=(b_cvU[i], b_cvC[i]), writes=(b_cvO[i],),
                   tag=f"w0sub{i}")

        def s3(kc=kc, h=h, i=i):
            dma("sync", dw0s.ap()[kc, :, h * 2048:(h + 1) * 2048], sb_cvO[i][:],
                reads=(b_cvO[i],), writes=(b_w0s[kc][h],), tag=f"w0w{kc}_{h}")
        conv_steps.extend([s1, s1b, s2, s3])

    def conv_w1(kc, h):
        i = (kc * 2 + h) % 2

        def s1(kc=kc, h=h, i=i):
            dma("sync", sb_cvU[i][:], dk1.ap()[kc, :, h * 2048:(h + 1) * 2048],
                writes=(b_cvU[i],), tag=f"w1l{kc}_{h}")

        def s2(i=i):
            sch.op("vector", lambda e, i=i: e.tensor_scalar(
                sb_cvO[i][:], sb_cvU[i][:], 2.0, -1.0, op0=ALU.mult, op1=ALU.add),
                   reads=(b_cvU[i],), writes=(b_cvO[i],), tag=f"w1a{i}")

        def s3(kc=kc, h=h, i=i):
            dma("sync", dw1s.ap()[kc, :, h * 2048:(h + 1) * 2048], sb_cvO[i][:],
                reads=(b_cvO[i],), writes=(b_w1s[kc][h],), tag=f"w1w{kc}_{h}")
        conv_steps.extend([s1, s2, s3])

    def conv_w2(kc):
        def s1(kc=kc):
            dma("sync", sb_cvWU[:], dk2.ap()[kc], writes=(b_cvWU,),
                tag=f"w2l{kc}")

        def s2():
            sch.op("vector", lambda e: e.tensor_scalar(
                sb_cvWO[:], sb_cvWU[:], 2.0, -1.0, op0=ALU.mult, op1=ALU.add),
                   reads=(b_cvWU,), writes=(b_cvWO,), tag="w2a")

        def s3(kc=kc):
            dma("sync", dw2s.ap()[kc], sb_cvWO[:], reads=(b_cvWO,),
                writes=(b_w2s[kc],), tag=f"w2w{kc}")
        conv_steps.extend([s1, s2, s3])

    def _cv_to_AC():
        b_cvU[0].absorb([b_cvF[0]])
        b_cvU[1].absorb([b_cvF[0]])
        b_cvO[0].absorb([b_cvF[1]])
        b_cvO[1].absorb([b_cvB[0], b_cvB[1]])

    def _cv_to_FB():
        b_cvF[0].absorb([b_cvU[0], b_cvU[1]])
        b_cvF[1].absorb([b_cvO[0]])
        b_cvB[0].absorb([b_cvO[1]])
        b_cvB[1].absorb([b_cvO[1]])

    conv_cast(dwqk, dwqkb, 16, b_wqkb, "cq")
    conv_cast(dwv, dwvb, 8, b_wvb, "cv")
    conv_steps.append(_cv_to_AC)
    for kc in range(8):
        for h in range(2):
            conv_w0(kc, h)
    for kc in range(32):
        conv_w1(kc, 0)
    for kc in range(32):
        conv_w1(kc, 1)
    conv_steps.append(_cv_to_FB)
    conv_cast(dwo, dwob, 8, b_wob, "co")
    for kc in range(32):
        conv_w2(kc)
    conv_iter = iter(conv_steps)

    def emit_conv(k):
        for _ in range(k):
            step = next(conv_iter, None)
            if step is None:
                return
            step()

    xs = sb_lnst[:, 0:1]
    mu = sb_lnst[:, 1:2]
    vs = sb_lnst[:, 2:3]
    v1 = sb_lnst[:, 3:4]
    ln1 = sb_lnst[:, 4:5]
    inv = sb_lnst[:, 5:6]
    xnf_r = sb_xn[:].rearrange("p (c n) -> p c n", c=8)

    def ln_tile(seg, tl):
        xb = sb_xb[tl % 2]
        bxb = b_xb[tl % 2]
        gt = seg * 8 + tl
        if gt >= 2:
            dma("sync", xb[:], dx.ap()[gt * 128:(gt + 1) * 128, :], writes=(bxb,),
                tag=f"ldx{gt}")
        sch.op("vector", lambda e, xb=xb: e.reduce_sum(xs, xb[:], axis=AX.X),
               reads=(bxb,), writes=(b_lnst,), tag=f"ln_sum{gt}")
        sch.op("vector", lambda e: e.tensor_scalar_mul(mu, xs, 1.0 / DM),
               reads=(b_lnst,), pwrites=(b_lnst,), tag=f"ln_mu{gt}")
        sch.op("vector", lambda e, xb=xb: e.tensor_scalar(
            xb[:], xb[:], mu, None, op0=ALU.subtract),
               reads=(b_lnst,), writes=(bxb,), tag=f"ln_sub{gt}")
        sch.op("scalar", lambda e, xb=xb: e.activation(
            sb_sqd[:], xb[:], AF.Square, accum_out=vs),
               reads=(bxb,), writes=(b_sqd,), pwrites=(b_lnst,), tag=f"ln_sq{gt}")
        sch.op("vector", lambda e: e.tensor_scalar(
            v1, vs, 1.0 / DM, 1e-5, op0=ALU.mult, op1=ALU.add),
               reads=(b_lnst,), pwrites=(b_lnst,), tag=f"ln_v1{gt}")
        sch.op("scalar", lambda e: e.activation(ln1, v1, AF.Ln),
               reads=(b_lnst,), pwrites=(b_lnst,), tag=f"ln_ln{gt}")
        sch.op("scalar", lambda e: e.activation(inv, ln1, AF.Exp, scale=-0.5),
               reads=(b_lnst,), pwrites=(b_lnst,), tag=f"ln_inv{gt}")
        sch.op("vector", lambda e, xb=xb: e.tensor_scalar_mul(xb[:], xb[:], inv),
               reads=(b_lnst,), writes=(bxb,), tag=f"ln_fin{gt}")
        banks = [next_bank(), next_bank()]
        for g in range(2):
            bk, bb = psf[banks[g]], b_psf[banks[g]]
            for qq in range(4):
                cchunk = g * 4 + qq
                transp(bk[:, qq * 128:(qq + 1) * 128],
                       xb[:, cchunk * 128:(cchunk + 1) * 128], sb_idf[:],
                       reads=(bxb, b_const),
                       writes=(bb,) if qq == 0 else (),
                       pwrites=(bb,) if qq else (),
                       tag=f"tp{gt}_{cchunk}")
            dst = xnf_r[:, g * 4:(g + 1) * 4, tl * 128:(tl + 1) * 128]
            src = bk[:].rearrange("p (c n) -> p c n", c=4)
            sch.op("scalar", lambda e, d=dst, s=src: e.copy(d, s),
                   reads=(bb,),
                   pwrites=tuple(b_xnf[g * 4 + c][tl // 4] for c in range(4)),
                   tag=f"tpev{gt}_{g}")

    swmask = []
    for i in range(16):
        swmask += [2 * i + 1, 2 * i]
    ip_done = set()

    def ip_round(seg, m, nt):
        ip_done.add((seg, m, nt))
        wr, bwr = sb_wqkr[m % 2], b_wqkr[m % 2]
        dma("sync", wr[:], dwqkb.ap()[m], reads=(b_wqkb[m],), writes=(bwr,),
            tag=f"ldwqk{seg}_{m}_{nt}")
        nb = next_bank()
        bk, bb = psf[nb], b_psf[nb]
        for kc in range(8):
            mm(bk[:, 0:512], wr[:, kc * 128:(kc + 1) * 128],
               sb_xn[:, kc * 1024 + nt * 512: kc * 1024 + nt * 512 + 512],
               kc == 0, kc == 7,
               reads=(bwr, b_xnf[kc][nt]),
               writes=(bb,) if kc == 0 else (), pwrites=(bb,) if kc else (),
               tag=f"qk{seg}_{m}_{nt}_{kc}")
        isq = m < 8
        c = m if isq else m - 8
        dstt = sb_q if isq else sb_k
        bdst = b_q[c][nt] if isq else b_k[c][nt]
        ctab, stab = (sb_Cq, sb_Sq) if isq else (sb_Ck, sb_Sk)
        cs = slice(nt * 512, nt * 512 + 512)
        sch.op("vector", lambda e, bk=bk: e.stream_shuffle(
            sb_swp[:], bk[:, 0:512], swmask),
               reads=(bb,), writes=(b_swp,), tag=f"rp_sh{seg}_{m}_{nt}")
        sch.op("vector", lambda e, bk=bk, cs=cs, ctab=ctab: e.tensor_mul(
            sb_tr1[:], bk[:, 0:512], ctab[:, cs]),
               reads=(bb, b_cs), writes=(b_tr1,), tag=f"rp_c{seg}_{m}_{nt}")
        sch.op("vector", lambda e, cs=cs, stab=stab: e.tensor_mul(
            sb_swp[:], sb_swp[:], stab[:, cs]),
               reads=(b_swp, b_cs), writes=(b_swp,), tag=f"rp_s{seg}_{m}_{nt}")
        dst = dstt[:, c * 1024 + nt * 512: c * 1024 + nt * 512 + 512]
        sch.op("vector", lambda e, d=dst: e.tensor_add(d, sb_tr1[:], sb_swp[:]),
               reads=(b_tr1, b_swp), writes=(bdst,), tag=f"rp_o{seg}_{m}_{nt}")

    v_done = set()

    def v_round(seg, tt, og):
        v_done.add((seg, tt, og))
        slabs = []
        for q4 in range(2):
            slab, bslab = next_slab()
            src = dwvb.ap()[q4 * 4:(q4 + 1) * 4, :, og * 512:(og + 1) * 512]
            src = src.rearrange("c p f -> p c f")
            dst = slab[:, 0:2048].rearrange("p (c f) -> p c f", c=4)
            dma("sync", dst, src,
                reads=tuple(b_wvb[q4 * 4:(q4 + 1) * 4]), writes=(bslab,),
                tag=f"vld{seg}_{tt}_{og}_{q4}")
            slabs.append((slab, bslab))
        nb = next_bank()
        bk, bb = psf[nb], b_psf[nb]
        for kc in range(8):
            slab, bslab = slabs[kc // 4]
            mm(bk[:, 0:512],
               sb_xn[:, kc * 1024 + tt * 128: kc * 1024 + tt * 128 + 128],
               slab[:, (kc % 4) * 512:(kc % 4) * 512 + 512],
               kc == 0, kc == 7,
               reads=(b_xnf[kc][tt // 4], bslab),
               writes=(bb,) if kc == 0 else (), pwrites=(bb,) if kc else (),
               tag=f"v{seg}_{tt}_{og}_{kc}")
        dst = sb_v[:, tt * 1024 + og * 512: tt * 1024 + og * 512 + 512]
        sch.op("scalar", lambda e, d=dst, bk=bk: e.copy(d, bk[:, 0:512]),
               reads=(bb,), pwrites=(b_v[tt],), tag=f"vev{seg}_{tt}_{og}")

    def f0_round(seg, nt, r):
        slab, bslab = next_slab()
        src = dw0s.ap()[:, :, r * 128:(r + 1) * 128].rearrange("c p f -> p c f")
        dst = slab[:, 0:1024].rearrange("p (c f) -> p c f", c=8)
        dma("sync", dst, src, reads=tuple(b_w0s[kc][r // 16] for kc in range(8)),
            writes=(bslab,), tag=f"f0ld{seg}_{nt}_{r}")
        nb = next_bank()
        bk, bb = psf[nb], b_psf[nb]
        for kc in range(8):
            mm(bk[:, 0:512],
               slab[:, kc * 128: kc * 128 + 128],
               sb_xn[:, kc * 1024 + nt * 512: kc * 1024 + nt * 512 + 512],
               kc == 0, kc == 7,
               reads=(bslab, b_xnf[kc][nt]),
               writes=(bb,) if kc == 0 else (), pwrites=(bb,) if kc else (),
               tag=f"f0mm{seg}_{nt}_{r}_{kc}")
        dst2 = sb_t1[:, r * 512:(r + 1) * 512]
        sch.op("scalar", lambda e, d=dst2, bk=bk: e.activation(
            d, bk[:, 0:512], AF.Relu, scale=sb_sb2[:, 0:1]),
               reads=(bb, b_sb2), writes=(b_t1[r],),
               tag=f"f0ev{seg}_{nt}_{r}")

    def f1_round(seg, nt, r):
        nb = next_bank()
        bk, bb = psf[nb], b_psf[nb]
        for g4 in range(4):
            slab, bslab = next_slab()
            src = dw1s.ap()[g4 * 8:(g4 + 1) * 8, :, r * 128:(r + 1) * 128]
            src = src.rearrange("c p f -> p c f")
            dst = slab[:, 0:1024].rearrange("p (c f) -> p c f", c=8)
            dma("sync", dst, src,
                reads=tuple(b_w1s[g4 * 8 + j][r // 16] for j in range(8)),
                writes=(bslab,), tag=f"f1ld{seg}_{nt}_{r}_{g4}")
            for kci in range(8):
                kc = g4 * 8 + kci
                mm(bk[:, 0:512],
                   slab[:, kci * 128: kci * 128 + 128],
                   sb_t1[:, kc * 512:(kc + 1) * 512],
                   kc == 0, kc == 31,
                   reads=(bslab, b_t1[kc]),
                   writes=(bb,) if kc == 0 else (), pwrites=(bb,) if kc else (),
                   tag=f"f1mm{seg}_{nt}_{r}_{kc}")
        dst = sb_t2[:, r * 512:(r + 1) * 512]
        sch.op("scalar", lambda e, d=dst, bk=bk: e.activation(
            d, bk[:, 0:512], AF.Relu, scale=sb_sb2[:, 1:2]),
               reads=(bb, b_sb2), writes=(b_t2[r],),
               tag=f"f1ev{seg}_{nt}_{r}")

    def af_slot(seg, hp):
        return (seg * 4 + hp) % 3

    def c_round(seg, nt, r):
        mo = r
        wr, bwr = sb_wor[mo % 2], b_wor[mo % 2]
        dma("sync", wr[:], dwob.ap()[mo], reads=(b_wob[mo],), writes=(bwr,),
            tag=f"ldwo{seg}_{nt}_{mo}")
        nb = next_bank()
        bk, bb = psf[nb], b_psf[nb]
        for g in range(2):
            sl = af_slot(seg, nt * 2 + g)
            for kc in range(8):
                mm(bk[:, g * 256: g * 256 + 256],
                   wr[:, kc * 128:(kc + 1) * 128],
                   sb_af[:, sl * 2048 + kc * 256: sl * 2048 + kc * 256 + 256],
                   kc == 0 and g == 0, kc == 7 and g == 1,
                   reads=(bwr, b_af[sl]),
                   writes=(bb,) if (kc == 0 and g == 0) else (),
                   pwrites=(bb,) if (kc or g) else (),
                   tag=f"op{seg}_{nt}_{mo}_{g}_{kc}")
        dst = sb_s[:, mo * 512:(mo + 1) * 512]
        sch.op("scalar", lambda e, d=dst, bk=bk: e.copy(d, bk[:, 0:512]),
               reads=(bb,), writes=(b_s[mo],), tag=f"opev{seg}_{nt}_{mo}")

    def ob_loads(seg, nt):
        for j in range(4):
            gt = seg * 8 + nt * 4 + j
            dma("sync", sb_ob[j][:], dx.ap()[gt * 128:(gt + 1) * 128, :],
                writes=(b_ob[j],), tag=f"ldxo{seg}_{nt}_{j}")

    def f2_round(seg, nt, r):
        m3 = r
        nb = next_bank()
        bk, bb = psf[nb], b_psf[nb]
        for g4 in range(4):
            slab, bslab = next_slab()
            src = dw2s.ap()[g4 * 8:(g4 + 1) * 8, :, r * 128:(r + 1) * 128]
            src = src.rearrange("c p f -> p c f")
            dst = slab[:, 0:1024].rearrange("p (c f) -> p c f", c=8)
            dma("sync", dst, src,
                reads=tuple(b_w2s[g4 * 8 + j] for j in range(8)),
                writes=(bslab,), tag=f"f2ld{seg}_{nt}_{r}_{g4}")
            for kci in range(8):
                kc = g4 * 8 + kci
                mm(bk[:, 0:512],
                   slab[:, kci * 128: kci * 128 + 128],
                   sb_t2[:, kc * 512:(kc + 1) * 512],
                   kc == 0, kc == 31,
                   reads=(bslab, b_t2[kc]),
                   writes=(bb,) if kc == 0 else (), pwrites=(bb,) if kc else (),
                   tag=f"f2mm{seg}_{nt}_{r}_{kc}")
        sch.op("scalar", lambda e, bk=bk: e.activation(
            sb_gb[:], bk[:, 0:512], AF.Silu, scale=sb_sb2[:, 2:3]),
               reads=(bb, b_sb2), writes=(b_gb,),
               tag=f"silu{seg}_{nt}_{m3}")
        ssl = sb_s[:, m3 * 512:(m3 + 1) * 512]
        sch.op("vector", lambda e, ssl=ssl: e.tensor_mul(
            sb_pb[:], sb_gb[:], ssl),
               reads=(b_gb, b_s[m3]), writes=(b_pb,),
               tag=f"prod{seg}_{nt}_{m3}")
        for j in range(4):
            transp(bk[:, j * 128:(j + 1) * 128],
                   sb_pb[:, j * 128:(j + 1) * 128], sb_idf[:],
                   reads=(b_pb, b_const),
                   writes=(bb,) if j == 0 else (),
                   pwrites=(bb,) if j else (),
                   tag=f"ptr{seg}_{nt}_{m3}_{j}")
        for j in range(4):
            osl = sb_ob[j][:, m3 * 128:(m3 + 1) * 128]
            sch.op("vector",
                   lambda e, osl=osl, bk=bk, j=j: e.tensor_add(
                       osl, bk[:, j * 128:(j + 1) * 128], osl),
                   reads=(bb, b_ob[j]), pwrites=(b_ob[j],),
                   tag=f"radd{seg}_{nt}_{m3}_{j}")

    def ob_stores(seg, nt):
        for j in range(4):
            gt = seg * 8 + nt * 4 + j
            dma("sync", dout.ap()[gt * 128:(gt + 1) * 128, :], sb_ob[j][:],
                reads=(b_ob[j],), pwrites=(b_out,), tag=f"stout{seg}_{nt}_{j}")

    def runs_of(il, jj0):
        slots = [(il - 2 + jj) % 8 for jj in range(jj0, 3)]
        runs = []
        st = 0
        for j in range(1, len(slots)):
            if slots[j] != slots[j - 1] + 1:
                runs.append((jj0 + st, slots[st], j - st))
                st = j
        runs.append((jj0 + st, slots[st], len(slots) - st))
        return runs

    class Attn:
        def __init__(self, seg):
            self.seg = seg
            self.it = 0

        def pump(self, n):
            for _ in range(n):
                for lag, fn in STAGES:
                    u = self.it - lag
                    if 0 <= u < 128:
                        fn(self, u)
                self.it += 1

        def drain(self):
            if self.it < 128 + 5:
                self.pump(128 + 5 - self.it)

        def ctx(self, u):
            il, h = u // 16, u % 16
            gi = self.seg * 8 + il
            jj0 = max(0, 2 - gi)
            return il, h, gi, jj0

    def st_mm(a, u):
        il, h, gi, jj0 = a.ctx(u)
        c, off = h // 2, (h % 2) * 64
        s4 = u % NSL
        assert (a.seg, c, il // 4) in ip_done, ("q not ready", a.seg, u)
        bank, bb = pa[s4], b_pa[s4]
        runs = runs_of(il, jj0)
        for jj in range(jj0, 3):
            sl = (il - 2 + jj) % 8
            if il - 2 + jj >= 0:
                assert (a.seg, 8 + c, sl // 4) in ip_done, ("k", a.seg, u, sl)
            else:
                assert (a.seg, 8 + c, sl // 4) not in ip_done, \
                    ("k overwritten", a.seg, u, sl)
        first = True
        for jjs, sl0, ln in runs:
            kread = [b_k[c][0 if sl0 < 4 else 1]]
            if sl0 + ln - 1 >= 4 and sl0 < 4:
                kread.append(b_k[c][1])
            mm(bank[:, jjs * 128: (jjs + ln) * 128],
               sb_q[off:off + 64, c * 1024 + il * 128: c * 1024 + il * 128 + 128],
               sb_k[off:off + 64, c * 1024 + sl0 * 128: c * 1024 + (sl0 + ln) * 128],
               first, (jjs, sl0, ln) == runs[-1],
               reads=tuple([b_q[c][il // 4]] + kread),
               writes=(bb, b_pav[s4]) if first else (),
               pwrites=(bb,) if not first else (),
               tag=f"sq{a.seg}_{il}_{h}")
            first = False

    def st_msk(a, u):
        il, h, gi, jj0 = a.ctx(u)
        s4 = u % NSL
        bank, bb = pa[s4], b_pa[s4]
        sch.op("vector", lambda e, bank=bank, jj0=jj0: e.tensor_add(
            bank[:, jj0 * 128:384], bank[:, jj0 * 128:384],
            sb_mask[:, jj0 * 128:384]),
               reads=(bb, b_const), pwrites=(bb,), tag=f"msk{a.seg}_{il}_{h}")

    def st_max(a, u):
        il, h, gi, jj0 = a.ctx(u)
        s4 = u % NSL
        bank, bb = pa[s4], b_pa[s4]
        sch.op("vector", lambda e, bank=bank, jj0=jj0, s4=s4: e.reduce_max(
            sb_mng[:, s4:s4 + 1], bank[:, jj0 * 128:384], axis=AX.X,
            negate=True),
               reads=(bb,), writes=(b_mng[s4],), tag=f"mx{a.seg}_{il}_{h}")

    def st_exp(a, u):
        il, h, gi, jj0 = a.ctx(u)
        s4, s2 = u % NSL, u % 2
        bank, bb = pa[s4], b_pa[s4]
        pes = sb_Pe[:, s2 * 384 + jj0 * 128: s2 * 384 + 384]
        sch.op("scalar", lambda e, pes=pes, bank=bank, jj0=jj0, s4=s4:
               e.activation(pes, bank[:, jj0 * 128:384], AF.Exp,
                            bias=sb_mng[:, s4:s4 + 1],
                            accum_out=sb_smx[:, s4:s4 + 1]),
               reads=(bb, b_mng[s4]),
               writes=(b_Pe[s2], b_smx[s4]), tag=f"exp{a.seg}_{il}_{h}")

    def st_rcp(a, u):
        s4 = u % NSL
        sch.op("vector", lambda e, s4=s4: e.reciprocal(
            sb_rec[:, s4:s4 + 1], sb_smx[:, s4:s4 + 1]),
               reads=(b_smx[s4],), writes=(b_rec[s4],), tag=f"rcp{a.seg}_{u}")

    def st_pnorm(a, u):
        il, h, gi, jj0 = a.ctx(u)
        s4, s2 = u % NSL, u % 2
        pes = sb_Pe[:, s2 * 384 + jj0 * 128: s2 * 384 + 384]
        sch.op("vector", lambda e, pes=pes, s4=s4: e.tensor_scalar_mul(
            pes, pes, sb_rec[:, s4:s4 + 1]),
               reads=(b_rec[s4],), pwrites=(b_Pe[s2],),
               tag=f"pn{a.seg}_{il}_{h}")

    def st_tp(a, u):
        il, h, gi, jj0 = a.ctx(u)
        s4, s2 = u % NSL, u % 2
        bank, bb = pa[s4], b_pa[s4]
        for jj in range(jj0, 3):
            transp(bank[:, jj * 128:(jj + 1) * 128],
                   sb_Pe[:, s2 * 384 + jj * 128: s2 * 384 + jj * 128 + 128],
                   sb_idf[:],
                   reads=(b_Pe[s2], b_const),
                   writes=(bb,) if jj == jj0 else (),
                   pwrites=(bb,) if jj != jj0 else (),
                   tag=f"pt{a.seg}_{il}_{h}_{jj}")

    def st_ptev(a, u):
        il, h, gi, jj0 = a.ctx(u)
        s4, s2 = u % NSL, u % 2
        bank, bb = pa[s4], b_pa[s4]
        pets = sb_PeT[:, s2 * 384 + jj0 * 128: s2 * 384 + 384]
        sch.op("scalar", lambda e, pets=pets, bank=bank, jj0=jj0: e.copy(
            pets, bank[:, jj0 * 128:384]),
               reads=(bb,), writes=(b_PeT[s2],), tag=f"ptev{a.seg}_{il}_{h}")

    def st_av(a, u):
        il, h, gi, jj0 = a.ctx(u)
        s4, s2 = u % NSL, u % 2
        bank = pa[s4]
        po = (h % 2) * 64
        first = True
        for jj in range(jj0, 3):
            jt = (il - 2 + jj) % 8
            if il - 2 + jj >= 0:
                assert (a.seg, jt, 0) in v_done and \
                    (a.seg, jt, 1) in v_done, ("v", a.seg, u, jt)
            else:
                assert (a.seg, jt, 0) not in v_done and \
                    (a.seg, jt, 1) not in v_done, ("v ow", a.seg, u, jt)
            mm(bank[po:po + 64, 384:512],
               sb_v[:, jt * 1024 + h * 64: jt * 1024 + h * 64 + 64],
               sb_PeT[:, s2 * 384 + jj * 128: s2 * 384 + jj * 128 + 128],
               first, jj == 2,
               reads=(b_PeT[s2], b_v[jt]),
               writes=(b_pav[s4],) if first else (),
               pwrites=(b_pav[s4],) if not first else (),
               tag=f"av{a.seg}_{il}_{h}_{jj}")
            first = False

    def st_aev(a, u):
        il, h, gi, jj0 = a.ctx(u)
        s4 = u % NSL
        bank = pa[s4]
        po = (h % 2) * 64
        sl = af_slot(a.seg, il // 2)
        base = sl * 2048 + (h // 2) * 256 + (il % 2) * 128
        dst = sb_af[po:po + 64, base: base + 128]
        sch.op("scalar", lambda e, d=dst, bank=bank, po=po: e.copy(
            d, bank[po:po + 64, 384:512]),
               reads=(b_pav[s4],), pwrites=(b_af[sl],),
               tag=f"aev{a.seg}_{il}_{h}")

    STAGES = [(4, st_ptev), (4, st_av), (4, st_aev),
              (3, st_rcp), (3, st_pnorm), (3, st_tp),
              (2, st_exp), (1, st_msk), (1, st_max),
              (0, st_mm)]

    for tl in range(8):
        ln_tile(0, tl)
        emit_conv(6)
    for m1 in range(8):
        b_t1[m1].absorb([b_xb[0], b_xb[1]])
    b_swp.absorb([b_sqd])
    b_tr1.absorb([b_sqd])
    for nt in range(2):
        for m in range(16):
            ip_round(0, m, nt)
            emit_conv(3)
    for og in range(2):
        for tt in range(8):
            v_round(0, tt, og)
            emit_conv(2)
    at0 = Attn(0)
    for r in range(32):
        f0_round(0, 0, r)
        at0.pump(1)
        emit_conv(2)
    while at0.it < 72:
        at0.pump(1)
        emit_conv(4)
    for r in range(8):
        c_round(0, 0, r)
        at0.pump(1)
        emit_conv(4)
    while at0.it < 96:
        at0.pump(1)
        emit_conv(4)
    for m2 in range(32):
        b_t2[m2].absorb([b_cvF[0], b_cvF[1], b_cvB[0], b_cvB[1],
                         b_cvU[0], b_cvU[1], b_cvO[0], b_cvO[1],
                         b_cvC[0], b_cvC[1]])
    for r in range(32):
        f1_round(0, 0, r)
        at0.pump(1)
        emit_conv(4)
    at0.drain()
    emit_conv(10 ** 9)
    b_gb.absorb([b_cvWU])
    b_pb.absorb([b_cvWO])
    if "xn_f" in ddbg:
        dma("sync", ddbg["xn_f"].ap(), sb_xn[:],
            reads=tuple(b for row in b_xnf for b in row), pwrites=(b_dbg,),
            tag="dbg_xn")
    if "q_f" in ddbg:
        dma("sync", ddbg["q_f"].ap(), sb_q[:],
            reads=tuple(b for row in b_q for b in row), pwrites=(b_dbg,),
            tag="dbg_q")
    if "k_f" in ddbg:
        dma("sync", ddbg["k_f"].ap(), sb_k[:],
            reads=tuple(b for row in b_k for b in row), pwrites=(b_dbg,),
            tag="dbg_k")
    if "v_t" in ddbg:
        dma("sync", ddbg["v_t"].ap(), sb_v[:], reads=tuple(b_v),
            pwrites=(b_dbg,), tag="dbg_v")
    if "s_t" in ddbg:
        dma("sync", ddbg["s_t"].ap(), sb_s[:], reads=tuple(b_s),
            pwrites=(b_dbg,), tag="dbg_s")
    if "t1" in ddbg:
        dma("sync", ddbg["t1"].ap(), sb_t1[:], reads=tuple(b_t1),
            pwrites=(b_dbg,), tag="dbg_t1")
    if "af" in ddbg:
        dma("sync", ddbg["af"].ap(), sb_af[:], reads=tuple(b_af),
            pwrites=(b_dbg,), tag="dbg_af")
    ob_loads(0, 0)
    for r in range(8):
        f2_round(0, 0, r)
    ob_stores(0, 0)
    if "t2" in ddbg:
        dma("sync", ddbg["t2"].ap(), sb_t2[:], reads=tuple(b_t2),
            pwrites=(b_dbg,), tag="dbg_t2")
    for r in range(32):
        f0_round(0, 1, r)
    for r in range(8):
        c_round(0, 1, r)
    for r in range(32):
        f1_round(0, 1, r)
    ob_loads(0, 1)
    for tb, dst in (("Cq", sb_Cq), ("Sq", sb_Sq), ("Ck", sb_Ck), ("Sk", sb_Sk)):
        dma("sync", dst[:], dconst[tb].ap()[:, 1024:2048], writes=(b_cs,),
            tag=f"ld_{tb}1")

    def _absorb_sqd():
        b_sqd.absorb([b_swp, b_tr1])

    def _absorb_rope():
        b_swp.absorb([b_sqd])
        b_tr1.absorb([b_sqd])

    def _absorb_xb():
        for i in range(2):
            b_xb[i].absorb([b_t1[c] for c in range(8)])
    tail = [_absorb_sqd, _absorb_xb]
    for tl in range(4):
        tail.append(lambda tl=tl: ln_tile(1, tl))
    tail.append(_absorb_rope)
    for m in range(16):
        tail.append(lambda m=m: ip_round(1, m, 0))
    tail.append(_absorb_sqd)
    for tl in range(4, 8):
        tail.append(lambda tl=tl: ln_tile(1, tl))
    tail.append(_absorb_rope)
    for m in range(8):
        tail.append(lambda m=m: ip_round(1, m, 1))
    for tt in range(4):
        for og in range(2):
            tail.append(lambda tt=tt, og=og: v_round(1, tt, og))
    ti = iter(tail)
    for r in range(8):
        f2_round(0, 1, r)
        for _ in range(6):
            step = next(ti, None)
            if step is not None:
                step()
    for step in ti:
        step()
    ob_stores(0, 1)
    for m1 in range(8):
        b_t1[m1].absorb([b_xb[0], b_xb[1]])
    at1 = Attn(1)
    deferred = []
    for m in range(8, 16):
        deferred.append(lambda m=m: ip_round(1, m, 1))
    for tt in range(4, 8):
        for og in range(2):
            deferred.append(lambda tt=tt, og=og: v_round(1, tt, og))
    di = iter(deferred)
    for r in range(32):
        f0_round(1, 0, r)
        at1.pump(1)
    for r in range(32):
        f1_round(1, 0, r)
        at1.pump(2)
        if at1.it >= 40:
            for _ in range(2):
                step = next(di, None)
                if step is not None:
                    step()
        if r >= 24:
            c_round(1, 0, r - 24)
    for step in di:
        step()
    ob_loads(1, 0)
    for r in range(8):
        f2_round(1, 0, r)
        at1.pump(1)
    ob_stores(1, 0)
    for r in range(32):
        f0_round(1, 1, r)
        at1.pump(1)
    at1.drain()
    for r in range(32):
        f1_round(1, 1, r)
    for r in range(8):
        c_round(1, 1, r)
    ob_loads(1, 1)
    for r in range(8):
        f2_round(1, 1, r)
    ob_stores(1, 1)

    sch.fin([b_out, b_dbg])
    return _emit(nc, sch)


def _emit(nc, sch):
    sch.finalize()
    sems = {}
    for eng in _Sched.ENGINES:
        sems[(eng, None)] = nc.alloc_semaphore(f"sem_{eng}")
        for ln in range(_Sched.NLANE):
            sems[(eng, ln)] = nc.alloc_semaphore(f"sem_{eng}_d{ln}")

    with nc.Block() as block:
        @block.tensor
        def _(e):
            sch.run("tensor", e, sems)

        @block.vector
        def _(e):
            sch.run("vector", e, sems)

        @block.scalar
        def _(e):
            sch.run("scalar", e, sems)

        @block.gpsimd
        def _(e):
            sch.run("gpsimd", e, sems)

        @block.sync
        def _(e):
            sch.run("sync", e, sems)

    return nc


def _prep(L0_attn_query, L0_attn_in_proj_w, L0_attn_out_proj_w,
          L0_ffn_k0, L0_ffn_k1, L0_ffn_k2, L0_ffn_s0, L0_ffn_s1, L0_ffn_s2):
    x = np.ascontiguousarray(np.asarray(L0_attn_query, np.float32))
    win = np.ascontiguousarray(np.asarray(L0_attn_in_proj_w, np.float32))
    wout = np.ascontiguousarray(np.asarray(L0_attn_out_proj_w, np.float32))
    wqk = np.ascontiguousarray(
        win[:2048].T.reshape(8, 128, 16, 128).transpose(2, 1, 0, 3)
        .reshape(16, 128, 1024))
    wv = np.ascontiguousarray(win[2048:].T.reshape(8, 128, 1024))
    wo = np.ascontiguousarray(
        wout.T.reshape(8, 128, 8, 128).transpose(2, 1, 0, 3)
        .reshape(8, 128, 1024))
    k0 = np.ascontiguousarray(np.asarray(L0_ffn_k0)).view(np.uint8)
    k0a = np.ascontiguousarray(k0[:1024].reshape(8, 128, 4096))
    k0b = np.ascontiguousarray(k0[1024:].reshape(8, 128, 4096))
    k1 = np.ascontiguousarray(np.asarray(L0_ffn_k1)).view(np.uint8)
    k1 = k1.reshape(32, 128, 4096)
    k2 = np.ascontiguousarray(np.asarray(L0_ffn_k2)).view(np.uint8)
    k2 = k2.reshape(32, 128, 1024)
    shared = dict(wqk=wqk, wv=wv, wo=wo, k0a=k0a, k0b=k0b, k1=k1, k2=k2,
                  s0=np.asarray(L0_ffn_s0, np.float32).reshape(1, 1),
                  s1=np.asarray(L0_ffn_s1, np.float32).reshape(1, 1),
                  s2=np.asarray(L0_ffn_s2, np.float32).reshape(1, 1))
    for name, arr in _host_constants().items():
        shared["c_" + name] = np.ascontiguousarray(arr)
    return x, shared


def kernel(L0_attn_query, L0_attn_in_proj_w, L0_attn_out_proj_w,
           L0_ffn_k0, L0_ffn_k1, L0_ffn_k2, L0_ffn_s0, L0_ffn_s1, L0_ffn_s2):
    x, shared = _prep(L0_attn_query, L0_attn_in_proj_w, L0_attn_out_proj_w,
                      L0_ffn_k0, L0_ffn_k1, L0_ffn_k2,
                      L0_ffn_s0, L0_ffn_s1, L0_ffn_s2)
    nc = build_nc()
    in_maps = [dict(shared, x=np.ascontiguousarray(x[c])) for c in range(8)]
    res = run_bass_kernel_spmd(nc, in_maps, core_ids=list(range(8)))
    return np.stack([np.asarray(r["out"], np.float32) for r in res.results],
                    axis=0)


# version 11
# speedup vs baseline: 1.0248x; 1.0201x over previous
import numpy as np
import ml_dtypes
import concourse.bass as bass
import concourse.mybir as mybir
from concourse.bass_utils import run_bass_kernel_spmd

F32 = mybir.dt.float32
BF16 = mybir.dt.bfloat16
U8 = mybir.dt.uint8
AF = mybir.ActivationFunctionType
ALU = mybir.AluOpType
AX = mybir.AxisListType
BFD = ml_dtypes.bfloat16

T_SEQ = 2048
DM = 1024
NH = 16
NEG_BIG = -1.0e30
NSL = 5


class _Rec:
    __slots__ = ("eng", "dma", "thunk", "deps", "marked", "value", "waits",
                 "seq", "tag", "lane", "lane_pred")

    def __init__(self, eng, dma, thunk, seq, tag):
        self.eng = eng
        self.dma = dma
        self.thunk = thunk
        self.deps = []
        self.marked = False
        self.value = None
        self.waits = None
        self.seq = seq
        self.tag = tag
        self.lane = None
        self.lane_pred = None


class _Buf:
    __slots__ = ("name", "writers", "readers")

    def __init__(self, name):
        self.name = name
        self.writers = []
        self.readers = []

    def absorb(self, others):
        for o in others:
            self.writers = self.writers + o.writers + o.readers


class _Sched:
    ENGINES = ("tensor", "vector", "scalar", "gpsimd", "sync")
    NLANE = 8

    def __init__(self):
        self.recs = {e: [] for e in self.ENGINES}
        self.nseq = 0
        self.dma_count = {e: 0 for e in self.ENGINES}
        self.lane_last = {}

    def op(self, eng, thunk, reads=(), writes=(), pwrites=(), dma=False, tag=""):
        r = _Rec(eng, dma, thunk, self.nseq, tag)
        self.nseq += 1
        if dma:
            lane = self.dma_count[eng] % self.NLANE
            self.dma_count[eng] += 1
            r.lane = lane
            r.lane_pred = self.lane_last.get((eng, lane))
            self.lane_last[(eng, lane)] = r
        deps = {}
        for b in reads:
            for d in b.writers:
                deps[id(d)] = d
        for b in tuple(writes) + tuple(pwrites):
            for d in b.writers:
                deps[id(d)] = d
            for d in b.readers:
                deps[id(d)] = d
        if dma and r.lane_pred is not None:
            deps[id(r.lane_pred)] = r.lane_pred
        for d in deps.values():
            if d is r:
                continue
            if (not d.dma) and (not dma) and d.eng == eng == "tensor":
                continue
            assert d.seq < r.seq, (d.tag, r.tag)
            r.deps.append(d)
        for b in reads:
            b.readers.append(r)
        for b in writes:
            b.writers = [r]
            b.readers = []
        for b in pwrites:
            b.writers.append(r)
        self.recs[eng].append(r)
        return r

    def finalize(self):
        for lst in self.recs.values():
            for r in lst:
                for d in r.deps:
                    d.marked = True
                if r.dma:
                    r.marked = True
        for lst in self.recs.values():
            cnt = {}
            for r in lst:
                if r.marked:
                    key = ("d", r.lane) if r.dma else "c"
                    cnt[key] = cnt.get(key, 0) + 1
                    r.value = cnt[key] * (16 if r.dma else 1)
        for lst in self.recs.values():
            hw = {}
            for r in lst:
                ws = []
                for d in r.deps:
                    key = (d.eng, d.lane) if d.dma else (d.eng, None)
                    assert d.value is not None
                    if hw.get(key, 0) < d.value:
                        ws.append((key, d.value))
                        hw[key] = d.value
                ws.sort(key=lambda kv: kv[1])
                r.waits = ws

    def fin(self, bufs, tag="fin"):
        r = self.op("sync", lambda e: e.nop(), reads=tuple(bufs), tag=tag)
        have = {id(d) for d in r.deps}
        for rec in self.lane_last.values():
            if id(rec) not in have:
                r.deps.append(rec)
                have.add(id(rec))
        return r

    def run(self, eng_name, eng, sems):
        for r in self.recs[eng_name]:
            for key, val in r.waits:
                eng.wait_ge(sems[key], val)
            ins = r.thunk(eng)
            if r.marked:
                assert ins is not None, r.tag
                key = (eng_name, r.lane) if r.dma else (eng_name, None)
                ins.then_inc(sems[key], 16 if r.dma else 1)


def _host_constants():
    c = {}
    c["ident_f32"] = np.eye(128, dtype=np.float32)
    c["ident_bf"] = np.eye(128, dtype=np.float32).astype(BFD)
    i_of_r = (np.arange(128) % 64) // 2
    freqs = np.exp(-np.log(10000.0) * i_of_r / 32.0)
    t = np.arange(T_SEQ, dtype=np.float64)
    ang = freqs[:, None] * t[None, :]
    cos = np.cos(ang)
    sin = np.sin(ang)
    sgn = np.where(np.arange(128) % 2 == 0, -1.0, 1.0)[:, None]
    c["Cq"] = (0.125 * cos).astype(BFD)
    c["Sq"] = (0.125 * sgn * sin).astype(BFD)
    c["Ck"] = cos.astype(BFD)
    c["Sk"] = (sgn * sin).astype(BFD)
    tq = np.arange(128)[:, None]
    cw = np.arange(384)[None, :]
    valid = (cw > tq + 6) & (cw <= tq + 256)
    c["maskT"] = np.where(valid, 0.0, NEG_BIG).astype(BFD)
    c["onesc2"] = np.full((1, 128), 2.0, np.float32)
    return c


_CONST_SPECS = [
    ("ident_f32", (128, 128), F32), ("ident_bf", (128, 128), BF16),
    ("Cq", (128, 2048), BF16), ("Sq", (128, 2048), BF16),
    ("Ck", (128, 2048), BF16), ("Sk", (128, 2048), BF16),
    ("maskT", (128, 384), BF16),
    ("onesc2", (1, 128), F32),
]


def build_nc(debug=()):
    nc = bass.Bass("TRN2", target_bir_lowering=False, debug=False)
    sch = _Sched()

    dx = nc.dram_tensor("x", [T_SEQ, DM], F32, kind="ExternalInput")
    dwqk = nc.dram_tensor("wqk", [16, 128, 1024], F32, kind="ExternalInput")
    dwv = nc.dram_tensor("wv", [8, 128, 1024], F32, kind="ExternalInput")
    dwo = nc.dram_tensor("wo", [8, 128, 1024], F32, kind="ExternalInput")
    dk0a = nc.dram_tensor("k0a", [8, 128, 4096], U8, kind="ExternalInput")
    dk0b = nc.dram_tensor("k0b", [8, 128, 4096], U8, kind="ExternalInput")
    dk1 = nc.dram_tensor("k1", [32, 128, 4096], U8, kind="ExternalInput")
    dk2 = nc.dram_tensor("k2", [32, 128, 1024], U8, kind="ExternalInput")
    dsc = [nc.dram_tensor(f"s{j}", [1, 1], F32, kind="ExternalInput")
           for j in range(3)]
    dconst = {name: nc.dram_tensor("c_" + name, list(shp), dt, kind="ExternalInput")
              for name, shp, dt in _CONST_SPECS}
    dout = nc.dram_tensor("out", [T_SEQ, DM], F32, kind="ExternalOutput")
    dwqkb = nc.dram_tensor("wqkb", [16, 128, 1024], BF16, kind="Internal")
    dwvb = nc.dram_tensor("wvb", [8, 128, 1024], BF16, kind="Internal")
    dwob = nc.dram_tensor("wob", [8, 128, 1024], BF16, kind="Internal")
    dw0s = nc.dram_tensor("w0s", [8, 128, 4096], BF16, kind="Internal")
    dw1s = nc.dram_tensor("w1s", [32, 128, 4096], BF16, kind="Internal")
    dw2s = nc.dram_tensor("w2s", [32, 128, 1024], BF16, kind="Internal")

    ddbg = {}
    for name, shp, dt in [
        ("xn_f", [128, 8192], BF16), ("q_f", [128, 8192], BF16),
        ("k_f", [128, 8192], BF16), ("v_t", [128, 8192], BF16),
        ("s_t", [128, 4096], BF16), ("t1", [128, 16384], BF16),
        ("t2", [128, 16384], BF16), ("af", [128, 6144], BF16),
    ]:
        if name in debug:
            ddbg[name] = nc.dram_tensor("dbg_" + name, shp, dt,
                                        kind="ExternalOutput")

    alloc_log = []

    def sba(name, shape, dtype):
        t = nc.alloc_sbuf_tensor(name, list(shape), dtype)
        alloc_log.append((name, int(np.prod(shape[1:])) * mybir.dt.size(dtype)))
        return t

    sb_idf = sba("ident_f32", [128, 128], F32)
    sb_idb = sba("ident_bf", [128, 128], BF16)
    sb_mask = sba("maskT", [128, 384], BF16)
    sb_oc2 = sba("onesc2", [1, 128], F32)
    sb_sca = sba("scalars", [1, 4], F32)
    sb_sb2 = sba("s012b2", [128, 4], F32)
    sb_lnst = sba("lnst", [128, 12], F32)
    sb_mng = sba("mng", [128, NSL], F32)
    sb_smx = sba("smx", [128, NSL], F32)
    sb_rec = sba("rec", [128, NSL], F32)
    sb_Cq = sba("Cq", [128, 1024], BF16)
    sb_Sq = sba("Sq", [128, 1024], BF16)
    sb_Ck = sba("Ck", [128, 1024], BF16)
    sb_Sk = sba("Sk", [128, 1024], BF16)
    sb_Pe = sba("Pe", [128, 2 * 384], BF16)
    sb_PeT = sba("PeT", [128, 2 * 384], BF16)
    sb_xbh = sba("xbh", [128, 1024], BF16)
    sb_xn = sba("xn_f", [128, 8192], BF16)
    sb_q = sba("q_f", [128, 8192], BF16)
    sb_k = sba("k_f", [128, 8192], BF16)
    sb_v = sba("v_t", [128, 8192], BF16)
    sb_t1 = sba("t1", [128, 16384], BF16)
    sb_t2 = sba("t2", [128, 16384], BF16)
    sb_af = sba("af", [128, 3 * 2048], BF16)
    sb_s = sba("s_t", [128, 4096], BF16)
    sb_slab = [sba(f"slab{i}", [128, 2048], BF16) for i in range(3)]
    sb_wqkr = [sba(f"wqkr{i}", [128, 1024], BF16) for i in range(2)]
    sb_wor = [sba(f"wor{i}", [128, 1024], BF16) for i in range(2)]
    sb_swp = sba("swp", [128, 512], F32)
    sb_tr1 = sba("tr1", [128, 512], F32)
    sb_gb = sba("gb", [128, 512], F32)
    sb_pb = sba("pb", [128, 512], F32)
    sb_ob = [sba(f"outb{i}", [128, 1024], F32) for i in range(4)]

    used = sum(v for _, v in alloc_log)
    assert used <= 212832, (used, sorted(alloc_log, key=lambda kv: -kv[1])[:8])

    a_t1 = nc.lookup_mloc(sb_t1).addr
    sb_xb = [nc.alloc_sbuf_tensor_at(f"xbuf{i}", [128, 1024], F32,
                                     offset=a_t1 + i * 4096) for i in range(2)]
    a_t2 = nc.lookup_mloc(sb_t2).addr
    sb_cvF = [nc.alloc_sbuf_tensor_at(f"cvF{i}", [128, 1024], F32,
                                      offset=a_t2 + i * 4096) for i in range(2)]
    sb_cvB = [nc.alloc_sbuf_tensor_at(f"cvB{i}", [128, 1024], BF16,
                                      offset=a_t2 + 8192 + i * 2048)
              for i in range(2)]
    sb_cvU = [nc.alloc_sbuf_tensor_at(f"cvU{i}", [128, 2048], U8,
                                      offset=a_t2 + i * 2048) for i in range(2)]
    sb_cvO = [nc.alloc_sbuf_tensor_at(f"cvO{i}", [128, 2048], BF16,
                                      offset=a_t2 + 4096 + i * 4096)
              for i in range(2)]
    sb_cvC = [nc.alloc_sbuf_tensor_at(f"cvC{i}", [128, 2048], U8,
                                      offset=a_t2 + 12288 + i * 2048)
              for i in range(2)]
    a_af = nc.lookup_mloc(sb_af).addr
    sb_sqd = nc.alloc_sbuf_tensor_at("sqdump", [128, 1024], F32, offset=a_af)
    a_gb = nc.lookup_mloc(sb_gb).addr
    sb_cvWU = nc.alloc_sbuf_tensor_at("cvWU", [128, 1024], U8, offset=a_gb)
    a_pb = nc.lookup_mloc(sb_pb).addr
    sb_cvWO = nc.alloc_sbuf_tensor_at("cvWO", [128, 1024], BF16, offset=a_pb)

    psf = [nc.alloc_psum_tensor(f"psf{i}", [128, 512], F32) for i in range(3)]
    pa = [nc.alloc_psum_tensor(f"pa{i}", [128, 512], F32) for i in range(3)]
    psb = [nc.alloc_psum_tensor(f"psb{i}", [128, 1024], BF16) for i in range(2)]

    B = _Buf
    b_const = B("const")
    b_cs = B("cs_tables")
    b_sca = B("sca")
    b_sb2 = B("sb2")
    b_lnst = [B(f"lnst{i}") for i in range(2)]
    b_xbh = B("xbh")
    b_sqd = B("sqd")
    b_swp = B("swp")
    b_tr1 = B("tr1")
    b_xb = [B(f"xb{i}") for i in range(2)]
    b_xnf = [[B(f"xnf{c}_{n}") for n in range(2)] for c in range(8)]
    b_q = [[B(f"q{c}_{n}") for n in range(2)] for c in range(8)]
    b_k = [[B(f"k{c}_{n}") for n in range(2)] for c in range(8)]
    b_v = [B(f"v{t}") for t in range(8)]
    b_t1 = [B(f"t1_{m}") for m in range(32)]
    b_t2 = [B(f"t2_{m}") for m in range(32)]
    b_af = [B(f"af{s}") for s in range(3)]
    b_s = [B(f"s{m}") for m in range(8)]
    b_slab = [B(f"slab{i}") for i in range(3)]
    b_wqkr = [B(f"wqkr{i}") for i in range(2)]
    b_wor = [B(f"wor{i}") for i in range(2)]
    b_gb = B("gb")
    b_pb = B("pb")
    b_ob = [B(f"ob{i}") for i in range(4)]
    b_psf = [B(f"psf{i}") for i in range(3)]
    b_pa = [B(f"paS{i}") for i in range(3)]
    b_psb = [B(f"psb{i}") for i in range(2)]
    b_Pe = [B(f"Pe{i}") for i in range(2)]
    b_PeT = [B(f"PeT{i}") for i in range(2)]
    b_mng = [B(f"mng{i}") for i in range(NSL)]
    b_smx = [B(f"smx{i}") for i in range(NSL)]
    b_rec = [B(f"rec{i}") for i in range(NSL)]
    b_cvF = [B(f"cvF{i}") for i in range(2)]
    b_cvB = [B(f"cvB{i}") for i in range(2)]
    b_cvU = [B(f"cvU{i}") for i in range(2)]
    b_cvO = [B(f"cvO{i}") for i in range(2)]
    b_cvC = [B(f"cvC{i}") for i in range(2)]
    b_cvWU = B("cvWU")
    b_cvWO = B("cvWO")
    b_wqkb = [B(f"wqkb{m}") for m in range(16)]
    b_wvb = [B(f"wvb{c}") for c in range(8)]
    b_wob = [B(f"wob{c}") for c in range(8)]
    b_w0s = [[B(f"w0s{c}_{h}") for h in range(2)] for c in range(8)]
    b_w1s = [[B(f"w1s{c}_{h}") for h in range(2)] for c in range(32)]
    b_w2s = [B(f"w2s{c}") for c in range(32)]
    b_out = B("out_dram")
    b_dbg = B("dbg")

    def dma(q, dst, src, reads=(), writes=(), pwrites=(), tag=""):
        return sch.op(q, lambda e, d=dst, s=src: e.dma_start(d, s),
                      reads=reads, writes=writes, pwrites=pwrites, dma=True,
                      tag=tag)

    def mm(out, lhsT, rhs, start, stop, reads=(), writes=(), pwrites=(), tag=""):
        return sch.op("tensor",
                      lambda e, o=out, l=lhsT, r=rhs, st=start, sp=stop:
                      e.matmul(o, l, r, start=st, stop=sp),
                      reads=reads, writes=writes, pwrites=pwrites, tag=tag)

    def transp(out, inp, ident, reads=(), writes=(), pwrites=(), tag=""):
        return sch.op("tensor",
                      lambda e, o=out, i=inp, d=ident: e.transpose(o, i, d),
                      reads=reads, writes=writes, pwrites=pwrites, tag=tag)

    bank_tog = [0]

    def next_bank():
        bank_tog[0] = (bank_tog[0] + 1) % 3
        return bank_tog[0]

    slab_i = [0]

    def next_slab():
        slab_i[0] = (slab_i[0] + 1) % 3
        return sb_slab[slab_i[0]], b_slab[slab_i[0]]

    sb_konst = dict(ident_f32=sb_idf, ident_bf=sb_idb, maskT=sb_mask,
                    onesc2=sb_oc2)
    dma("sync", sb_xb[0][:], dx.ap()[0:128, :], writes=(b_xb[0],), tag="ldx0")
    dma("sync", sb_xb[1][:], dx.ap()[128:256, :], writes=(b_xb[1],), tag="ldx1")
    for name in ("ident_f32", "ident_bf", "maskT", "onesc2"):
        dma("sync", sb_konst[name][:], dconst[name].ap(), pwrites=(b_const,),
            tag=f"ld_{name}")
    for tb, dst in (("Cq", sb_Cq), ("Sq", sb_Sq), ("Ck", sb_Ck), ("Sk", sb_Sk)):
        dma("sync", dst[:], dconst[tb].ap()[:, 0:1024], pwrites=(b_cs,),
            tag=f"ld_{tb}0")
    for j in range(3):
        dma("sync", sb_sca[0:1, j:j + 1], dsc[j].ap(), pwrites=(b_sca,),
            tag=f"ld_s{j}")
    for j in range(3):
        mm(psf[0][:, j:j + 1], sb_oc2[:, :], sb_sca[0:1, j:j + 1], True, True,
           reads=(b_const, b_sca),
           writes=(b_psf[0],) if j == 0 else (), pwrites=(b_psf[0],) if j else (),
           tag=f"bc_s{j}")
    sch.op("vector", lambda e: e.tensor_copy(sb_sb2[:, 0:3], psf[0][:, 0:3]),
           reads=(b_psf[0],), writes=(b_sb2,), tag="ev_s012")

    conv_steps = []

    def conv_cast(dsrc, ddst, n, bufs, tagp):
        for m in range(n):
            i = m % 2

            def s1(m=m, i=i):
                dma("sync", sb_cvF[i][:], dsrc.ap()[m], writes=(b_cvF[i],),
                    tag=f"{tagp}l{m}")

            def s2(m=m, i=i):
                sch.op("scalar", lambda e, i=i: e.copy(sb_cvB[i][:], sb_cvF[i][:]),
                       reads=(b_cvF[i],), writes=(b_cvB[i],), tag=f"{tagp}c{m}")

            def s3(m=m, i=i):
                dma("sync", ddst.ap()[m], sb_cvB[i][:], reads=(b_cvB[i],),
                    writes=(bufs[m],), tag=f"{tagp}s{m}")
            conv_steps.extend([s1, s2, s3])

    def conv_w0(kc, h):
        i = (kc * 2 + h) % 2

        def s1(kc=kc, h=h, i=i):
            dma("sync", sb_cvU[i][:], dk0a.ap()[kc, :, h * 2048:(h + 1) * 2048],
                writes=(b_cvU[i],), tag=f"w0a{kc}_{h}")

        def s1b(kc=kc, h=h, i=i):
            dma("sync", sb_cvC[i][:], dk0b.ap()[kc, :, h * 2048:(h + 1) * 2048],
                writes=(b_cvC[i],), tag=f"w0b{kc}_{h}")

        def s2(i=i):
            sch.op("vector", lambda e, i=i: e.tensor_sub(
                sb_cvO[i][:], sb_cvU[i][:], sb_cvC[i][:]),
                   reads=(b_cvU[i], b_cvC[i]), writes=(b_cvO[i],),
                   tag=f"w0sub{i}")

        def s3(kc=kc, h=h, i=i):
            dma("sync", dw0s.ap()[kc, :, h * 2048:(h + 1) * 2048], sb_cvO[i][:],
                reads=(b_cvO[i],), writes=(b_w0s[kc][h],), tag=f"w0w{kc}_{h}")
        conv_steps.extend([s1, s1b, s2, s3])

    def conv_w1(kc, h):
        i = (kc * 2 + h) % 2

        def s1(kc=kc, h=h, i=i):
            dma("sync", sb_cvU[i][:], dk1.ap()[kc, :, h * 2048:(h + 1) * 2048],
                writes=(b_cvU[i],), tag=f"w1l{kc}_{h}")

        def s2(i=i):
            sch.op("vector", lambda e, i=i: e.tensor_scalar(
                sb_cvO[i][:], sb_cvU[i][:], 2.0, -1.0, op0=ALU.mult, op1=ALU.add),
                   reads=(b_cvU[i],), writes=(b_cvO[i],), tag=f"w1a{i}")

        def s3(kc=kc, h=h, i=i):
            dma("sync", dw1s.ap()[kc, :, h * 2048:(h + 1) * 2048], sb_cvO[i][:],
                reads=(b_cvO[i],), writes=(b_w1s[kc][h],), tag=f"w1w{kc}_{h}")
        conv_steps.extend([s1, s2, s3])

    def conv_w2(kc):
        def s1(kc=kc):
            dma("sync", sb_cvWU[:], dk2.ap()[kc], writes=(b_cvWU,),
                tag=f"w2l{kc}")

        def s2():
            sch.op("vector", lambda e: e.tensor_scalar(
                sb_cvWO[:], sb_cvWU[:], 2.0, -1.0, op0=ALU.mult, op1=ALU.add),
                   reads=(b_cvWU,), writes=(b_cvWO,), tag="w2a")

        def s3(kc=kc):
            dma("sync", dw2s.ap()[kc], sb_cvWO[:], reads=(b_cvWO,),
                writes=(b_w2s[kc],), tag=f"w2w{kc}")
        conv_steps.extend([s1, s2, s3])

    def _cv_to_AC():
        b_cvU[0].absorb([b_cvF[0]])
        b_cvU[1].absorb([b_cvF[0]])
        b_cvO[0].absorb([b_cvF[1]])
        b_cvO[1].absorb([b_cvB[0], b_cvB[1]])

    def _cv_to_FB():
        b_cvF[0].absorb([b_cvU[0], b_cvU[1]])
        b_cvF[1].absorb([b_cvO[0]])
        b_cvB[0].absorb([b_cvO[1]])
        b_cvB[1].absorb([b_cvO[1]])

    conv_cast(dwqk, dwqkb, 16, b_wqkb, "cq")
    conv_cast(dwv, dwvb, 8, b_wvb, "cv")
    conv_steps.append(_cv_to_AC)
    for kc in range(8):
        for h in range(2):
            conv_w0(kc, h)
    conv_steps.append(_cv_to_FB)
    conv_cast(dwo, dwob, 8, b_wob, "co")
    conv_steps.append(_cv_to_AC)
    for kc in range(32):
        conv_w1(kc, 0)
    for kc in range(32):
        conv_w1(kc, 1)
    for kc in range(32):
        conv_w2(kc)
    conv_iter = iter(conv_steps)
    conv_cnt = [0]

    def emit_conv(k):
        for _ in range(k):
            step = next(conv_iter, None)
            if step is None:
                return
            if step.__name__ not in ("_cv_to_AC", "_cv_to_FB"):
                conv_cnt[0] += 1
            step()

    def drain_conv(upto):
        while conv_cnt[0] < upto:
            step = next(conv_iter, None)
            assert step is not None
            if step.__name__ not in ("_cv_to_AC", "_cv_to_FB"):
                conv_cnt[0] += 1
            step()

    xnf_r = sb_xn[:].rearrange("p (c n) -> p c n", c=8)

    def ln_tile(seg, tl):
        sl = tl % 2
        xb, bxb = sb_xb[sl], b_xb[sl]
        bst = b_lnst[sl]
        st = sb_lnst[:, sl * 6: sl * 6 + 6]
        xs, mu, vs, v1, ln1, inv = (st[:, i:i + 1] for i in range(6))
        gt = seg * 8 + tl
        if gt >= 2:
            dma("sync", xb[:], dx.ap()[gt * 128:(gt + 1) * 128, :], writes=(bxb,),
                tag=f"ldx{gt}")
        sch.op("vector", lambda e, xb=xb, xs=xs: e.reduce_sum(xs, xb[:], axis=AX.X),
               reads=(bxb,), writes=(bst,), tag=f"ln_sum{gt}")
        sch.op("vector", lambda e, mu=mu, xs=xs: e.tensor_scalar_mul(
            mu, xs, 1.0 / DM),
               reads=(bst,), pwrites=(bst,), tag=f"ln_mu{gt}")
        sch.op("vector", lambda e, xb=xb, mu=mu: e.tensor_scalar(
            xb[:], xb[:], mu, None, op0=ALU.subtract),
               reads=(bst,), writes=(bxb,), tag=f"ln_sub{gt}")
        sch.op("scalar", lambda e, xb=xb, vs=vs: e.activation(
            sb_sqd[:], xb[:], AF.Square, accum_out=vs),
               reads=(bxb,), writes=(b_sqd,), pwrites=(bst,), tag=f"ln_sq{gt}")
        sch.op("vector", lambda e, v1=v1, vs=vs: e.tensor_scalar(
            v1, vs, 1.0 / DM, 1e-5, op0=ALU.mult, op1=ALU.add),
               reads=(bst,), pwrites=(bst,), tag=f"ln_v1{gt}")
        sch.op("scalar", lambda e, ln1=ln1, v1=v1: e.activation(ln1, v1, AF.Ln),
               reads=(bst,), pwrites=(bst,), tag=f"ln_ln{gt}")
        sch.op("scalar", lambda e, inv=inv, ln1=ln1: e.activation(
            inv, ln1, AF.Exp, scale=-0.5),
               reads=(bst,), pwrites=(bst,), tag=f"ln_inv{gt}")
        sch.op("vector", lambda e, xb=xb, inv=inv: e.tensor_scalar_mul(
            xb[:], xb[:], inv),
               reads=(bst,), writes=(bxb,), tag=f"ln_fin{gt}")
        xh = sb_xbh[:]
        sch.op("scalar", lambda e, xh=xh, xb=xb: e.copy(xh, xb[:]),
               reads=(bxb,), writes=(b_xbh,), tag=f"ln_cast{gt}")
        bk, bb = psb[sl], b_psb[sl]
        for cchunk in range(8):
            transp(bk[:, cchunk * 128:(cchunk + 1) * 128],
                   xh[:, cchunk * 128:(cchunk + 1) * 128], sb_idb[:],
                   reads=(b_xbh, b_const),
                   writes=(bb,) if cchunk == 0 else (),
                   pwrites=(bb,) if cchunk else (),
                   tag=f"tp{gt}_{cchunk}")
        dst = xnf_r[:, 0:8, tl * 128:(tl + 1) * 128]
        src = bk[:].rearrange("p (c n) -> p c n", c=8)
        sch.op("scalar", lambda e, d=dst, s=src: e.copy(d, s),
               reads=(bb,),
               pwrites=tuple(b_xnf[c][tl // 4] for c in range(8)),
               tag=f"tpev{gt}")

    swmask = []
    for i in range(16):
        swmask += [2 * i + 1, 2 * i]
    ip_done = set()

    def ip_round(seg, m, nt):
        drain_conv(3 * (m + 1))
        ip_done.add((seg, m, nt))
        wr, bwr = sb_wqkr[m % 2], b_wqkr[m % 2]
        dma("sync", wr[:], dwqkb.ap()[m], reads=(b_wqkb[m],), writes=(bwr,),
            tag=f"ldwqk{seg}_{m}_{nt}")
        nb = next_bank()
        bk, bb = psf[nb], b_psf[nb]
        for kc in range(8):
            mm(bk[:, 0:512], wr[:, kc * 128:(kc + 1) * 128],
               sb_xn[:, kc * 1024 + nt * 512: kc * 1024 + nt * 512 + 512],
               kc == 0, kc == 7,
               reads=(bwr, b_xnf[kc][nt]),
               writes=(bb,) if kc == 0 else (), pwrites=(bb,) if kc else (),
               tag=f"qk{seg}_{m}_{nt}_{kc}")
        isq = m < 8
        c = m if isq else m - 8
        dstt = sb_q if isq else sb_k
        bdst = b_q[c][nt] if isq else b_k[c][nt]
        ctab, stab = (sb_Cq, sb_Sq) if isq else (sb_Ck, sb_Sk)
        cs = slice(nt * 512, nt * 512 + 512)
        sch.op("vector", lambda e, bk=bk: e.stream_shuffle(
            sb_swp[:], bk[:, 0:512], swmask),
               reads=(bb,), writes=(b_swp,), tag=f"rp_sh{seg}_{m}_{nt}")
        sch.op("vector", lambda e, bk=bk, cs=cs, ctab=ctab: e.tensor_mul(
            sb_tr1[:], bk[:, 0:512], ctab[:, cs]),
               reads=(bb, b_cs), writes=(b_tr1,), tag=f"rp_c{seg}_{m}_{nt}")
        sch.op("vector", lambda e, cs=cs, stab=stab: e.tensor_mul(
            sb_swp[:], sb_swp[:], stab[:, cs]),
               reads=(b_swp, b_cs), writes=(b_swp,), tag=f"rp_s{seg}_{m}_{nt}")
        dst = dstt[:, c * 1024 + nt * 512: c * 1024 + nt * 512 + 512]
        sch.op("vector", lambda e, d=dst: e.tensor_add(d, sb_tr1[:], sb_swp[:]),
               reads=(b_tr1, b_swp), writes=(bdst,), tag=f"rp_o{seg}_{m}_{nt}")

    v_done = set()

    def v_round(seg, tt, og):
        v_done.add((seg, tt, og))
        slabs = []
        for q4 in range(2):
            slab, bslab = next_slab()
            src = dwvb.ap()[q4 * 4:(q4 + 1) * 4, :, og * 512:(og + 1) * 512]
            src = src.rearrange("c p f -> p c f")
            dst = slab[:, 0:2048].rearrange("p (c f) -> p c f", c=4)
            dma("sync", dst, src,
                reads=tuple(b_wvb[q4 * 4:(q4 + 1) * 4]), writes=(bslab,),
                tag=f"vld{seg}_{tt}_{og}_{q4}")
            slabs.append((slab, bslab))
        nb = next_bank()
        bk, bb = psf[nb], b_psf[nb]
        for kc in range(8):
            slab, bslab = slabs[kc // 4]
            mm(bk[:, 0:512],
               sb_xn[:, kc * 1024 + tt * 128: kc * 1024 + tt * 128 + 128],
               slab[:, (kc % 4) * 512:(kc % 4) * 512 + 512],
               kc == 0, kc == 7,
               reads=(b_xnf[kc][tt // 4], bslab),
               writes=(bb,) if kc == 0 else (), pwrites=(bb,) if kc else (),
               tag=f"v{seg}_{tt}_{og}_{kc}")
        dst = sb_v[:, tt * 1024 + og * 512: tt * 1024 + og * 512 + 512]
        sch.op("scalar", lambda e, d=dst, bk=bk: e.copy(d, bk[:, 0:512]),
               reads=(bb,), pwrites=(b_v[tt],), tag=f"vev{seg}_{tt}_{og}")

    def f0_round(seg, nt, r):
        slab, bslab = next_slab()
        src = dw0s.ap()[:, :, r * 256:(r + 1) * 256].rearrange("c p f -> p c f")
        dst = slab[:, 0:2048].rearrange("p (c f) -> p c f", c=8)
        dma("sync", dst, src, reads=tuple(b_w0s[kc][r // 8] for kc in range(8)),
            writes=(bslab,), tag=f"f0ld{seg}_{nt}_{r}")
        banks = [next_bank(), next_bank()]
        for kc in range(8):
            for widx in range(2):
                bb = b_psf[banks[widx]]
                mm(psf[banks[widx]][:, 0:512],
                   slab[:, kc * 256 + widx * 128: kc * 256 + widx * 128 + 128],
                   sb_xn[:, kc * 1024 + nt * 512: kc * 1024 + nt * 512 + 512],
                   kc == 0, kc == 7,
                   reads=(bslab, b_xnf[kc][nt]),
                   writes=(bb,) if kc == 0 else (), pwrites=(bb,) if kc else (),
                   tag=f"f0mm{seg}_{nt}_{r}_{kc}_{widx}")
        for widx in range(2):
            m1 = 2 * r + widx
            bk = banks[widx]
            dst2 = sb_t1[:, m1 * 512:(m1 + 1) * 512]
            sch.op("scalar", lambda e, d=dst2, bk=bk: e.activation(
                d, psf[bk][:, 0:512], AF.Relu, scale=sb_sb2[:, 0:1]),
                   reads=(b_psf[bk], b_sb2), writes=(b_t1[m1],),
                   tag=f"f0ev{seg}_{nt}_{m1}")

    def f1_round(seg, nt, r):
        banks = [next_bank(), next_bank()]
        for g4 in range(4):
            slab, bslab = next_slab()
            src = dw1s.ap()[g4 * 8:(g4 + 1) * 8, :, r * 256:(r + 1) * 256]
            src = src.rearrange("c p f -> p c f")
            dst = slab[:, 0:2048].rearrange("p (c f) -> p c f", c=8)
            dma("sync", dst, src,
                reads=tuple(b_w1s[g4 * 8 + j][r // 8] for j in range(8)),
                writes=(bslab,), tag=f"f1ld{seg}_{nt}_{r}_{g4}")
            for kci in range(8):
                kc = g4 * 8 + kci
                for widx in range(2):
                    bb = b_psf[banks[widx]]
                    mm(psf[banks[widx]][:, 0:512],
                       slab[:, kci * 256 + widx * 128: kci * 256 + widx * 128 + 128],
                       sb_t1[:, kc * 512:(kc + 1) * 512],
                       kc == 0, kc == 31,
                       reads=(bslab, b_t1[kc]),
                       writes=(bb,) if kc == 0 else (), pwrites=(bb,) if kc else (),
                       tag=f"f1mm{seg}_{nt}_{r}_{kc}_{widx}")
        for widx in range(2):
            m2 = 2 * r + widx
            bk = banks[widx]
            dst = sb_t2[:, m2 * 512:(m2 + 1) * 512]
            sch.op("scalar", lambda e, d=dst, bk=bk: e.activation(
                d, psf[bk][:, 0:512], AF.Relu, scale=sb_sb2[:, 1:2]),
                   reads=(b_psf[bk], b_sb2), writes=(b_t2[m2],),
                   tag=f"f1ev{seg}_{nt}_{m2}")

    def af_slot(seg, hp):
        return (seg * 4 + hp) % 3

    def c_round(seg, nt, r):
        banks = [next_bank(), next_bank()]
        for widx in range(2):
            mo = 2 * r + widx
            wr, bwr = sb_wor[mo % 2], b_wor[mo % 2]
            dma("sync", wr[:], dwob.ap()[mo], reads=(b_wob[mo],), writes=(bwr,),
                tag=f"ldwo{seg}_{nt}_{mo}")
        for g in range(2):
            sl = af_slot(seg, nt * 2 + g)
            for kc in range(8):
                for widx in range(2):
                    bb = b_psf[banks[widx]]
                    wr, bwr = sb_wor[(2 * r + widx) % 2], b_wor[(2 * r + widx) % 2]
                    mm(psf[banks[widx]][:, g * 256: g * 256 + 256],
                       wr[:, kc * 128:(kc + 1) * 128],
                       sb_af[:, sl * 2048 + kc * 256: sl * 2048 + kc * 256 + 256],
                       kc == 0 and g == 0, kc == 7 and g == 1,
                       reads=(bwr, b_af[sl]),
                       writes=(bb,) if (kc == 0 and g == 0) else (),
                       pwrites=(bb,) if (kc or g) else (),
                       tag=f"op{seg}_{nt}_{r}_{g}_{kc}_{widx}")
        for widx in range(2):
            mo = 2 * r + widx
            bk = banks[widx]
            dst = sb_s[:, mo * 512:(mo + 1) * 512]
            sch.op("scalar", lambda e, d=dst, bk=bk: e.copy(d, psf[bk][:, 0:512]),
                   reads=(b_psf[bk],), writes=(b_s[mo],), tag=f"opev{seg}_{nt}_{mo}")

    def ob_loads(seg, nt):
        for j in range(4):
            gt = seg * 8 + nt * 4 + j
            dma("sync", sb_ob[j][:], dx.ap()[gt * 128:(gt + 1) * 128, :],
                writes=(b_ob[j],), tag=f"ldxo{seg}_{nt}_{j}")

    def f2_round(seg, nt, r):
        banks = [next_bank(), next_bank()]
        for g4 in range(4):
            slab, bslab = next_slab()
            src = dw2s.ap()[g4 * 8:(g4 + 1) * 8, :, r * 256:(r + 1) * 256]
            src = src.rearrange("c p f -> p c f")
            dst = slab[:, 0:2048].rearrange("p (c f) -> p c f", c=8)
            dma("sync", dst, src,
                reads=tuple(b_w2s[g4 * 8 + j] for j in range(8)),
                writes=(bslab,), tag=f"f2ld{seg}_{nt}_{r}_{g4}")
            for kci in range(8):
                kc = g4 * 8 + kci
                for widx in range(2):
                    bb = b_psf[banks[widx]]
                    mm(psf[banks[widx]][:, 0:512],
                       slab[:, kci * 256 + widx * 128: kci * 256 + widx * 128 + 128],
                       sb_t2[:, kc * 512:(kc + 1) * 512],
                       kc == 0, kc == 31,
                       reads=(bslab, b_t2[kc]),
                       writes=(bb,) if kc == 0 else (), pwrites=(bb,) if kc else (),
                       tag=f"f2mm{seg}_{nt}_{r}_{kc}_{widx}")
        for widx in range(2):
            m3 = 2 * r + widx
            bk = banks[widx]
            bb = b_psf[bk]
            sch.op("scalar", lambda e, bk=bk: e.activation(
                sb_gb[:], psf[bk][:, 0:512], AF.Silu, scale=sb_sb2[:, 2:3]),
                   reads=(bb, b_sb2), writes=(b_gb,),
                   tag=f"silu{seg}_{nt}_{m3}")
            ssl = sb_s[:, m3 * 512:(m3 + 1) * 512]
            sch.op("vector", lambda e, ssl=ssl: e.tensor_mul(
                sb_pb[:], sb_gb[:], ssl),
                   reads=(b_gb, b_s[m3]), writes=(b_pb,),
                   tag=f"prod{seg}_{nt}_{m3}")
            for j in range(4):
                transp(psf[bk][:, j * 128:(j + 1) * 128],
                       sb_pb[:, j * 128:(j + 1) * 128], sb_idf[:],
                       reads=(b_pb, b_const),
                       writes=(bb,) if j == 0 else (),
                       pwrites=(bb,) if j else (),
                       tag=f"ptr{seg}_{nt}_{m3}_{j}")
            for j in range(4):
                osl = sb_ob[j][:, m3 * 128:(m3 + 1) * 128]
                sch.op("vector",
                       lambda e, osl=osl, bk=bk, j=j: e.tensor_add(
                           osl, psf[bk][:, j * 128:(j + 1) * 128], osl),
                       reads=(bb, b_ob[j]), pwrites=(b_ob[j],),
                       tag=f"radd{seg}_{nt}_{m3}_{j}")

    def ob_stores(seg, nt):
        for j in range(4):
            gt = seg * 8 + nt * 4 + j
            dma("sync", dout.ap()[gt * 128:(gt + 1) * 128, :], sb_ob[j][:],
                reads=(b_ob[j],), pwrites=(b_out,), tag=f"stout{seg}_{nt}_{j}")

    def runs_of(il, jj0):
        slots = [(il - 2 + jj) % 8 for jj in range(jj0, 3)]
        runs = []
        st = 0
        for j in range(1, len(slots)):
            if slots[j] != slots[j - 1] + 1:
                runs.append((jj0 + st, slots[st], j - st))
                st = j
        runs.append((jj0 + st, slots[st], len(slots) - st))
        return runs

    class Attn:
        def __init__(self, seg):
            self.seg = seg
            self.it = 0

        def pump(self, n):
            for _ in range(n):
                for lag, fn in STAGES:
                    u = self.it - lag
                    if 0 <= u < 128:
                        fn(self, u)
                self.it += 1

        def drain(self):
            if self.it < 128 + 5:
                self.pump(128 + 5 - self.it)

        def ctx(self, u):
            il, h = u // 16, u % 16
            gi = self.seg * 8 + il
            jj0 = max(0, 2 - gi)
            return il, h, gi, jj0

    def st_mm(a, u):
        il, h, gi, jj0 = a.ctx(u)
        c, off = h // 2, (h % 2) * 64
        s3 = u % 3
        assert (a.seg, c, il // 4) in ip_done, ("q not ready", a.seg, u)
        bank, bb = pa[s3], b_pa[s3]
        runs = runs_of(il, jj0)
        for jj in range(jj0, 3):
            sl = (il - 2 + jj) % 8
            if il - 2 + jj >= 0:
                assert (a.seg, 8 + c, sl // 4) in ip_done, ("k", a.seg, u, sl)
            else:
                assert (a.seg, 8 + c, sl // 4) not in ip_done, \
                    ("k overwritten", a.seg, u, sl)
        first = True
        for jjs, sl0, ln in runs:
            kread = [b_k[c][0 if sl0 < 4 else 1]]
            if sl0 + ln - 1 >= 4 and sl0 < 4:
                kread.append(b_k[c][1])
            mm(bank[:, jjs * 128: (jjs + ln) * 128],
               sb_q[off:off + 64, c * 1024 + il * 128: c * 1024 + il * 128 + 128],
               sb_k[off:off + 64, c * 1024 + sl0 * 128: c * 1024 + (sl0 + ln) * 128],
               first, (jjs, sl0, ln) == runs[-1],
               reads=tuple([b_q[c][il // 4]] + kread),
               writes=(bb,) if first else (),
               pwrites=(bb,) if not first else (),
               tag=f"sq{a.seg}_{il}_{h}")
            first = False

    def st_msk(a, u):
        il, h, gi, jj0 = a.ctx(u)
        s3 = u % 3
        bank, bb = pa[s3], b_pa[s3]
        sch.op("vector", lambda e, bank=bank, jj0=jj0: e.tensor_add(
            bank[:, jj0 * 128:384], bank[:, jj0 * 128:384],
            sb_mask[:, jj0 * 128:384]),
               reads=(bb, b_const), pwrites=(bb,), tag=f"msk{a.seg}_{il}_{h}")

    def st_max(a, u):
        il, h, gi, jj0 = a.ctx(u)
        s3, s5 = u % 3, u % NSL
        bank, bb = pa[s3], b_pa[s3]
        sch.op("vector", lambda e, bank=bank, jj0=jj0, s5=s5: e.reduce_max(
            sb_mng[:, s5:s5 + 1], bank[:, jj0 * 128:384], axis=AX.X,
            negate=True),
               reads=(bb,), writes=(b_mng[s5],), tag=f"mx{a.seg}_{il}_{h}")

    def st_exp(a, u):
        il, h, gi, jj0 = a.ctx(u)
        s3, s5, s2 = u % 3, u % NSL, u % 2
        bank, bb = pa[s3], b_pa[s3]
        pes = sb_Pe[:, s2 * 384 + jj0 * 128: s2 * 384 + 384]
        sch.op("scalar", lambda e, pes=pes, bank=bank, jj0=jj0, s5=s5:
               e.activation(pes, bank[:, jj0 * 128:384], AF.Exp,
                            bias=sb_mng[:, s5:s5 + 1],
                            accum_out=sb_smx[:, s5:s5 + 1]),
               reads=(bb, b_mng[s5]),
               writes=(b_Pe[s2], b_smx[s5]), tag=f"exp{a.seg}_{il}_{h}")

    def st_rcp(a, u):
        s5 = u % NSL
        sch.op("vector", lambda e, s5=s5: e.reciprocal(
            sb_rec[:, s5:s5 + 1], sb_smx[:, s5:s5 + 1]),
               reads=(b_smx[s5],), writes=(b_rec[s5],), tag=f"rcp{a.seg}_{u}")

    def st_pnorm(a, u):
        il, h, gi, jj0 = a.ctx(u)
        s5, s2 = u % NSL, u % 2
        pes = sb_Pe[:, s2 * 384 + jj0 * 128: s2 * 384 + 384]
        sch.op("vector", lambda e, pes=pes, s5=s5: e.tensor_scalar_mul(
            pes, pes, sb_rec[:, s5:s5 + 1]),
               reads=(b_rec[s5],), pwrites=(b_Pe[s2],),
               tag=f"pn{a.seg}_{il}_{h}")

    def st_tp(a, u):
        il, h, gi, jj0 = a.ctx(u)
        s2 = u % 2
        bank, bb = psb[s2], b_psb[s2]
        for jj in range(jj0, 3):
            transp(bank[:, jj * 128:(jj + 1) * 128],
                   sb_Pe[:, s2 * 384 + jj * 128: s2 * 384 + jj * 128 + 128],
                   sb_idb[:],
                   reads=(b_Pe[s2], b_const),
                   writes=(bb,) if jj == jj0 else (),
                   pwrites=(bb,) if jj != jj0 else (),
                   tag=f"pt{a.seg}_{il}_{h}_{jj}")

    def st_ptev(a, u):
        il, h, gi, jj0 = a.ctx(u)
        s2 = u % 2
        bank, bb = psb[s2], b_psb[s2]
        pets = sb_PeT[:, s2 * 384 + jj0 * 128: s2 * 384 + 384]
        sch.op("scalar", lambda e, pets=pets, bank=bank, jj0=jj0: e.copy(
            pets, bank[:, jj0 * 128:384]),
               reads=(bb,), writes=(b_PeT[s2],), tag=f"ptev{a.seg}_{il}_{h}")

    def st_avev(a, u):
        il, h, gi, jj0 = a.ctx(u)
        s2 = u % 2
        po = (h % 2) * 64
        nb = next_bank()
        bank, bb = psf[nb], b_psf[nb]
        first = True
        for jj in range(jj0, 3):
            jt = (il - 2 + jj) % 8
            if il - 2 + jj >= 0:
                assert (a.seg, jt, 0) in v_done and \
                    (a.seg, jt, 1) in v_done, ("v", a.seg, u, jt)
            else:
                assert (a.seg, jt, 0) not in v_done and \
                    (a.seg, jt, 1) not in v_done, ("v ow", a.seg, u, jt)
            mm(bank[po:po + 64, 384:512],
               sb_v[:, jt * 1024 + h * 64: jt * 1024 + h * 64 + 64],
               sb_PeT[:, s2 * 384 + jj * 128: s2 * 384 + jj * 128 + 128],
               first, jj == 2,
               reads=(b_PeT[s2], b_v[jt]),
               writes=(bb,) if first else (),
               pwrites=(bb,) if not first else (),
               tag=f"av{a.seg}_{il}_{h}_{jj}")
            first = False
        sl = af_slot(a.seg, il // 2)
        base = sl * 2048 + (h // 2) * 256 + (il % 2) * 128
        dst = sb_af[po:po + 64, base: base + 128]
        sch.op("scalar", lambda e, d=dst, bank=bank, po=po: e.copy(
            d, bank[po:po + 64, 384:512]),
               reads=(bb,), pwrites=(b_af[sl],),
               tag=f"aev{a.seg}_{il}_{h}")

    STAGES = [(4, st_ptev), (4, st_avev),
              (3, st_rcp), (3, st_pnorm), (3, st_tp),
              (2, st_exp), (1, st_msk), (1, st_max),
              (0, st_mm)]

    for tl in range(4):
        ln_tile(0, tl)
        emit_conv(6)
    mi = 0
    for tl in range(4, 8):
        ln_tile(0, tl)
        emit_conv(2)
        for _ in range(4):
            ip_round(0, mi, 0)
            mi += 1
            emit_conv(1)
    for m1 in range(8):
        b_t1[m1].absorb([b_xb[0], b_xb[1]])
    for m in range(16):
        ip_round(0, m, 1)
        emit_conv(2)
    drain_conv(72)
    for og in range(2):
        for tt in range(8):
            v_round(0, tt, og)
            emit_conv(2)
    b_af[0].absorb([b_sqd])
    b_af[1].absorb([b_sqd])
    at0 = Attn(0)
    drain_conv(136)
    for r in range(16):
        f0_round(0, 0, r)
        at0.pump(1)
        emit_conv(4)
    while at0.it < 72:
        at0.pump(1)
        emit_conv(4)
    drain_conv(160)
    if "af" in ddbg:
        dma("sync", ddbg["af"].ap()[:, 0:4096], sb_af[:, 0:4096],
            reads=tuple(b_af), pwrites=(b_dbg,), tag="dbg_af0")
    for r in range(4):
        c_round(0, 0, r)
        at0.pump(1)
        emit_conv(4)
    while at0.it < 96:
        at0.pump(1)
        emit_conv(4)
    for m2 in range(32):
        b_t2[m2].absorb([b_cvF[0], b_cvF[1], b_cvB[0], b_cvB[1],
                         b_cvU[0], b_cvU[1], b_cvO[0], b_cvO[1],
                         b_cvC[0], b_cvC[1]])
    drain_conv(352)
    for r in range(16):
        f1_round(0, 0, r)
        at0.pump(2)
        emit_conv(8)
    at0.drain()
    emit_conv(10 ** 9)
    b_gb.absorb([b_cvWU])
    b_pb.absorb([b_cvWO])
    if "xn_f" in ddbg:
        dma("sync", ddbg["xn_f"].ap(), sb_xn[:],
            reads=tuple(b for row in b_xnf for b in row), pwrites=(b_dbg,),
            tag="dbg_xn")
    if "q_f" in ddbg:
        dma("sync", ddbg["q_f"].ap(), sb_q[:],
            reads=tuple(b for row in b_q for b in row), pwrites=(b_dbg,),
            tag="dbg_q")
    if "k_f" in ddbg:
        dma("sync", ddbg["k_f"].ap(), sb_k[:],
            reads=tuple(b for row in b_k for b in row), pwrites=(b_dbg,),
            tag="dbg_k")
    if "v_t" in ddbg:
        dma("sync", ddbg["v_t"].ap(), sb_v[:], reads=tuple(b_v),
            pwrites=(b_dbg,), tag="dbg_v")
    if "s_t" in ddbg:
        dma("sync", ddbg["s_t"].ap(), sb_s[:], reads=tuple(b_s),
            pwrites=(b_dbg,), tag="dbg_s")
    if "t1" in ddbg:
        dma("sync", ddbg["t1"].ap(), sb_t1[:], reads=tuple(b_t1),
            pwrites=(b_dbg,), tag="dbg_t1")
    if "afX" in ddbg:
        dma("sync", ddbg["af"].ap(), sb_af[:], reads=tuple(b_af),
            pwrites=(b_dbg,), tag="dbg_af")
    ob_loads(0, 0)
    for r in range(4):
        f2_round(0, 0, r)
    ob_stores(0, 0)
    if "t2" in ddbg:
        dma("sync", ddbg["t2"].ap(), sb_t2[:], reads=tuple(b_t2),
            pwrites=(b_dbg,), tag="dbg_t2")
    for r in range(16):
        f0_round(0, 1, r)
    for r in range(4):
        c_round(0, 1, r)
    for r in range(16):
        f1_round(0, 1, r)
    ob_loads(0, 1)
    for tb, dst in (("Cq", sb_Cq), ("Sq", sb_Sq), ("Ck", sb_Ck), ("Sk", sb_Sk)):
        dma("sync", dst[:], dconst[tb].ap()[:, 1024:2048], writes=(b_cs,),
            tag=f"ld_{tb}1")

    def _absorb_sqd():
        b_sqd.absorb([b_af[0], b_af[1]])

    def _absorb_xb():
        for i in range(2):
            b_xb[i].absorb([b_t1[c] for c in range(8)])
    tail = [_absorb_sqd, _absorb_xb]
    for tl in range(4):
        tail.append(lambda tl=tl: ln_tile(1, tl))
    for m in range(16):
        tail.append(lambda m=m: ip_round(1, m, 0))
    for tl in range(4, 8):
        tail.append(lambda tl=tl: ln_tile(1, tl))
    for m in range(8):
        tail.append(lambda m=m: ip_round(1, m, 1))
    for tt in range(4):
        for og in range(2):
            tail.append(lambda tt=tt, og=og: v_round(1, tt, og))
    ti = iter(tail)
    for r in range(4):
        f2_round(0, 1, r)
        for _ in range(10):
            step = next(ti, None)
            if step is not None:
                step()
    for step in ti:
        step()
    ob_stores(0, 1)
    for m1 in range(8):
        b_t1[m1].absorb([b_xb[0], b_xb[1]])
    b_af[0].absorb([b_sqd])
    b_af[1].absorb([b_sqd])
    at1 = Attn(1)
    deferred = []
    for m in range(8, 16):
        deferred.append(lambda m=m: ip_round(1, m, 1))
    for tt in range(4, 8):
        for og in range(2):
            deferred.append(lambda tt=tt, og=og: v_round(1, tt, og))
    di = iter(deferred)
    for r in range(16):
        f0_round(1, 0, r)
        at1.pump(1)
    for r in range(16):
        f1_round(1, 0, r)
        at1.pump(4)
        if at1.it >= 40:
            for _ in range(2):
                step = next(di, None)
                if step is not None:
                    step()
        if r >= 13:
            c_round(1, 0, r - 13)
    for step in di:
        step()
    c_round(1, 0, 3)
    ob_loads(1, 0)
    for r in range(4):
        f2_round(1, 0, r)
        at1.pump(2)
    ob_stores(1, 0)
    for r in range(16):
        f0_round(1, 1, r)
        at1.pump(1)
    for r in range(16):
        f1_round(1, 1, r)
        at1.pump(2)
    at1.drain()
    for r in range(4):
        c_round(1, 1, r)
    ob_loads(1, 1)
    for r in range(4):
        f2_round(1, 1, r)
    ob_stores(1, 1)

    sch.fin([b_out, b_dbg])
    return _emit(nc, sch)


def _emit(nc, sch):
    sch.finalize()
    sems = {}
    for eng in _Sched.ENGINES:
        sems[(eng, None)] = nc.alloc_semaphore(f"sem_{eng}")
        for ln in range(_Sched.NLANE):
            sems[(eng, ln)] = nc.alloc_semaphore(f"sem_{eng}_d{ln}")

    with nc.Block() as block:
        @block.tensor
        def _(e):
            sch.run("tensor", e, sems)

        @block.vector
        def _(e):
            sch.run("vector", e, sems)

        @block.scalar
        def _(e):
            sch.run("scalar", e, sems)

        @block.gpsimd
        def _(e):
            sch.run("gpsimd", e, sems)

        @block.sync
        def _(e):
            sch.run("sync", e, sems)

    return nc


def _prep(L0_attn_query, L0_attn_in_proj_w, L0_attn_out_proj_w,
          L0_ffn_k0, L0_ffn_k1, L0_ffn_k2, L0_ffn_s0, L0_ffn_s1, L0_ffn_s2):
    x = np.ascontiguousarray(np.asarray(L0_attn_query, np.float32))
    win = np.ascontiguousarray(np.asarray(L0_attn_in_proj_w, np.float32))
    wout = np.ascontiguousarray(np.asarray(L0_attn_out_proj_w, np.float32))
    wqk = np.ascontiguousarray(
        win[:2048].T.reshape(8, 128, 16, 128).transpose(2, 1, 0, 3)
        .reshape(16, 128, 1024))
    wv = np.ascontiguousarray(win[2048:].T.reshape(8, 128, 1024))
    wo = np.ascontiguousarray(
        wout.T.reshape(8, 128, 8, 128).transpose(2, 1, 0, 3)
        .reshape(8, 128, 1024))
    k0 = np.ascontiguousarray(np.asarray(L0_ffn_k0)).view(np.uint8)
    k0a = np.ascontiguousarray(k0[:1024].reshape(8, 128, 4096))
    k0b = np.ascontiguousarray(k0[1024:].reshape(8, 128, 4096))
    k1 = np.ascontiguousarray(np.asarray(L0_ffn_k1)).view(np.uint8)
    k1 = k1.reshape(32, 128, 4096)
    k2 = np.ascontiguousarray(np.asarray(L0_ffn_k2)).view(np.uint8)
    k2 = k2.reshape(32, 128, 1024)
    shared = dict(wqk=wqk, wv=wv, wo=wo, k0a=k0a, k0b=k0b, k1=k1, k2=k2,
                  s0=np.asarray(L0_ffn_s0, np.float32).reshape(1, 1),
                  s1=np.asarray(L0_ffn_s1, np.float32).reshape(1, 1),
                  s2=np.asarray(L0_ffn_s2, np.float32).reshape(1, 1))
    for name, arr in _host_constants().items():
        shared["c_" + name] = np.ascontiguousarray(arr)
    return x, shared


def kernel(L0_attn_query, L0_attn_in_proj_w, L0_attn_out_proj_w,
           L0_ffn_k0, L0_ffn_k1, L0_ffn_k2, L0_ffn_s0, L0_ffn_s1, L0_ffn_s2):
    x, shared = _prep(L0_attn_query, L0_attn_in_proj_w, L0_attn_out_proj_w,
                      L0_ffn_k0, L0_ffn_k1, L0_ffn_k2,
                      L0_ffn_s0, L0_ffn_s1, L0_ffn_s2)
    nc = build_nc()
    in_maps = [dict(shared, x=np.ascontiguousarray(x[c])) for c in range(8)]
    res = run_bass_kernel_spmd(nc, in_maps, core_ids=list(range(8)))
    return np.stack([np.asarray(r["out"], np.float32) for r in res.results],
                    axis=0)


# version 12
# speedup vs baseline: 1.0294x; 1.0045x over previous
import numpy as np
import ml_dtypes
import concourse.bass as bass
import concourse.mybir as mybir
from concourse.bass_utils import run_bass_kernel_spmd

F32 = mybir.dt.float32
BF16 = mybir.dt.bfloat16
U8 = mybir.dt.uint8
AF = mybir.ActivationFunctionType
ALU = mybir.AluOpType
AX = mybir.AxisListType
BFD = ml_dtypes.bfloat16

T_SEQ = 2048
DM = 1024
NH = 16
NEG_BIG = -1.0e30
NSL = 5


class _Rec:
    __slots__ = ("eng", "dma", "thunk", "deps", "marked", "value", "waits",
                 "seq", "tag", "lane", "lane_pred")

    def __init__(self, eng, dma, thunk, seq, tag):
        self.eng = eng
        self.dma = dma
        self.thunk = thunk
        self.deps = []
        self.marked = False
        self.value = None
        self.waits = None
        self.seq = seq
        self.tag = tag
        self.lane = None
        self.lane_pred = None


class _Buf:
    __slots__ = ("name", "writers", "readers")

    def __init__(self, name):
        self.name = name
        self.writers = []
        self.readers = []

    def absorb(self, others):
        for o in others:
            self.writers = self.writers + o.writers + o.readers


class _Sched:
    ENGINES = ("tensor", "vector", "scalar", "gpsimd", "sync")
    NLANE = 8

    def __init__(self):
        self.recs = {e: [] for e in self.ENGINES}
        self.nseq = 0
        self.dma_count = {e: 0 for e in self.ENGINES}
        self.lane_last = {}

    def op(self, eng, thunk, reads=(), writes=(), pwrites=(), dma=False, tag=""):
        r = _Rec(eng, dma, thunk, self.nseq, tag)
        self.nseq += 1
        if dma:
            lane = self.dma_count[eng] % self.NLANE
            self.dma_count[eng] += 1
            r.lane = lane
            r.lane_pred = self.lane_last.get((eng, lane))
            self.lane_last[(eng, lane)] = r
        deps = {}
        for b in reads:
            for d in b.writers:
                deps[id(d)] = d
        for b in tuple(writes) + tuple(pwrites):
            for d in b.writers:
                deps[id(d)] = d
            for d in b.readers:
                deps[id(d)] = d
        if dma and r.lane_pred is not None:
            deps[id(r.lane_pred)] = r.lane_pred
        for d in deps.values():
            if d is r:
                continue
            if (not d.dma) and (not dma) and d.eng == eng == "tensor":
                continue
            assert d.seq < r.seq, (d.tag, r.tag)
            r.deps.append(d)
        for b in reads:
            b.readers.append(r)
        for b in writes:
            b.writers = [r]
            b.readers = []
        for b in pwrites:
            b.writers.append(r)
        self.recs[eng].append(r)
        return r

    def finalize(self):
        for lst in self.recs.values():
            for r in lst:
                for d in r.deps:
                    d.marked = True
                if r.dma:
                    r.marked = True
        for lst in self.recs.values():
            cnt = {}
            for r in lst:
                if r.marked:
                    key = ("d", r.lane) if r.dma else "c"
                    cnt[key] = cnt.get(key, 0) + 1
                    r.value = cnt[key] * (16 if r.dma else 1)
        for lst in self.recs.values():
            hw = {}
            for r in lst:
                ws = []
                for d in r.deps:
                    key = (d.eng, d.lane) if d.dma else (d.eng, None)
                    assert d.value is not None
                    if hw.get(key, 0) < d.value:
                        ws.append((key, d.value))
                        hw[key] = d.value
                ws.sort(key=lambda kv: kv[1])
                r.waits = ws

    def fin(self, bufs, tag="fin"):
        r = self.op("sync", lambda e: e.nop(), reads=tuple(bufs), tag=tag)
        have = {id(d) for d in r.deps}
        for rec in self.lane_last.values():
            if id(rec) not in have:
                r.deps.append(rec)
                have.add(id(rec))
        return r

    def run(self, eng_name, eng, sems):
        for r in self.recs[eng_name]:
            for key, val in r.waits:
                eng.wait_ge(sems[key], val)
            ins = r.thunk(eng)
            if r.marked:
                assert ins is not None, r.tag
                key = (eng_name, r.lane) if r.dma else (eng_name, None)
                ins.then_inc(sems[key], 16 if r.dma else 1)


def _host_constants():
    c = {}
    c["ident_f32"] = np.eye(128, dtype=np.float32)
    c["ident_bf"] = np.eye(128, dtype=np.float32).astype(BFD)
    i_of_r = (np.arange(128) % 64) // 2
    freqs = np.exp(-np.log(10000.0) * i_of_r / 32.0)
    t = np.arange(T_SEQ, dtype=np.float64)
    ang = freqs[:, None] * t[None, :]
    cos = np.cos(ang)
    sin = np.sin(ang)
    sgn = np.where(np.arange(128) % 2 == 0, -1.0, 1.0)[:, None]
    c["Cq"] = (0.125 * cos).astype(BFD)
    c["Sq"] = (0.125 * sgn * sin).astype(BFD)
    c["Ck"] = cos.astype(BFD)
    c["Sk"] = (sgn * sin).astype(BFD)
    tq = np.arange(128)[:, None]
    cw = np.arange(384)[None, :]
    valid = (cw > tq + 6) & (cw <= tq + 256)
    c["maskT"] = np.where(valid, 0.0, NEG_BIG).astype(BFD)
    c["onesc2"] = np.full((1, 128), 2.0, np.float32)
    c["onesc4"] = np.full((1, 128), 4.0, np.float32)
    c["ones_bf"] = np.ones((128, 1), np.float32).astype(BFD)
    c["onesrow"] = np.ones((1, 128), np.float32).astype(BFD)
    return c


_CONST_SPECS = [
    ("ident_f32", (128, 128), F32), ("ident_bf", (128, 128), BF16),
    ("Cq", (128, 2048), BF16), ("Sq", (128, 2048), BF16),
    ("Ck", (128, 2048), BF16), ("Sk", (128, 2048), BF16),
    ("maskT", (128, 384), BF16),
    ("onesc2", (1, 128), F32), ("onesc4", (1, 128), F32),
    ("ones_bf", (128, 1), BF16), ("onesrow", (1, 128), BF16),
]


def build_nc(debug=()):
    nc = bass.Bass("TRN2", target_bir_lowering=False, debug=False)
    sch = _Sched()

    dx = nc.dram_tensor("x", [T_SEQ, DM], F32, kind="ExternalInput")
    dwqk = nc.dram_tensor("wqk", [16, 128, 1024], F32, kind="ExternalInput")
    dwv = nc.dram_tensor("wv", [8, 128, 1024], F32, kind="ExternalInput")
    dwo = nc.dram_tensor("wo", [8, 128, 1024], F32, kind="ExternalInput")
    dk0a = nc.dram_tensor("k0a", [8, 128, 4096], U8, kind="ExternalInput")
    dk0b = nc.dram_tensor("k0b", [8, 128, 4096], U8, kind="ExternalInput")
    dk1 = nc.dram_tensor("k1", [32, 128, 4096], U8, kind="ExternalInput")
    dk2 = nc.dram_tensor("k2", [32, 128, 1024], U8, kind="ExternalInput")
    dsc = [nc.dram_tensor(f"s{j}", [1, 1], F32, kind="ExternalInput")
           for j in range(3)]
    dconst = {name: nc.dram_tensor("c_" + name, list(shp), dt, kind="ExternalInput")
              for name, shp, dt in _CONST_SPECS}
    dout = nc.dram_tensor("out", [T_SEQ, DM], F32, kind="ExternalOutput")
    dwqkb = nc.dram_tensor("wqkb", [16, 128, 1024], BF16, kind="Internal")
    dwvb = nc.dram_tensor("wvb", [8, 128, 1024], BF16, kind="Internal")
    dwob = nc.dram_tensor("wob", [8, 128, 1024], BF16, kind="Internal")
    dw0s = nc.dram_tensor("w0s", [8, 128, 4096], BF16, kind="Internal")
    dw1s = nc.dram_tensor("w1s", [32, 128, 4096], BF16, kind="Internal")
    dw2s = nc.dram_tensor("w2s", [32, 128, 1024], BF16, kind="Internal")

    ddbg = {}
    for name, shp, dt in [
        ("xn_f", [128, 8192], BF16), ("q_f", [128, 8192], BF16),
        ("k_f", [128, 8192], BF16), ("v_t", [128, 8192], BF16),
        ("s_t", [128, 4096], BF16), ("t1", [128, 16384], BF16),
        ("t2", [128, 16384], BF16), ("af", [128, 6144], BF16),
    ]:
        if name in debug:
            ddbg[name] = nc.dram_tensor("dbg_" + name, shp, dt,
                                        kind="ExternalOutput")

    alloc_log = []

    def sba(name, shape, dtype):
        t = nc.alloc_sbuf_tensor(name, list(shape), dtype)
        alloc_log.append((name, int(np.prod(shape[1:])) * mybir.dt.size(dtype)))
        return t

    sb_idf = sba("ident_f32", [128, 128], F32)
    sb_idb = sba("ident_bf", [128, 128], BF16)
    sb_mask = sba("maskT", [128, 384], BF16)
    sb_oc2 = sba("onesc2", [1, 128], F32)
    sb_oc4 = sba("onesc4", [1, 128], F32)
    sb_1bf = sba("ones_bf", [128, 1], BF16)
    sb_1row = sba("onesrow", [1, 128], BF16)
    sb_ylo = sba("ylo", [1, 1024], BF16)
    sb_sca = sba("scalars", [1, 4], F32)
    sb_sb2 = sba("s012b2", [128, 4], F32)
    sb_lnst = sba("lnst", [128, 12], F32)
    sb_mng = sba("mng", [128, NSL], F32)
    sb_smx = sba("smx", [128, NSL], F32)
    sb_rec = sba("rec", [128, NSL], F32)
    sb_Cq = sba("Cq", [128, 1024], BF16)
    sb_Sq = sba("Sq", [128, 1024], BF16)
    sb_Ck = sba("Ck", [128, 1024], BF16)
    sb_Sk = sba("Sk", [128, 1024], BF16)
    sb_Pe = sba("Pe", [128, 2 * 384], BF16)
    sb_PeT = sba("PeT", [128, 2 * 384], BF16)
    sb_xbh = sba("xbh", [128, 1024], BF16)
    sb_xn = sba("xn_f", [128, 8192], BF16)
    sb_q = sba("q_f", [128, 8192], BF16)
    sb_k = sba("k_f", [128, 8192], BF16)
    sb_v = sba("v_t", [128, 8192], BF16)
    sb_t1 = sba("t1", [128, 16384], BF16)
    sb_t2 = sba("t2", [128, 16384], BF16)
    sb_af = sba("af", [128, 3 * 2048], BF16)
    sb_s = sba("s_t", [128, 4096], BF16)
    sb_slab = [sba(f"slab{i}", [128, 2048], BF16) for i in range(2)]
    sb_wqkr = [sba(f"wqkr{i}", [128, 1024], BF16) for i in range(2)]
    sb_wor = [sba(f"wor{i}", [128, 1024], BF16) for i in range(2)]
    sb_swp = sba("swp", [128, 512], F32)
    sb_tr1 = sba("tr1", [128, 512], F32)
    sb_gb = sba("gb", [128, 512], F32)
    sb_pb = sba("pb", [128, 512], F32)
    sb_ob = [sba(f"outb{i}", [128, 1024], F32) for i in range(4)]

    used = sum(v for _, v in alloc_log)
    assert used <= 212832, (used, sorted(alloc_log, key=lambda kv: -kv[1])[:8])

    a_t1 = nc.lookup_mloc(sb_t1).addr
    sb_xb = [nc.alloc_sbuf_tensor_at(f"xbuf{i}", [128, 1024], F32,
                                     offset=a_t1 + i * 4096) for i in range(2)]
    a_t2 = nc.lookup_mloc(sb_t2).addr
    sb_cvF = [nc.alloc_sbuf_tensor_at(f"cvF{i}", [128, 1024], F32,
                                      offset=a_t2 + i * 4096) for i in range(2)]
    sb_cvB = [nc.alloc_sbuf_tensor_at(f"cvB{i}", [128, 1024], BF16,
                                      offset=a_t2 + 8192 + i * 2048)
              for i in range(2)]
    sb_cvU = [nc.alloc_sbuf_tensor_at(f"cvU{i}", [128, 2048], U8,
                                      offset=a_t2 + i * 2048) for i in range(2)]
    sb_cvO = [nc.alloc_sbuf_tensor_at(f"cvO{i}", [128, 2048], BF16,
                                      offset=a_t2 + 4096 + i * 4096)
              for i in range(2)]
    sb_cvC = [nc.alloc_sbuf_tensor_at(f"cvC{i}", [128, 2048], U8,
                                      offset=a_t2 + 12288 + i * 2048)
              for i in range(2)]
    a_af = nc.lookup_mloc(sb_af).addr
    sb_sqd = nc.alloc_sbuf_tensor_at("sqdump", [128, 1024], F32, offset=a_af)
    a_gb = nc.lookup_mloc(sb_gb).addr
    sb_cvWU = nc.alloc_sbuf_tensor_at("cvWU", [128, 1024], U8, offset=a_gb)
    a_pb = nc.lookup_mloc(sb_pb).addr
    sb_cvWO = nc.alloc_sbuf_tensor_at("cvWO", [128, 1024], BF16, offset=a_pb)
    sb_su8 = [nc.alloc_sbuf_tensor_at(f"su8_{i}", [128, 2048], U8,
                                      offset=(a_gb, a_pb)[i]) for i in range(2)]

    psf = [nc.alloc_psum_tensor(f"psf{i}", [128, 512], F32) for i in range(3)]
    pa = [nc.alloc_psum_tensor(f"pa{i}", [128, 512], F32) for i in range(3)]
    psb = [nc.alloc_psum_tensor(f"psb{i}", [128, 1024], BF16) for i in range(2)]

    B = _Buf
    b_const = B("const")
    b_cs = B("cs_tables")
    b_sca = B("sca")
    b_sb2 = B("sb2")
    b_lnst = [B(f"lnst{i}") for i in range(2)]
    b_xbh = B("xbh")
    b_sqd = B("sqd")
    b_swp = B("swp")
    b_tr1 = B("tr1")
    b_xb = [B(f"xb{i}") for i in range(2)]
    b_xnf = [[B(f"xnf{c}_{n}") for n in range(2)] for c in range(8)]
    b_q = [[B(f"q{c}_{n}") for n in range(2)] for c in range(8)]
    b_k = [[B(f"k{c}_{n}") for n in range(2)] for c in range(8)]
    b_v = [B(f"v{t}") for t in range(8)]
    b_t1 = [B(f"t1_{m}") for m in range(32)]
    b_t2 = [B(f"t2_{m}") for m in range(32)]
    b_af = [B(f"af{s}") for s in range(3)]
    b_s = [B(f"s{m}") for m in range(8)]
    b_slab = [B(f"slab{i}") for i in range(2)]
    b_wqkr = [B(f"wqkr{i}") for i in range(2)]
    b_wor = [B(f"wor{i}") for i in range(2)]
    b_gb = B("gb")
    b_pb = B("pb")
    b_ob = [B(f"ob{i}") for i in range(4)]
    b_psf = [B(f"psf{i}") for i in range(3)]
    b_pa = [B(f"paS{i}") for i in range(3)]
    b_psb = [B(f"psb{i}") for i in range(2)]
    b_Pe = [B(f"Pe{i}") for i in range(2)]
    b_PeT = [B(f"PeT{i}") for i in range(2)]
    b_mng = [B(f"mng{i}") for i in range(NSL)]
    b_smx = [B(f"smx{i}") for i in range(NSL)]
    b_rec = [B(f"rec{i}") for i in range(NSL)]
    b_cvF = [B(f"cvF{i}") for i in range(2)]
    b_cvB = [B(f"cvB{i}") for i in range(2)]
    b_cvU = [B(f"cvU{i}") for i in range(2)]
    b_cvO = [B(f"cvO{i}") for i in range(2)]
    b_cvC = [B(f"cvC{i}") for i in range(2)]
    b_cvWU = B("cvWU")
    b_cvWO = B("cvWO")
    b_su8 = [B(f"su8_{i}") for i in range(2)]
    b_ylo = B("ylo")
    b_wqkb = [B(f"wqkb{m}") for m in range(16)]
    b_wvb = [B(f"wvb{c}") for c in range(8)]
    b_wob = [B(f"wob{c}") for c in range(8)]
    b_w0s = [[B(f"w0s{c}_{h}") for h in range(2)] for c in range(8)]
    b_w1s = [[B(f"w1s{c}_{h}") for h in range(2)] for c in range(32)]
    b_w2s = [B(f"w2s{c}") for c in range(32)]
    b_out = B("out_dram")
    b_dbg = B("dbg")

    def dma(q, dst, src, reads=(), writes=(), pwrites=(), tag=""):
        return sch.op(q, lambda e, d=dst, s=src: e.dma_start(d, s),
                      reads=reads, writes=writes, pwrites=pwrites, dma=True,
                      tag=tag)

    def mm(out, lhsT, rhs, start, stop, reads=(), writes=(), pwrites=(), tag=""):
        return sch.op("tensor",
                      lambda e, o=out, l=lhsT, r=rhs, st=start, sp=stop:
                      e.matmul(o, l, r, start=st, stop=sp),
                      reads=reads, writes=writes, pwrites=pwrites, tag=tag)

    def transp(out, inp, ident, reads=(), writes=(), pwrites=(), tag=""):
        return sch.op("tensor",
                      lambda e, o=out, i=inp, d=ident: e.transpose(o, i, d),
                      reads=reads, writes=writes, pwrites=pwrites, tag=tag)

    bank_tog = [0]

    def next_bank():
        bank_tog[0] = (bank_tog[0] + 1) % 3
        return bank_tog[0]

    slab_i = [0]

    def next_slab():
        slab_i[0] = (slab_i[0] + 1) % 2
        return sb_slab[slab_i[0]], b_slab[slab_i[0]]

    sb_konst = dict(ident_f32=sb_idf, ident_bf=sb_idb, maskT=sb_mask,
                    onesc2=sb_oc2, onesc4=sb_oc4, ones_bf=sb_1bf,
                    onesrow=sb_1row)
    dma("sync", sb_xb[0][:], dx.ap()[0:128, :], writes=(b_xb[0],), tag="ldx0")
    dma("sync", sb_xb[1][:], dx.ap()[128:256, :], writes=(b_xb[1],), tag="ldx1")
    for name in ("ident_f32", "ident_bf", "maskT", "onesc2", "onesc4",
                 "ones_bf", "onesrow"):
        dma("sync", sb_konst[name][:], dconst[name].ap(), pwrites=(b_const,),
            tag=f"ld_{name}")
    for tb, dst in (("Cq", sb_Cq), ("Sq", sb_Sq), ("Ck", sb_Ck), ("Sk", sb_Sk)):
        dma("sync", dst[:], dconst[tb].ap()[:, 0:1024], pwrites=(b_cs,),
            tag=f"ld_{tb}0")
    for j in range(3):
        dma("sync", sb_sca[0:1, j:j + 1], dsc[j].ap(), pwrites=(b_sca,),
            tag=f"ld_s{j}")
    for j in range(3):
        mm(psf[0][:, j:j + 1], sb_oc2[:, :], sb_sca[0:1, j:j + 1], True, True,
           reads=(b_const, b_sca),
           writes=(b_psf[0],) if j == 0 else (), pwrites=(b_psf[0],) if j else (),
           tag=f"bc_s{j}")
    mm(psf[0][:, 3:4], sb_oc4[:, :], sb_sca[0:1, 1:2], True, True,
       reads=(b_const, b_sca), pwrites=(b_psf[0],), tag="bc_s1x4")
    sch.op("vector", lambda e: e.tensor_copy(sb_sb2[:, 0:4], psf[0][:, 0:4]),
           reads=(b_psf[0],), writes=(b_sb2,), tag="ev_s012")

    conv_steps = []

    def conv_cast(dsrc, ddst, n, bufs, tagp):
        for m in range(n):
            i = m % 2

            def s1(m=m, i=i):
                dma("sync", sb_cvF[i][:], dsrc.ap()[m], writes=(b_cvF[i],),
                    tag=f"{tagp}l{m}")

            def s2(m=m, i=i):
                sch.op("scalar", lambda e, i=i: e.copy(sb_cvB[i][:], sb_cvF[i][:]),
                       reads=(b_cvF[i],), writes=(b_cvB[i],), tag=f"{tagp}c{m}")

            def s3(m=m, i=i):
                dma("sync", ddst.ap()[m], sb_cvB[i][:], reads=(b_cvB[i],),
                    writes=(bufs[m],), tag=f"{tagp}s{m}")
            conv_steps.extend([s1, s2, s3])

    def conv_w0(kc, h):
        i = (kc * 2 + h) % 2

        def s1(kc=kc, h=h, i=i):
            dma("sync", sb_cvU[i][:], dk0a.ap()[kc, :, h * 2048:(h + 1) * 2048],
                writes=(b_cvU[i],), tag=f"w0a{kc}_{h}")

        def s1b(kc=kc, h=h, i=i):
            dma("sync", sb_cvC[i][:], dk0b.ap()[kc, :, h * 2048:(h + 1) * 2048],
                writes=(b_cvC[i],), tag=f"w0b{kc}_{h}")

        def s2(i=i):
            sch.op("vector", lambda e, i=i: e.tensor_sub(
                sb_cvO[i][:], sb_cvU[i][:], sb_cvC[i][:]),
                   reads=(b_cvU[i], b_cvC[i]), writes=(b_cvO[i],),
                   tag=f"w0sub{i}")

        def s3(kc=kc, h=h, i=i):
            dma("sync", dw0s.ap()[kc, :, h * 2048:(h + 1) * 2048], sb_cvO[i][:],
                reads=(b_cvO[i],), writes=(b_w0s[kc][h],), tag=f"w0w{kc}_{h}")
        conv_steps.extend([s1, s1b, s2, s3])

    def conv_w1(kc, h):
        i = (kc * 2 + h) % 2

        def s1(kc=kc, h=h, i=i):
            dma("sync", sb_cvU[i][:], dk1.ap()[kc, :, h * 2048:(h + 1) * 2048],
                writes=(b_cvU[i],), tag=f"w1l{kc}_{h}")

        def s2(i=i):
            sch.op("vector", lambda e, i=i: e.tensor_scalar(
                sb_cvO[i][:], sb_cvU[i][:], 2.0, -1.0, op0=ALU.mult, op1=ALU.add),
                   reads=(b_cvU[i],), writes=(b_cvO[i],), tag=f"w1a{i}")

        def s3(kc=kc, h=h, i=i):
            dma("sync", dw1s.ap()[kc, :, h * 2048:(h + 1) * 2048], sb_cvO[i][:],
                reads=(b_cvO[i],), writes=(b_w1s[kc][h],), tag=f"w1w{kc}_{h}")
        conv_steps.extend([s1, s2, s3])

    def conv_w2(kc):
        def s1(kc=kc):
            dma("sync", sb_cvWU[:], dk2.ap()[kc], writes=(b_cvWU,),
                tag=f"w2l{kc}")

        def s2():
            sch.op("vector", lambda e: e.tensor_scalar(
                sb_cvWO[:], sb_cvWU[:], 2.0, -1.0, op0=ALU.mult, op1=ALU.add),
                   reads=(b_cvWU,), writes=(b_cvWO,), tag="w2a")

        def s3(kc=kc):
            dma("sync", dw2s.ap()[kc], sb_cvWO[:], reads=(b_cvWO,),
                writes=(b_w2s[kc],), tag=f"w2w{kc}")
        conv_steps.extend([s1, s2, s3])

    def _cv_to_AC():
        b_cvU[0].absorb([b_cvF[0]])
        b_cvU[1].absorb([b_cvF[0]])
        b_cvO[0].absorb([b_cvF[1]])
        b_cvO[1].absorb([b_cvB[0], b_cvB[1]])

    def _cv_to_FB():
        b_cvF[0].absorb([b_cvU[0], b_cvU[1]])
        b_cvF[1].absorb([b_cvO[0]])
        b_cvB[0].absorb([b_cvO[1]])
        b_cvB[1].absorb([b_cvO[1]])

    conv_cast(dwqk, dwqkb, 16, b_wqkb, "cq")
    conv_cast(dwv, dwvb, 8, b_wvb, "cv")
    conv_steps.append(_cv_to_AC)
    for kc in range(8):
        for h in range(2):
            conv_w0(kc, h)
    conv_steps.append(_cv_to_FB)
    conv_cast(dwo, dwob, 8, b_wob, "co")
    for kc in range(32):
        conv_w2(kc)
    conv_iter = iter(conv_steps)
    conv_cnt = [0]

    def emit_conv(k):
        for _ in range(k):
            step = next(conv_iter, None)
            if step is None:
                return
            if step.__name__ not in ("_cv_to_AC", "_cv_to_FB"):
                conv_cnt[0] += 1
            step()

    def drain_conv(upto):
        while conv_cnt[0] < upto:
            step = next(conv_iter, None)
            assert step is not None
            if step.__name__ not in ("_cv_to_AC", "_cv_to_FB"):
                conv_cnt[0] += 1
            step()

    xnf_r = sb_xn[:].rearrange("p (c n) -> p c n", c=8)

    def ln_tile(seg, tl):
        sl = tl % 2
        xb, bxb = sb_xb[sl], b_xb[sl]
        bst = b_lnst[sl]
        st = sb_lnst[:, sl * 6: sl * 6 + 6]
        xs, mu, vs, v1, ln1, inv = (st[:, i:i + 1] for i in range(6))
        gt = seg * 8 + tl
        if gt >= 2:
            dma("sync", xb[:], dx.ap()[gt * 128:(gt + 1) * 128, :], writes=(bxb,),
                tag=f"ldx{gt}")
        sch.op("vector", lambda e, xb=xb, xs=xs: e.reduce_sum(xs, xb[:], axis=AX.X),
               reads=(bxb,), writes=(bst,), tag=f"ln_sum{gt}")
        sch.op("vector", lambda e, mu=mu, xs=xs: e.tensor_scalar_mul(
            mu, xs, 1.0 / DM),
               reads=(bst,), pwrites=(bst,), tag=f"ln_mu{gt}")
        sch.op("vector", lambda e, xb=xb, mu=mu: e.tensor_scalar(
            xb[:], xb[:], mu, None, op0=ALU.subtract),
               reads=(bst,), writes=(bxb,), tag=f"ln_sub{gt}")
        sch.op("scalar", lambda e, xb=xb, vs=vs: e.activation(
            sb_sqd[:], xb[:], AF.Square, accum_out=vs),
               reads=(bxb,), writes=(b_sqd,), pwrites=(bst,), tag=f"ln_sq{gt}")
        sch.op("vector", lambda e, v1=v1, vs=vs: e.tensor_scalar(
            v1, vs, 1.0 / DM, 1e-5, op0=ALU.mult, op1=ALU.add),
               reads=(bst,), pwrites=(bst,), tag=f"ln_v1{gt}")
        sch.op("scalar", lambda e, ln1=ln1, v1=v1: e.activation(ln1, v1, AF.Ln),
               reads=(bst,), pwrites=(bst,), tag=f"ln_ln{gt}")
        sch.op("scalar", lambda e, inv=inv, ln1=ln1: e.activation(
            inv, ln1, AF.Exp, scale=-0.5),
               reads=(bst,), pwrites=(bst,), tag=f"ln_inv{gt}")
        sch.op("vector", lambda e, xb=xb, inv=inv: e.tensor_scalar_mul(
            xb[:], xb[:], inv),
               reads=(bst,), writes=(bxb,), tag=f"ln_fin{gt}")
        xh = sb_xbh[:]
        sch.op("scalar", lambda e, xh=xh, xb=xb: e.copy(xh, xb[:]),
               reads=(bxb,), writes=(b_xbh,), tag=f"ln_cast{gt}")
        bk, bb = psb[sl], b_psb[sl]
        for cchunk in range(8):
            transp(bk[:, cchunk * 128:(cchunk + 1) * 128],
                   xh[:, cchunk * 128:(cchunk + 1) * 128], sb_idb[:],
                   reads=(b_xbh, b_const),
                   writes=(bb,) if cchunk == 0 else (),
                   pwrites=(bb,) if cchunk else (),
                   tag=f"tp{gt}_{cchunk}")
        dst = xnf_r[:, 0:8, tl * 128:(tl + 1) * 128]
        src = bk[:].rearrange("p (c n) -> p c n", c=8)
        sch.op("scalar", lambda e, d=dst, s=src: e.copy(d, s),
               reads=(bb,),
               pwrites=tuple(b_xnf[c][tl // 4] for c in range(8)),
               tag=f"tpev{gt}")

    swmask = []
    for i in range(16):
        swmask += [2 * i + 1, 2 * i]
    ip_done = set()

    def ip_round(seg, m, nt):
        drain_conv(3 * (m + 1))
        ip_done.add((seg, m, nt))
        wr, bwr = sb_wqkr[m % 2], b_wqkr[m % 2]
        dma("sync", wr[:], dwqkb.ap()[m], reads=(b_wqkb[m],), writes=(bwr,),
            tag=f"ldwqk{seg}_{m}_{nt}")
        nb = next_bank()
        bk, bb = psf[nb], b_psf[nb]
        for kc in range(8):
            mm(bk[:, 0:512], wr[:, kc * 128:(kc + 1) * 128],
               sb_xn[:, kc * 1024 + nt * 512: kc * 1024 + nt * 512 + 512],
               kc == 0, kc == 7,
               reads=(bwr, b_xnf[kc][nt]),
               writes=(bb,) if kc == 0 else (), pwrites=(bb,) if kc else (),
               tag=f"qk{seg}_{m}_{nt}_{kc}")
        isq = m < 8
        c = m if isq else m - 8
        dstt = sb_q if isq else sb_k
        bdst = b_q[c][nt] if isq else b_k[c][nt]
        ctab, stab = (sb_Cq, sb_Sq) if isq else (sb_Ck, sb_Sk)
        cs = slice(nt * 512, nt * 512 + 512)
        sch.op("vector", lambda e, bk=bk: e.stream_shuffle(
            sb_swp[:], bk[:, 0:512], swmask),
               reads=(bb,), writes=(b_swp,), tag=f"rp_sh{seg}_{m}_{nt}")
        sch.op("vector", lambda e, bk=bk, cs=cs, ctab=ctab: e.tensor_mul(
            sb_tr1[:], bk[:, 0:512], ctab[:, cs]),
               reads=(bb, b_cs), writes=(b_tr1,), tag=f"rp_c{seg}_{m}_{nt}")
        sch.op("vector", lambda e, cs=cs, stab=stab: e.tensor_mul(
            sb_swp[:], sb_swp[:], stab[:, cs]),
               reads=(b_swp, b_cs), writes=(b_swp,), tag=f"rp_s{seg}_{m}_{nt}")
        dst = dstt[:, c * 1024 + nt * 512: c * 1024 + nt * 512 + 512]
        sch.op("vector", lambda e, d=dst: e.tensor_add(d, sb_tr1[:], sb_swp[:]),
               reads=(b_tr1, b_swp), writes=(bdst,), tag=f"rp_o{seg}_{m}_{nt}")

    v_done = set()

    def v_round(seg, tt, og):
        v_done.add((seg, tt, og))
        slabs = []
        for q4 in range(2):
            slab, bslab = next_slab()
            src = dwvb.ap()[q4 * 4:(q4 + 1) * 4, :, og * 512:(og + 1) * 512]
            src = src.rearrange("c p f -> p c f")
            dst = slab[:, 0:2048].rearrange("p (c f) -> p c f", c=4)
            dma("sync", dst, src,
                reads=tuple(b_wvb[q4 * 4:(q4 + 1) * 4]), writes=(bslab,),
                tag=f"vld{seg}_{tt}_{og}_{q4}")
            slabs.append((slab, bslab))
        nb = next_bank()
        bk, bb = psf[nb], b_psf[nb]
        for kc in range(8):
            slab, bslab = slabs[kc // 4]
            mm(bk[:, 0:512],
               sb_xn[:, kc * 1024 + tt * 128: kc * 1024 + tt * 128 + 128],
               slab[:, (kc % 4) * 512:(kc % 4) * 512 + 512],
               kc == 0, kc == 7,
               reads=(b_xnf[kc][tt // 4], bslab),
               writes=(bb,) if kc == 0 else (), pwrites=(bb,) if kc else (),
               tag=f"v{seg}_{tt}_{og}_{kc}")
        dst = sb_v[:, tt * 1024 + og * 512: tt * 1024 + og * 512 + 512]
        sch.op("scalar", lambda e, d=dst, bk=bk: e.copy(d, bk[:, 0:512]),
               reads=(bb,), pwrites=(b_v[tt],), tag=f"vev{seg}_{tt}_{og}")

    def f0_round(seg, nt, r):
        slab, bslab = next_slab()
        src = dw0s.ap()[:, :, r * 256:(r + 1) * 256].rearrange("c p f -> p c f")
        dst = slab[:, 0:2048].rearrange("p (c f) -> p c f", c=8)
        dma("sync", dst, src, reads=tuple(b_w0s[kc][r // 8] for kc in range(8)),
            writes=(bslab,), tag=f"f0ld{seg}_{nt}_{r}")
        banks = [next_bank(), next_bank()]
        for kc in range(8):
            for widx in range(2):
                bb = b_psf[banks[widx]]
                mm(psf[banks[widx]][:, 0:512],
                   slab[:, kc * 256 + widx * 128: kc * 256 + widx * 128 + 128],
                   sb_xn[:, kc * 1024 + nt * 512: kc * 1024 + nt * 512 + 512],
                   kc == 0, kc == 7,
                   reads=(bslab, b_xnf[kc][nt]),
                   writes=(bb,) if kc == 0 else (), pwrites=(bb,) if kc else (),
                   tag=f"f0mm{seg}_{nt}_{r}_{kc}_{widx}")
        for widx in range(2):
            m1 = 2 * r + widx
            bk = banks[widx]
            dst2 = sb_t1[:, m1 * 512:(m1 + 1) * 512]
            sch.op("scalar", lambda e, d=dst2, bk=bk: e.activation(
                d, psf[bk][:, 0:512], AF.Relu, scale=sb_sb2[:, 0:1]),
                   reads=(b_psf[bk], b_sb2), writes=(b_t1[m1],),
                   tag=f"f0ev{seg}_{nt}_{m1}")

    def colsum_round(seg, nt):
        nb = next_bank()
        bk, bb = psf[nb], b_psf[nb]
        for kc in range(32):
            mm(bk[0:1, 0:512], sb_1bf[:, 0:1], sb_t1[:, kc * 512:(kc + 1) * 512],
               kc == 0, kc == 31,
               reads=(b_const, b_t1[kc]),
               writes=(bb,) if kc == 0 else (), pwrites=(bb,) if kc else (),
               tag=f"cs{seg}_{nt}_{kc}")
        sch.op("scalar", lambda e, bk=bk: e.activation(
            sb_ylo[0:1, 0:512], bk[0:1, 0:512], AF.Copy, scale=-0.5),
               reads=(bb,), writes=(b_ylo,), tag=f"csyh{seg}_{nt}")
        sch.op("vector", lambda e, bk=bk: e.scalar_tensor_tensor(
            sb_ylo[0:1, 512:1024], bk[0:1, 0:512], -0.5, sb_ylo[0:1, 0:512],
            op0=ALU.mult, op1=ALU.subtract),
               reads=(bb, b_ylo), pwrites=(b_ylo,), tag=f"cslo{seg}_{nt}")

    def f1_round(seg, nt, r):
        banks = [next_bank(), next_bank()]
        for g4 in range(4):
            su8, bsu8 = sb_su8[g4 % 2], b_su8[g4 % 2]
            src = dk1.ap()[g4 * 8:(g4 + 1) * 8, :, r * 256:(r + 1) * 256]
            src = src.rearrange("c p f -> p c f")
            dstu = su8[:, 0:2048].rearrange("p (c f) -> p c f", c=8)
            dma("sync", dstu, src, writes=(bsu8,), tag=f"f1ld{seg}_{nt}_{r}_{g4}")
            slab, bslab = next_slab()
            eng = "vector" if g4 % 2 == 0 else "scalar"
            if eng == "vector":
                sch.op(eng, lambda e, slab=slab, su8=su8: e.tensor_copy(
                    slab[:, 0:2048], su8[:, 0:2048]),
                       reads=(bsu8,), writes=(bslab,),
                       tag=f"f1cast{seg}_{nt}_{r}_{g4}")
            else:
                sch.op(eng, lambda e, slab=slab, su8=su8: e.copy(
                    slab[:, 0:2048], su8[:, 0:2048]),
                       reads=(bsu8,), writes=(bslab,),
                       tag=f"f1cast{seg}_{nt}_{r}_{g4}")
            for kci in range(8):
                kc = g4 * 8 + kci
                for widx in range(2):
                    bb = b_psf[banks[widx]]
                    mm(psf[banks[widx]][:, 0:512],
                       slab[:, kci * 256 + widx * 128: kci * 256 + widx * 128 + 128],
                       sb_t1[:, kc * 512:(kc + 1) * 512],
                       kc == 0, False,
                       reads=(bslab, b_t1[kc]),
                       writes=(bb,) if kc == 0 else (), pwrites=(bb,) if kc else (),
                       tag=f"f1mm{seg}_{nt}_{r}_{kc}_{widx}")
        for widx in range(2):
            bb = b_psf[banks[widx]]
            mm(psf[banks[widx]][:, 0:512], sb_1row[:, :], sb_ylo[0:1, 0:512],
               False, False, reads=(b_const, b_ylo), pwrites=(bb,),
               tag=f"f1ch{seg}_{nt}_{r}_{widx}")
            mm(psf[banks[widx]][:, 0:512], sb_1row[:, :], sb_ylo[0:1, 512:1024],
               False, True, reads=(b_const, b_ylo), pwrites=(bb,),
               tag=f"f1cl{seg}_{nt}_{r}_{widx}")
        for widx in range(2):
            m2 = 2 * r + widx
            bk = banks[widx]
            dst = sb_t2[:, m2 * 512:(m2 + 1) * 512]
            sch.op("scalar", lambda e, d=dst, bk=bk: e.activation(
                d, psf[bk][:, 0:512], AF.Relu, scale=sb_sb2[:, 3:4]),
                   reads=(b_psf[bk], b_sb2), writes=(b_t2[m2],),
                   tag=f"f1ev{seg}_{nt}_{m2}")

    def af_slot(seg, hp):
        return (seg * 4 + hp) % 3

    def abs_u8():
        for i in range(2):
            b_su8[i].absorb([b_gb, b_pb, b_cvWU, b_cvWO])

    def abs_gp():
        b_gb.absorb([b_su8[0], b_su8[1]])
        b_pb.absorb([b_su8[0], b_su8[1]])

    def c_round(seg, nt, r):
        banks = [next_bank(), next_bank()]
        for widx in range(2):
            mo = 2 * r + widx
            wr, bwr = sb_wor[mo % 2], b_wor[mo % 2]
            dma("sync", wr[:], dwob.ap()[mo], reads=(b_wob[mo],), writes=(bwr,),
                tag=f"ldwo{seg}_{nt}_{mo}")
        for g in range(2):
            sl = af_slot(seg, nt * 2 + g)
            for kc in range(8):
                for widx in range(2):
                    bb = b_psf[banks[widx]]
                    wr, bwr = sb_wor[(2 * r + widx) % 2], b_wor[(2 * r + widx) % 2]
                    mm(psf[banks[widx]][:, g * 256: g * 256 + 256],
                       wr[:, kc * 128:(kc + 1) * 128],
                       sb_af[:, sl * 2048 + kc * 256: sl * 2048 + kc * 256 + 256],
                       kc == 0 and g == 0, kc == 7 and g == 1,
                       reads=(bwr, b_af[sl]),
                       writes=(bb,) if (kc == 0 and g == 0) else (),
                       pwrites=(bb,) if (kc or g) else (),
                       tag=f"op{seg}_{nt}_{r}_{g}_{kc}_{widx}")
        for widx in range(2):
            mo = 2 * r + widx
            bk = banks[widx]
            dst = sb_s[:, mo * 512:(mo + 1) * 512]
            sch.op("scalar", lambda e, d=dst, bk=bk: e.copy(d, psf[bk][:, 0:512]),
                   reads=(b_psf[bk],), writes=(b_s[mo],), tag=f"opev{seg}_{nt}_{mo}")

    def ob_loads(seg, nt):
        for j in range(4):
            gt = seg * 8 + nt * 4 + j
            dma("sync", sb_ob[j][:], dx.ap()[gt * 128:(gt + 1) * 128, :],
                writes=(b_ob[j],), tag=f"ldxo{seg}_{nt}_{j}")

    def f2_round(seg, nt, r):
        banks = [next_bank(), next_bank()]
        for g4 in range(4):
            slab, bslab = next_slab()
            src = dw2s.ap()[g4 * 8:(g4 + 1) * 8, :, r * 256:(r + 1) * 256]
            src = src.rearrange("c p f -> p c f")
            dst = slab[:, 0:2048].rearrange("p (c f) -> p c f", c=8)
            dma("sync", dst, src,
                reads=tuple(b_w2s[g4 * 8 + j] for j in range(8)),
                writes=(bslab,), tag=f"f2ld{seg}_{nt}_{r}_{g4}")
            for kci in range(8):
                kc = g4 * 8 + kci
                for widx in range(2):
                    bb = b_psf[banks[widx]]
                    mm(psf[banks[widx]][:, 0:512],
                       slab[:, kci * 256 + widx * 128: kci * 256 + widx * 128 + 128],
                       sb_t2[:, kc * 512:(kc + 1) * 512],
                       kc == 0, kc == 31,
                       reads=(bslab, b_t2[kc]),
                       writes=(bb,) if kc == 0 else (), pwrites=(bb,) if kc else (),
                       tag=f"f2mm{seg}_{nt}_{r}_{kc}_{widx}")
        for widx in range(2):
            m3 = 2 * r + widx
            bk = banks[widx]
            bb = b_psf[bk]
            sch.op("scalar", lambda e, bk=bk: e.activation(
                sb_gb[:], psf[bk][:, 0:512], AF.Silu, scale=sb_sb2[:, 2:3]),
                   reads=(bb, b_sb2), writes=(b_gb,),
                   tag=f"silu{seg}_{nt}_{m3}")
            ssl = sb_s[:, m3 * 512:(m3 + 1) * 512]
            sch.op("vector", lambda e, ssl=ssl: e.tensor_mul(
                sb_pb[:], sb_gb[:], ssl),
                   reads=(b_gb, b_s[m3]), writes=(b_pb,),
                   tag=f"prod{seg}_{nt}_{m3}")
            for j in range(4):
                transp(psf[bk][:, j * 128:(j + 1) * 128],
                       sb_pb[:, j * 128:(j + 1) * 128], sb_idf[:],
                       reads=(b_pb, b_const),
                       writes=(bb,) if j == 0 else (),
                       pwrites=(bb,) if j else (),
                       tag=f"ptr{seg}_{nt}_{m3}_{j}")
            for j in range(4):
                osl = sb_ob[j][:, m3 * 128:(m3 + 1) * 128]
                sch.op("vector",
                       lambda e, osl=osl, bk=bk, j=j: e.tensor_add(
                           osl, psf[bk][:, j * 128:(j + 1) * 128], osl),
                       reads=(bb, b_ob[j]), pwrites=(b_ob[j],),
                       tag=f"radd{seg}_{nt}_{m3}_{j}")

    def ob_stores(seg, nt):
        for j in range(4):
            gt = seg * 8 + nt * 4 + j
            dma("sync", dout.ap()[gt * 128:(gt + 1) * 128, :], sb_ob[j][:],
                reads=(b_ob[j],), pwrites=(b_out,), tag=f"stout{seg}_{nt}_{j}")

    def runs_of(il, jj0):
        slots = [(il - 2 + jj) % 8 for jj in range(jj0, 3)]
        runs = []
        st = 0
        for j in range(1, len(slots)):
            if slots[j] != slots[j - 1] + 1:
                runs.append((jj0 + st, slots[st], j - st))
                st = j
        runs.append((jj0 + st, slots[st], len(slots) - st))
        return runs

    class Attn:
        def __init__(self, seg):
            self.seg = seg
            self.it = 0

        def pump(self, n):
            for _ in range(n):
                for lag, fn in STAGES:
                    u = self.it - lag
                    if 0 <= u < 128:
                        fn(self, u)
                self.it += 1

        def drain(self):
            if self.it < 128 + 5:
                self.pump(128 + 5 - self.it)

        def ctx(self, u):
            il, h = u // 16, u % 16
            gi = self.seg * 8 + il
            jj0 = max(0, 2 - gi)
            return il, h, gi, jj0

    def st_mm(a, u):
        il, h, gi, jj0 = a.ctx(u)
        c, off = h // 2, (h % 2) * 64
        s3 = u % 3
        assert (a.seg, c, il // 4) in ip_done, ("q not ready", a.seg, u)
        bank, bb = pa[s3], b_pa[s3]
        runs = runs_of(il, jj0)
        for jj in range(jj0, 3):
            sl = (il - 2 + jj) % 8
            if il - 2 + jj >= 0:
                assert (a.seg, 8 + c, sl // 4) in ip_done, ("k", a.seg, u, sl)
            else:
                assert (a.seg, 8 + c, sl // 4) not in ip_done, \
                    ("k overwritten", a.seg, u, sl)
        first = True
        for jjs, sl0, ln in runs:
            kread = [b_k[c][0 if sl0 < 4 else 1]]
            if sl0 + ln - 1 >= 4 and sl0 < 4:
                kread.append(b_k[c][1])
            mm(bank[:, jjs * 128: (jjs + ln) * 128],
               sb_q[off:off + 64, c * 1024 + il * 128: c * 1024 + il * 128 + 128],
               sb_k[off:off + 64, c * 1024 + sl0 * 128: c * 1024 + (sl0 + ln) * 128],
               first, (jjs, sl0, ln) == runs[-1],
               reads=tuple([b_q[c][il // 4]] + kread),
               writes=(bb,) if first else (),
               pwrites=(bb,) if not first else (),
               tag=f"sq{a.seg}_{il}_{h}")
            first = False

    def st_msk(a, u):
        il, h, gi, jj0 = a.ctx(u)
        s3 = u % 3
        bank, bb = pa[s3], b_pa[s3]
        sch.op("vector", lambda e, bank=bank, jj0=jj0: e.tensor_add(
            bank[:, jj0 * 128:384], bank[:, jj0 * 128:384],
            sb_mask[:, jj0 * 128:384]),
               reads=(bb, b_const), pwrites=(bb,), tag=f"msk{a.seg}_{il}_{h}")

    def st_max(a, u):
        il, h, gi, jj0 = a.ctx(u)
        s3, s5 = u % 3, u % NSL
        bank, bb = pa[s3], b_pa[s3]
        sch.op("vector", lambda e, bank=bank, jj0=jj0, s5=s5: e.reduce_max(
            sb_mng[:, s5:s5 + 1], bank[:, jj0 * 128:384], axis=AX.X,
            negate=True),
               reads=(bb,), writes=(b_mng[s5],), tag=f"mx{a.seg}_{il}_{h}")

    def st_exp(a, u):
        il, h, gi, jj0 = a.ctx(u)
        s3, s5, s2 = u % 3, u % NSL, u % 2
        bank, bb = pa[s3], b_pa[s3]
        pes = sb_Pe[:, s2 * 384 + jj0 * 128: s2 * 384 + 384]
        sch.op("scalar", lambda e, pes=pes, bank=bank, jj0=jj0, s5=s5:
               e.activation(pes, bank[:, jj0 * 128:384], AF.Exp,
                            bias=sb_mng[:, s5:s5 + 1],
                            accum_out=sb_smx[:, s5:s5 + 1]),
               reads=(bb, b_mng[s5]),
               writes=(b_Pe[s2], b_smx[s5]), tag=f"exp{a.seg}_{il}_{h}")

    def st_rcp(a, u):
        s5 = u % NSL
        sch.op("vector", lambda e, s5=s5: e.reciprocal(
            sb_rec[:, s5:s5 + 1], sb_smx[:, s5:s5 + 1]),
               reads=(b_smx[s5],), writes=(b_rec[s5],), tag=f"rcp{a.seg}_{u}")

    def st_pnorm(a, u):
        il, h, gi, jj0 = a.ctx(u)
        s5, s2 = u % NSL, u % 2
        pes = sb_Pe[:, s2 * 384 + jj0 * 128: s2 * 384 + 384]
        sch.op("vector", lambda e, pes=pes, s5=s5: e.tensor_scalar_mul(
            pes, pes, sb_rec[:, s5:s5 + 1]),
               reads=(b_rec[s5],), pwrites=(b_Pe[s2],),
               tag=f"pn{a.seg}_{il}_{h}")

    def st_tp(a, u):
        il, h, gi, jj0 = a.ctx(u)
        s2 = u % 2
        bank, bb = psb[s2], b_psb[s2]
        for jj in range(jj0, 3):
            transp(bank[:, jj * 128:(jj + 1) * 128],
                   sb_Pe[:, s2 * 384 + jj * 128: s2 * 384 + jj * 128 + 128],
                   sb_idb[:],
                   reads=(b_Pe[s2], b_const),
                   writes=(bb,) if jj == jj0 else (),
                   pwrites=(bb,) if jj != jj0 else (),
                   tag=f"pt{a.seg}_{il}_{h}_{jj}")

    def st_ptev(a, u):
        il, h, gi, jj0 = a.ctx(u)
        s2 = u % 2
        bank, bb = psb[s2], b_psb[s2]
        pets = sb_PeT[:, s2 * 384 + jj0 * 128: s2 * 384 + 384]
        sch.op("scalar", lambda e, pets=pets, bank=bank, jj0=jj0: e.copy(
            pets, bank[:, jj0 * 128:384]),
               reads=(bb,), writes=(b_PeT[s2],), tag=f"ptev{a.seg}_{il}_{h}")

    def st_avev(a, u):
        il, h, gi, jj0 = a.ctx(u)
        s2 = u % 2
        po = (h % 2) * 64
        nb = next_bank()
        bank, bb = psf[nb], b_psf[nb]
        first = True
        for jj in range(jj0, 3):
            jt = (il - 2 + jj) % 8
            if il - 2 + jj >= 0:
                assert (a.seg, jt, 0) in v_done and \
                    (a.seg, jt, 1) in v_done, ("v", a.seg, u, jt)
            else:
                assert (a.seg, jt, 0) not in v_done and \
                    (a.seg, jt, 1) not in v_done, ("v ow", a.seg, u, jt)
            mm(bank[po:po + 64, 384:512],
               sb_v[:, jt * 1024 + h * 64: jt * 1024 + h * 64 + 64],
               sb_PeT[:, s2 * 384 + jj * 128: s2 * 384 + jj * 128 + 128],
               first, jj == 2,
               reads=(b_PeT[s2], b_v[jt]),
               writes=(bb,) if first else (),
               pwrites=(bb,) if not first else (),
               tag=f"av{a.seg}_{il}_{h}_{jj}")
            first = False
        sl = af_slot(a.seg, il // 2)
        base = sl * 2048 + (h // 2) * 256 + (il % 2) * 128
        dst = sb_af[po:po + 64, base: base + 128]
        sch.op("scalar", lambda e, d=dst, bank=bank, po=po: e.copy(
            d, bank[po:po + 64, 384:512]),
               reads=(bb,), pwrites=(b_af[sl],),
               tag=f"aev{a.seg}_{il}_{h}")

    STAGES = [(4, st_ptev), (4, st_avev),
              (3, st_rcp), (3, st_pnorm), (3, st_tp),
              (2, st_exp), (1, st_msk), (1, st_max),
              (0, st_mm)]

    for tl in range(4):
        ln_tile(0, tl)
        emit_conv(6)
    mi = 0
    for tl in range(4, 8):
        ln_tile(0, tl)
        emit_conv(2)
        for _ in range(4):
            ip_round(0, mi, 0)
            mi += 1
            emit_conv(1)
    for m1 in range(8):
        b_t1[m1].absorb([b_xb[0], b_xb[1]])
    for m in range(16):
        ip_round(0, m, 1)
        emit_conv(2)
    drain_conv(72)
    for og in range(2):
        for tt in range(8):
            v_round(0, tt, og)
            emit_conv(2)
    b_af[0].absorb([b_sqd])
    b_af[1].absorb([b_sqd])
    at0 = Attn(0)
    drain_conv(136)
    for r in range(16):
        f0_round(0, 0, r)
        at0.pump(1)
        emit_conv(4)
    while at0.it < 72:
        at0.pump(1)
        emit_conv(4)
    drain_conv(160)
    if "af" in ddbg:
        dma("sync", ddbg["af"].ap()[:, 0:4096], sb_af[:, 0:4096],
            reads=tuple(b_af), pwrites=(b_dbg,), tag="dbg_af0")
    for r in range(4):
        c_round(0, 0, r)
        at0.pump(1)
        emit_conv(4)
    while at0.it < 96:
        at0.pump(1)
        emit_conv(4)
    for m2 in range(32):
        b_t2[m2].absorb([b_cvF[0], b_cvF[1], b_cvB[0], b_cvB[1],
                         b_cvU[0], b_cvU[1], b_cvO[0], b_cvO[1],
                         b_cvC[0], b_cvC[1]])
    drain_conv(256)
    abs_u8()
    colsum_round(0, 0)
    for r in range(16):
        f1_round(0, 0, r)
        at0.pump(3)
    abs_gp()
    if "xn_f" in ddbg:
        dma("sync", ddbg["xn_f"].ap(), sb_xn[:],
            reads=tuple(b for row in b_xnf for b in row), pwrites=(b_dbg,),
            tag="dbg_xn")
    if "q_f" in ddbg:
        dma("sync", ddbg["q_f"].ap(), sb_q[:],
            reads=tuple(b for row in b_q for b in row), pwrites=(b_dbg,),
            tag="dbg_q")
    if "k_f" in ddbg:
        dma("sync", ddbg["k_f"].ap(), sb_k[:],
            reads=tuple(b for row in b_k for b in row), pwrites=(b_dbg,),
            tag="dbg_k")
    if "v_t" in ddbg:
        dma("sync", ddbg["v_t"].ap(), sb_v[:], reads=tuple(b_v),
            pwrites=(b_dbg,), tag="dbg_v")
    if "s_t" in ddbg:
        dma("sync", ddbg["s_t"].ap(), sb_s[:], reads=tuple(b_s),
            pwrites=(b_dbg,), tag="dbg_s")
    if "t1" in ddbg:
        dma("sync", ddbg["t1"].ap(), sb_t1[:], reads=tuple(b_t1),
            pwrites=(b_dbg,), tag="dbg_t1")
    if "afX" in ddbg:
        dma("sync", ddbg["af"].ap(), sb_af[:], reads=tuple(b_af),
            pwrites=(b_dbg,), tag="dbg_af")
    ob_loads(0, 0)
    for r in range(4):
        f2_round(0, 0, r)
        at0.pump(2)
    at0.drain()
    ob_stores(0, 0)
    if "t2" in ddbg:
        dma("sync", ddbg["t2"].ap(), sb_t2[:], reads=tuple(b_t2),
            pwrites=(b_dbg,), tag="dbg_t2")
    for r in range(16):
        f0_round(0, 1, r)
    for r in range(4):
        c_round(0, 1, r)
    abs_u8()
    colsum_round(0, 1)
    for r in range(16):
        f1_round(0, 1, r)
    abs_gp()
    ob_loads(0, 1)
    for tb, dst in (("Cq", sb_Cq), ("Sq", sb_Sq), ("Ck", sb_Ck), ("Sk", sb_Sk)):
        dma("sync", dst[:], dconst[tb].ap()[:, 1024:2048], writes=(b_cs,),
            tag=f"ld_{tb}1")

    def _absorb_sqd():
        b_sqd.absorb([b_af[0], b_af[1]])

    def _absorb_xb():
        for i in range(2):
            b_xb[i].absorb([b_t1[c] for c in range(8)])
    tail = [_absorb_sqd, _absorb_xb]
    for tl in range(4):
        tail.append(lambda tl=tl: ln_tile(1, tl))
    for m in range(16):
        tail.append(lambda m=m: ip_round(1, m, 0))
    for tl in range(4, 8):
        tail.append(lambda tl=tl: ln_tile(1, tl))
    for m in range(8):
        tail.append(lambda m=m: ip_round(1, m, 1))
    for tt in range(4):
        for og in range(2):
            tail.append(lambda tt=tt, og=og: v_round(1, tt, og))
    ti = iter(tail)
    for r in range(4):
        f2_round(0, 1, r)
        for _ in range(10):
            step = next(ti, None)
            if step is not None:
                step()
    for step in ti:
        step()
    ob_stores(0, 1)
    for m1 in range(8):
        b_t1[m1].absorb([b_xb[0], b_xb[1]])
    b_af[0].absorb([b_sqd])
    b_af[1].absorb([b_sqd])
    at1 = Attn(1)
    deferred = []
    for m in range(8, 16):
        deferred.append(lambda m=m: ip_round(1, m, 1))
    for tt in range(4, 8):
        for og in range(2):
            deferred.append(lambda tt=tt, og=og: v_round(1, tt, og))
    di = iter(deferred)
    for r in range(16):
        f0_round(1, 0, r)
        at1.pump(1)
    abs_u8()
    colsum_round(1, 0)
    for r in range(16):
        f1_round(1, 0, r)
        at1.pump(4)
        if at1.it >= 40:
            for _ in range(2):
                step = next(di, None)
                if step is not None:
                    step()
        if r >= 13:
            c_round(1, 0, r - 13)
    for step in di:
        step()
    c_round(1, 0, 3)
    abs_gp()
    ob_loads(1, 0)
    for r in range(4):
        f2_round(1, 0, r)
        at1.pump(2)
    ob_stores(1, 0)
    for r in range(16):
        f0_round(1, 1, r)
        at1.pump(1)
    abs_u8()
    colsum_round(1, 1)
    for r in range(16):
        f1_round(1, 1, r)
        at1.pump(2)
    at1.drain()
    abs_gp()
    for r in range(4):
        c_round(1, 1, r)
    ob_loads(1, 1)
    for r in range(4):
        f2_round(1, 1, r)
    ob_stores(1, 1)

    sch.fin([b_out, b_dbg])
    return _emit(nc, sch)


def _emit(nc, sch):
    sch.finalize()
    sems = {}
    for eng in _Sched.ENGINES:
        sems[(eng, None)] = nc.alloc_semaphore(f"sem_{eng}")
        for ln in range(_Sched.NLANE):
            sems[(eng, ln)] = nc.alloc_semaphore(f"sem_{eng}_d{ln}")

    with nc.Block() as block:
        @block.tensor
        def _(e):
            sch.run("tensor", e, sems)

        @block.vector
        def _(e):
            sch.run("vector", e, sems)

        @block.scalar
        def _(e):
            sch.run("scalar", e, sems)

        @block.gpsimd
        def _(e):
            sch.run("gpsimd", e, sems)

        @block.sync
        def _(e):
            sch.run("sync", e, sems)

    return nc


def _prep(L0_attn_query, L0_attn_in_proj_w, L0_attn_out_proj_w,
          L0_ffn_k0, L0_ffn_k1, L0_ffn_k2, L0_ffn_s0, L0_ffn_s1, L0_ffn_s2):
    x = np.ascontiguousarray(np.asarray(L0_attn_query, np.float32))
    win = np.ascontiguousarray(np.asarray(L0_attn_in_proj_w, np.float32))
    wout = np.ascontiguousarray(np.asarray(L0_attn_out_proj_w, np.float32))
    wqk = np.ascontiguousarray(
        win[:2048].T.reshape(8, 128, 16, 128).transpose(2, 1, 0, 3)
        .reshape(16, 128, 1024))
    wv = np.ascontiguousarray(win[2048:].T.reshape(8, 128, 1024))
    wo = np.ascontiguousarray(
        wout.T.reshape(8, 128, 8, 128).transpose(2, 1, 0, 3)
        .reshape(8, 128, 1024))
    k0 = np.ascontiguousarray(np.asarray(L0_ffn_k0)).view(np.uint8)
    k0a = np.ascontiguousarray(k0[:1024].reshape(8, 128, 4096))
    k0b = np.ascontiguousarray(k0[1024:].reshape(8, 128, 4096))
    k1 = np.ascontiguousarray(np.asarray(L0_ffn_k1)).view(np.uint8)
    k1 = k1.reshape(32, 128, 4096)
    k2 = np.ascontiguousarray(np.asarray(L0_ffn_k2)).view(np.uint8)
    k2 = k2.reshape(32, 128, 1024)
    shared = dict(wqk=wqk, wv=wv, wo=wo, k0a=k0a, k0b=k0b, k1=k1, k2=k2,
                  s0=np.asarray(L0_ffn_s0, np.float32).reshape(1, 1),
                  s1=np.asarray(L0_ffn_s1, np.float32).reshape(1, 1),
                  s2=np.asarray(L0_ffn_s2, np.float32).reshape(1, 1))
    for name, arr in _host_constants().items():
        shared["c_" + name] = np.ascontiguousarray(arr)
    return x, shared


def kernel(L0_attn_query, L0_attn_in_proj_w, L0_attn_out_proj_w,
           L0_ffn_k0, L0_ffn_k1, L0_ffn_k2, L0_ffn_s0, L0_ffn_s1, L0_ffn_s2):
    x, shared = _prep(L0_attn_query, L0_attn_in_proj_w, L0_attn_out_proj_w,
                      L0_ffn_k0, L0_ffn_k1, L0_ffn_k2,
                      L0_ffn_s0, L0_ffn_s1, L0_ffn_s2)
    nc = build_nc()
    in_maps = [dict(shared, x=np.ascontiguousarray(x[c])) for c in range(8)]
    res = run_bass_kernel_spmd(nc, in_maps, core_ids=list(range(8)))
    return np.stack([np.asarray(r["out"], np.float32) for r in res.results],
                    axis=0)


# version 13
# speedup vs baseline: 1.3444x; 1.3060x over previous
import numpy as np
import ml_dtypes
import concourse.bass as bass
import concourse.mybir as mybir
from concourse.bass_utils import run_bass_kernel_spmd

F32 = mybir.dt.float32
BF16 = mybir.dt.bfloat16
U8 = mybir.dt.uint8
AF = mybir.ActivationFunctionType
ALU = mybir.AluOpType
AX = mybir.AxisListType
BFD = ml_dtypes.bfloat16

T_SEQ = 2048
DM = 1024
NH = 16
NTT = 16
NEG_BIG = -1.0e30
N_SLOT = 6

class _Rec:
    __slots__ = ("eng", "dma", "thunk", "deps", "marked", "value", "waits",
                 "seq", "tag", "lane", "lane_pred")

    def __init__(self, eng, dma, thunk, seq, tag):
        self.eng = eng
        self.dma = dma
        self.thunk = thunk
        self.deps = []
        self.marked = False
        self.value = None
        self.waits = None
        self.seq = seq
        self.tag = tag
        self.lane = None
        self.lane_pred = None


class _Buf:
    __slots__ = ("name", "writers", "readers")

    def __init__(self, name):
        self.name = name
        self.writers = []
        self.readers = []

    def absorb(self, others):
        for o in others:
            self.writers = self.writers + o.writers + o.readers


class _Sched:
    ENGINES = ("tensor", "vector", "scalar", "gpsimd", "sync")
    NLANE = 8

    def __init__(self):
        self.recs = {e: [] for e in self.ENGINES}
        self.nseq = 0
        self.dma_count = {e: 0 for e in self.ENGINES}
        self.lane_last = {}

    def op(self, eng, thunk, reads=(), writes=(), pwrites=(), dma=False, tag=""):
        r = _Rec(eng, dma, thunk, self.nseq, tag)
        self.nseq += 1
        if dma:
            lane = self.dma_count[eng] % self.NLANE
            self.dma_count[eng] += 1
            r.lane = lane
            r.lane_pred = self.lane_last.get((eng, lane))
            self.lane_last[(eng, lane)] = r
        deps = {}
        for b in reads:
            for d in b.writers:
                deps[id(d)] = d
        for b in tuple(writes) + tuple(pwrites):
            for d in b.writers:
                deps[id(d)] = d
            for d in b.readers:
                deps[id(d)] = d
        if dma and r.lane_pred is not None:
            deps[id(r.lane_pred)] = r.lane_pred
        for d in deps.values():
            if d is r:
                continue
            if (not d.dma) and (not dma) and d.eng == eng == "tensor":
                continue
            assert d.seq < r.seq, (d.tag, r.tag)
            r.deps.append(d)
        for b in reads:
            b.readers.append(r)
        for b in writes:
            b.writers = [r]
            b.readers = []
        for b in pwrites:
            b.writers.append(r)
        self.recs[eng].append(r)
        return r

    def finalize(self):
        for lst in self.recs.values():
            for r in lst:
                for d in r.deps:
                    d.marked = True
                if r.dma:
                    r.marked = True
        for lst in self.recs.values():
            cnt = {}
            for r in lst:
                if r.marked:
                    key = ("d", r.lane) if r.dma else "c"
                    cnt[key] = cnt.get(key, 0) + 1
                    r.value = cnt[key] * (16 if r.dma else 1)
        for lst in self.recs.values():
            hw = {}
            for r in lst:
                ws = []
                for d in r.deps:
                    key = (d.eng, d.lane) if d.dma else (d.eng, None)
                    assert d.value is not None
                    if hw.get(key, 0) < d.value:
                        ws.append((key, d.value))
                        hw[key] = d.value
                ws.sort(key=lambda kv: kv[1])
                r.waits = ws

    def fin(self, bufs, tag="fin"):
        r = self.op("sync", lambda e: e.nop(), reads=tuple(bufs), tag=tag)
        have = {id(d) for d in r.deps}
        for rec in self.lane_last.values():
            if id(rec) not in have:
                r.deps.append(rec)
                have.add(id(rec))
        return r

    def run(self, eng_name, eng, sems):
        for r in self.recs[eng_name]:
            for key, val in r.waits:
                eng.wait_ge(sems[key], val)
            ins = r.thunk(eng)
            if r.marked:
                assert ins is not None, r.tag
                key = (eng_name, r.lane) if r.dma else (eng_name, None)
                ins.then_inc(sems[key], 16 if r.dma else 1)


def _host_constants():
    c = {}
    c["ident_f32"] = np.eye(128, dtype=np.float32)
    c["ident_bf"] = np.eye(128, dtype=np.float32).astype(BFD)
    i_of_r = (np.arange(128) % 64) // 2
    freqs = np.exp(-np.log(10000.0) * i_of_r / 32.0)
    t = np.arange(T_SEQ, dtype=np.float64)
    ang = freqs[:, None] * t[None, :]
    cos = np.cos(ang)
    sin = np.sin(ang)
    sgn = np.where(np.arange(128) % 2 == 0, -1.0, 1.0)[:, None]
    c["Cq"] = (0.125 * cos).astype(np.float32)
    c["Sq"] = (0.125 * sgn * sin).astype(np.float32)
    c["Ck"] = cos.astype(np.float32)
    c["Sk"] = (sgn * sin).astype(np.float32)
    tq = np.arange(128)[:, None]
    cw = np.arange(384)[None, :]
    valid = (cw > tq + 6) & (cw <= tq + 256)
    c["maskT"] = np.where(valid, 0.0, NEG_BIG).astype(np.float32)
    c["ones_bf"] = np.ones((128, 1), np.float32).astype(BFD)
    c["neg1"] = np.full((128, 1), -1.0, np.float32)
    c["onesc2"] = np.full((1, 128), 2.0, np.float32)
    c["one11"] = np.ones((1, 1), np.float32)
    c["onesc1"] = np.ones((1, 128), np.float32)
    return c


_CONST_SPECS = [
    ("ident_f32", (128, 128), F32), ("ident_bf", (128, 128), BF16),
    ("Cq", (128, 2048), F32), ("Sq", (128, 2048), F32),
    ("Ck", (128, 2048), F32), ("Sk", (128, 2048), F32),
    ("maskT", (128, 384), F32), ("ones_bf", (128, 1), BF16),
    ("neg1", (128, 1), F32),
    ("onesc2", (1, 128), F32), ("one11", (1, 1), F32),
    ("onesc1", (1, 128), F32),
]


def build_nc(debug=(), stage=4):
    nc = bass.Bass("TRN2", target_bir_lowering=False, debug=False)
    sch = _Sched()

    dx = nc.dram_tensor("x", [T_SEQ, DM], F32, kind="ExternalInput")
    dwqk = nc.dram_tensor("wqk", [16, 128, 8, 128], F32, kind="ExternalInput")
    dwv = nc.dram_tensor("wv", [8, 128, 1024], F32, kind="ExternalInput")
    dwo = nc.dram_tensor("wo", [8, 128, 8, 128], F32, kind="ExternalInput")
    dk0a = nc.dram_tensor("k0a", [8, 128, 4096], U8, kind="ExternalInput")
    dk0b = nc.dram_tensor("k0b", [8, 128, 4096], U8, kind="ExternalInput")
    dk1 = nc.dram_tensor("k1", [32, 128, 4096], U8, kind="ExternalInput")
    dk2 = nc.dram_tensor("k2", [32, 128, 1024], U8, kind="ExternalInput")
    dsc = [nc.dram_tensor(f"s{j}", [1, 1], F32, kind="ExternalInput")
           for j in range(3)]
    dconst = {name: nc.dram_tensor("c_" + name, list(shp), dt, kind="ExternalInput")
              for name, shp, dt in _CONST_SPECS}
    dout = nc.dram_tensor("out", [T_SEQ, DM], F32, kind="ExternalOutput")
    dw0s = nc.dram_tensor("w0s", [8, 128, 4096], BF16, kind="Internal")
    dw1s = nc.dram_tensor("w1s", [32, 128, 4096], BF16, kind="Internal")
    dw2s = nc.dram_tensor("w2s", [32, 128, 1024], BF16, kind="Internal")

    ddbg = {}
    for name, shp, dt in [
        ("xn_f", [128, 16384], BF16), ("q_f", [128, 16384], BF16),
        ("k_f", [128, 16384], BF16), ("v_t", [128, 16384], BF16),
        ("a_t", [128, 16384], BF16),
        ("a_f", [128, 16384], BF16), ("s_t", [128, 16384], F32),
        ("t1", [128, 16384], BF16), ("t2", [128, 16384], BF16),
        ("w0s", [8, 128, 4096], BF16), ("w1s", [32, 128, 4096], BF16),
    ]:
        if name in debug:
            ddbg[name] = nc.dram_tensor("dbg_" + name, shp, dt,
                                        kind="ExternalOutput")

    sb_xn = nc.alloc_sbuf_tensor("xn_f", [128, 16384], BF16)
    sb_mask = nc.alloc_sbuf_tensor("maskT", [128, 384], F32)
    sb_idf = nc.alloc_sbuf_tensor("ident_f32", [128, 128], F32)
    sb_idb = nc.alloc_sbuf_tensor("ident_bf", [128, 128], BF16)
    sb_1bf = nc.alloc_sbuf_tensor("ones_bf", [128, 1], BF16)
    sb_neg1 = nc.alloc_sbuf_tensor("neg1", [128, 1], F32)
    sb_oc2 = nc.alloc_sbuf_tensor("onesc2", [1, 128], F32)
    sb_one11 = nc.alloc_sbuf_tensor("one11", [1, 1], F32)
    sb_oc1 = nc.alloc_sbuf_tensor("onesc1", [1, 128], F32)
    sb_sca = nc.alloc_sbuf_tensor("scalars", [1, 4], F32)
    sb_sb2 = nc.alloc_sbuf_tensor("s012b2", [128, 4], F32)
    sb_lnst = nc.alloc_sbuf_tensor("lnst", [128, 12], F32)
    sb_rec = nc.alloc_sbuf_tensor("rec", [128, N_SLOT], F32)
    sb_smx = nc.alloc_sbuf_tensor("softmax_sums", [128, N_SLOT], F32)
    sb_mng = nc.alloc_sbuf_tensor("mneg", [128, N_SLOT], F32)

    sb_st = nc.alloc_sbuf_tensor("s_t", [128, 16384], F32)
    a1 = nc.lookup_mloc(sb_st).addr
    sb_q = nc.alloc_sbuf_tensor_at("q_f", [128, 16384], BF16, offset=a1)
    sb_k = nc.alloc_sbuf_tensor_at("k_f", [128, 16384], BF16, offset=a1 + 32768)
    sb_vt = nc.alloc_sbuf_tensor("v_t", [128, 16384], BF16)
    a2 = nc.lookup_mloc(sb_vt).addr
    sb_af = nc.alloc_sbuf_tensor_at("a_f", [128, 16384], BF16, offset=a2)
    sb_t2 = nc.alloc_sbuf_tensor_at("t2", [128, 16384], BF16, offset=a2)
    sb_at = nc.alloc_sbuf_tensor("a_t", [128, 16384], BF16)
    a3 = nc.lookup_mloc(sb_at).addr
    sb_Cq = nc.alloc_sbuf_tensor_at("Cq", [128, 2048], F32, offset=a3)
    sb_Sq = nc.alloc_sbuf_tensor_at("Sq", [128, 2048], F32, offset=a3 + 8192)
    sb_Ck = nc.alloc_sbuf_tensor_at("Ck", [128, 2048], F32, offset=a3 + 16384)
    sb_Sk = nc.alloc_sbuf_tensor_at("Sk", [128, 2048], F32, offset=a3 + 24576)
    sb_t1 = nc.alloc_sbuf_tensor_at("t1", [128, 16384], BF16, offset=a3)
    R4 = 45056
    sb_r4 = nc.alloc_sbuf_tensor("R4", [128, R4], U8)
    a4 = nc.lookup_mloc(sb_r4).addr

    def r4(name, shape, dtype, off):
        nbytes = int(np.prod(shape[1:])) * mybir.dt.size(dtype)
        assert off + nbytes <= R4, (name, off, nbytes)
        return nc.alloc_sbuf_tensor_at(name, list(shape), dtype, offset=a4 + off)

    sb_xb = [r4(f"xbuf{i}", [128, 1024], F32, i * 4096) for i in range(2)]
    sb_wqs = [r4(f"wqkst{i}", [128, 1024], F32, 8192 + i * 4096) for i in range(2)]
    sb_sqd = r4("sqdump", [128, 1024], F32, 12288)
    sb_wqb = [r4(f"wqkbf{i}", [128, 1024], BF16, 16384 + i * 2048) for i in range(2)]
    sb_wvb = r4("wv_bf", [128, 8192], BF16, 20480)
    sb_wvs = r4("wv_st", [128, 1024], F32, 36864)
    sb_swp = r4("swp", [128, 512], F32, 40960)
    sb_tr1 = r4("tr1", [128, 512], F32, 43008)
    sb_q2 = r4("q2buf", [128, 2048], BF16, 40960)
    sb_cvA = [r4(f"cvA{i}", [128, 2048], BF16, i * 4096) for i in range(2)]
    sb_cvC = [r4(f"cvC{i}", [128, 2048], BF16, 8192 + i * 4096) for i in range(2)]
    sb_T1 = r4("T1", [128, N_SLOT * 384], F32, 16384)
    sb_Pe = r4("Pe", [128, N_SLOT * 384], BF16, 25600)
    sb_PeT = r4("PeT", [128, N_SLOT * 384], BF16, 30208)
    sb_slab = [r4(f"slab{i}", [128, 6144], BF16, i * 6144) for i in range(3)]
    sb_gb = [r4(f"gbuf{i}", [128, 512], F32, 18432 + i * 2048) for i in range(2)]
    sb_pb = [r4(f"pbuf{i}", [128, 512], F32, 22528 + i * 2048) for i in range(2)]
    sb_ob = [r4(f"outb{i}", [128, 1024], F32, 26624 + i * 4096) for i in range(4)]

    psf = [nc.alloc_psum_tensor(f"psf{i}", [128, 512], F32) for i in range(6)]
    psb = [nc.alloc_psum_tensor(f"psb{i}", [128, 1024], BF16) for i in range(2)]

    B = _Buf
    b_const = B("const")
    b_cs = B("cs_tables")
    b_sca = B("sca")
    b_sb2 = B("sb2")
    b_xb = [B(f"xb{i}") for i in range(2)]
    b_lnst = [B(f"lnst{i}") for i in range(2)]
    b_sqd = B("sqd")
    b_xnf = [[B(f"xnf{c}_{q}") for q in range(4)] for c in range(8)]
    b_wqs = [B(f"wqs{i}") for i in range(2)]
    b_wqb = [B(f"wqb{i}") for i in range(2)]
    b_wvs = B("wvs")
    b_wvb = B("wvb")
    b_q = [B(f"q{c}") for c in range(8)]
    b_k = [B(f"k{c}") for c in range(8)]
    b_vt = [B(f"vt{t}") for t in range(NTT)]
    b_swp = B("swp")
    b_tr1 = B("tr1")
    b_q2 = B("q2")
    b_psf = [B(f"psf{i}") for i in range(6)]
    b_psb = [B(f"psb{i}") for i in range(2)]
    b_T1 = [B(f"T1_{s}") for s in range(N_SLOT)]
    b_Pe = [B(f"Pe_{s}") for s in range(N_SLOT)]
    b_PeT = [B(f"PeT_{s}") for s in range(N_SLOT)]
    b_smx = [B(f"smx{s}") for s in range(N_SLOT)]
    b_mng = [B(f"mng{s}") for s in range(N_SLOT)]
    b_recsb = [B(f"recsb{s}") for s in range(N_SLOT)]
    b_at = [B(f"at{t}") for t in range(NTT)]
    b_af = [B(f"af{c}") for c in range(8)]
    b_st = [B(f"st{t}") for t in range(NTT)]
    b_cvA = [B(f"cvA{i}") for i in range(2)]
    b_cvC = [B(f"cvC{i}") for i in range(2)]
    b_w0s = [B(f"w0s{c}") for c in range(8)]
    b_w1s = [B(f"w1s{c}") for c in range(32)]
    b_w2s = [B(f"w2s{c}") for c in range(32)]

    b_slab = [B(f"slab{i}") for i in range(3)]
    b_t1 = [B(f"t1_{m}") for m in range(32)]
    b_t2 = [B(f"t2_{m}") for m in range(32)]
    b_gb = [B(f"gb{i}") for i in range(2)]
    b_pb = [B(f"pb{i}") for i in range(2)]
    b_ob = [B(f"ob{i}") for i in range(4)]
    b_out = B("out_dram")
    b_dbg = B("dbg")

    def dma(q, dst, src, reads=(), writes=(), pwrites=(), tag=""):
        return sch.op(q, lambda e, d=dst, s=src: e.dma_start(d, s),
                      reads=reads, writes=writes, pwrites=pwrites, dma=True,
                      tag=tag)

    def mm(out, lhsT, rhs, start, stop, reads=(), writes=(), pwrites=(), tag=""):
        return sch.op("tensor",
                      lambda e, o=out, l=lhsT, r=rhs, st=start, sp=stop:
                      e.matmul(o, l, r, start=st, stop=sp),
                      reads=reads, writes=writes, pwrites=pwrites, tag=tag)

    def transp(out, inp, ident, reads=(), writes=(), pwrites=(), tag=""):
        return sch.op("tensor",
                      lambda e, o=out, i=inp, d=ident: e.transpose(o, i, d),
                      reads=reads, writes=writes, pwrites=pwrites, tag=tag)

    sb_konst = dict(ident_f32=sb_idf, ident_bf=sb_idb, Cq=sb_Cq, Sq=sb_Sq,
                    Ck=sb_Ck, Sk=sb_Sk, maskT=sb_mask, ones_bf=sb_1bf,
                    neg1=sb_neg1, onesc2=sb_oc2, one11=sb_one11, onesc1=sb_oc1)

    def ldconst(name):
        buf = b_cs if name[0] in "CS" and name != "maskT" else b_const
        dma("sync", sb_konst[name][:], dconst[name].ap(),
            pwrites=(buf,), tag=f"ld_{name}")

    for t in range(2):
        dma("sync", sb_xb[t][:], dx.ap()[t * 128:(t + 1) * 128, :],
            writes=(b_xb[t],), tag=f"ldx{t}")
    ldconst("ident_f32")
    for kc in range(8):
        dma("sync", sb_wvs[:], dwv.ap()[kc], writes=(b_wvs,), tag=f"ldwv{kc}")
        sch.op("scalar",
               lambda e, kc=kc: e.copy(sb_wvb[:, kc * 1024:(kc + 1) * 1024],
                                       sb_wvs[:]),
               reads=(b_wvs,), pwrites=(b_wvb,), tag=f"cwv{kc}")
    for m in range(2):
        dma("sync", sb_wqs[m][:], dwqk.ap()[m], writes=(b_wqs[m],),
            tag=f"ldwqk{m}")
        sch.op("scalar", lambda e, m=m: e.copy(sb_wqb[m][:], sb_wqs[m][:]),
               reads=(b_wqs[m],), writes=(b_wqb[m],), tag=f"cwqk{m}")
    for name, _, _dt in _CONST_SPECS:
        if name != "ident_f32":
            ldconst(name)
    for j in range(3):
        dma("sync", sb_sca[0:1, j:j + 1], dsc[j].ap(), pwrites=(b_sca,),
            tag=f"ld_s{j}")
    for j in range(3):
        mm(psf[5][:, j:j + 1], sb_oc2[:, :], sb_sca[0:1, j:j + 1], True, True,
           reads=(b_const, b_sca),
           writes=(b_psf[5],) if j == 0 else (), pwrites=(b_psf[5],) if j else (),
           tag=f"bc_s{j}")
    sch.op("vector", lambda e: e.tensor_copy(sb_sb2[:, 0:3], psf[5][:, 0:3]),
           reads=(b_psf[5],), writes=(b_sb2,), tag="ev_s012")
    b_sqd.absorb([b_wqs[1]])

    xnf_r = sb_xn[:].rearrange("p (c n) -> p c n", c=8)
    for t in range(NTT):
        xb = sb_xb[t % 2]
        bxb = b_xb[t % 2]
        sl = t % 2
        bst = b_lnst[sl]
        st_c = sb_lnst[:, sl * 6: sl * 6 + 6]
        xs, mu, vs, v1, ln1, inv = (st_c[:, i:i + 1] for i in range(6))
        if t >= 2:
            dma("sync", xb[:], dx.ap()[t * 128:(t + 1) * 128, :], writes=(bxb,),
                tag=f"ldx{t}")
        sch.op("vector", lambda e, xb=xb, xs=xs: e.reduce_sum(
            xs, xb[:], axis=AX.X),
               reads=(bxb,), writes=(bst,), tag=f"ln_sum{t}")
        sch.op("vector", lambda e, mu=mu, xs=xs: e.tensor_scalar_mul(
            mu, xs, 1.0 / DM),
               reads=(bst,), pwrites=(bst,), tag=f"ln_mu{t}")
        sch.op("vector", lambda e, xb=xb, mu=mu: e.tensor_scalar(
            xb[:], xb[:], mu, None, op0=ALU.subtract),
               reads=(bst,), writes=(bxb,), tag=f"ln_sub{t}")
        sch.op("scalar", lambda e, xb=xb, vs=vs: e.activation(
            sb_sqd[:], xb[:], AF.Square, accum_out=vs),
               reads=(bxb,), writes=(b_sqd,), pwrites=(bst,), tag=f"ln_sq{t}")
        sch.op("vector", lambda e, v1=v1, vs=vs: e.tensor_scalar(
            v1, vs, 1.0 / DM, 1e-5, op0=ALU.mult, op1=ALU.add),
               reads=(bst,), pwrites=(bst,), tag=f"ln_v1{t}")
        sch.op("scalar", lambda e, ln1=ln1, v1=v1: e.activation(
            ln1, v1, AF.Ln),
               reads=(bst,), pwrites=(bst,), tag=f"ln_ln{t}")
        sch.op("scalar", lambda e, inv=inv, ln1=ln1: e.activation(
            inv, ln1, AF.Exp, scale=-0.5),
               reads=(bst,), pwrites=(bst,), tag=f"ln_inv{t}")
        sch.op("vector", lambda e, xb=xb, inv=inv: e.tensor_scalar_mul(
            xb[:], xb[:], inv),
               reads=(bst,), writes=(bxb,), tag=f"ln_fin{t}")
        for g in range(2):
            bank = psf[4 + g]
            bb = b_psf[4 + g]
            for qq in range(4):
                cchunk = g * 4 + qq
                transp(bank[:, qq * 128:(qq + 1) * 128],
                       xb[:, cchunk * 128:(cchunk + 1) * 128], sb_idf[:],
                       reads=(bxb, b_const),
                       writes=(bb,) if qq == 0 else (),
                       pwrites=(bb,) if qq else (),
                       tag=f"tp{t}_{cchunk}")
            dst = xnf_r[:, g * 4:(g + 1) * 4, t * 128:(t + 1) * 128]
            src = bank[:].rearrange("p (c n) -> p c n", c=4)
            sch.op("scalar", lambda e, d=dst, s=src: e.copy(d, s),
                   reads=(bb,),
                   pwrites=tuple(b_xnf[g * 4 + c][t // 4] for c in range(4)),
                   tag=f"tpev{t}_{g}")
    b_wqs[1].absorb([b_sqd])

    swmask = []
    for i in range(16):
        swmask += [2 * i + 1, 2 * i]
    for m in range(16):
        stg, bstg = sb_wqs[m % 2], b_wqs[m % 2]
        wb, bwb = sb_wqb[m % 2], b_wqb[m % 2]
        if m >= 2:
            dma("sync", stg[:], dwqk.ap()[m], writes=(bstg,), tag=f"ldwqk{m}")
            sch.op("scalar", lambda e, wb=wb, stg=stg: e.copy(wb[:], stg[:]),
                   reads=(bstg,), writes=(bwb,), tag=f"cwqk{m}")
        isq = m < 8
        ctab, stab = (sb_Cq, sb_Sq) if isq else (sb_Ck, sb_Sk)
        dstt = sb_q if isq else sb_k
        bdst = b_q[m] if isq else b_k[m - 8]
        chunk = m if isq else m - 8
        for nt in range(4):
            bank, bb = psf[nt], b_psf[nt]
            for kc in range(8):
                mm(bank[:, 0:512], wb[:, kc * 128:(kc + 1) * 128],
                   sb_xn[:, kc * 2048 + nt * 512: kc * 2048 + nt * 512 + 512],
                   kc == 0, kc == 7,
                   reads=(bwb, b_xnf[kc][nt]),
                   writes=(bb,) if kc == 0 else (), pwrites=(bb,) if kc else (),
                   tag=f"qk{m}_{nt}_{kc}")
            cs = slice(nt * 512, nt * 512 + 512)
            sch.op("vector", lambda e, bank=bank: e.stream_shuffle(
                sb_swp[:], bank[:, 0:512], swmask),
                   reads=(bb,), writes=(b_swp,), tag=f"rp_sh{m}_{nt}")
            sch.op("vector", lambda e, bank=bank, ctab=ctab, cs=cs: e.tensor_mul(
                sb_tr1[:], bank[:, 0:512], ctab[:, cs]),
                   reads=(bb, b_cs), writes=(b_tr1,), tag=f"rp_c{m}_{nt}")
            sch.op("vector", lambda e, stab=stab, cs=cs: e.tensor_mul(
                sb_swp[:], sb_swp[:], stab[:, cs]),
                   reads=(b_swp, b_cs), writes=(b_swp,), tag=f"rp_s{m}_{nt}")
            dst = dstt[:, chunk * 2048 + nt * 512: chunk * 2048 + nt * 512 + 512]
            sch.op("vector", lambda e, d=dst: e.tensor_add(d, sb_tr1[:], sb_swp[:]),
                   reads=(b_tr1, b_swp), pwrites=(bdst,), tag=f"rp_o{m}_{nt}")
    for tt in range(NTT):
        for og in range(2):
            bank = psf[4 + ((tt * 2 + og) % 2)]
            bb = b_psf[4 + ((tt * 2 + og) % 2)]
            for kc in range(8):
                mm(bank[:, 0:512],
                   sb_xn[:, kc * 2048 + tt * 128: kc * 2048 + tt * 128 + 128],
                   sb_wvb[:, kc * 1024 + og * 512: kc * 1024 + og * 512 + 512],
                   kc == 0, kc == 7,
                   reads=(b_xnf[kc][tt // 4], b_wvb),
                   writes=(bb,) if kc == 0 else (), pwrites=(bb,) if kc else (),
                   tag=f"v{tt}_{og}_{kc}")
            dst = sb_vt[:, tt * 1024 + og * 512: tt * 1024 + og * 512 + 512]
            sch.op("scalar", lambda e, d=dst, bank=bank: e.copy(d, bank[:, 0:512]),
                   reads=(bb,), pwrites=(b_vt[tt],), tag=f"vev{tt}_{og}")

    if "xn_f" in ddbg:
        dma("sync", ddbg["xn_f"].ap(), sb_xn[:], reads=tuple(b for row in b_xnf for b in row), pwrites=(b_dbg,), tag="dbg_xn")
    if "q_f" in ddbg:
        dma("sync", ddbg["q_f"].ap(), sb_q[:], reads=tuple(b_q), pwrites=(b_dbg,), tag="dbg_q")
    if "k_f" in ddbg:
        dma("sync", ddbg["k_f"].ap(), sb_k[:], reads=tuple(b_k), pwrites=(b_dbg,), tag="dbg_k")
    if "v_t" in ddbg:
        dma("sync", ddbg["v_t"].ap(), sb_vt[:], reads=tuple(b_vt), pwrites=(b_dbg,), tag="dbg_v")

    if stage < 2:
        sch.fin([b_dbg])
        return _emit(nc, sch)
    b_cvA[0].absorb([b_xb[0], b_xb[1]])
    b_cvA[1].absorb([b_xb[1], b_wqs[0]])
    b_cvC[0].absorb([b_wqs[0], b_wqs[1], b_sqd])
    b_cvC[1].absorb([b_wqs[1], b_sqd])
    for s in range(N_SLOT):
        b_T1[s].absorb([b_wqb[0], b_wqb[1], b_wvb])
        b_Pe[s].absorb([b_wvb])
        b_PeT[s].absorb([b_wvb])

    conv_steps = []

    def conv_half(src_ap, dst_ap, wbuf, idx, src2_ap=None, tagn="", n=2048):
        ia, ic = idx % 2, idx % 2

        def s1():
            dma("gpsimd", sb_cvA[ia][:, 0:n], src_ap, writes=(b_cvA[ia],),
                tag=f"cva{tagn}")

        def s1b():
            dma("gpsimd", sb_cvC[ic][:, 0:n], src2_ap, writes=(b_cvC[ic],),
                tag=f"cvb{tagn}")

        def s2sub():
            sch.op("vector", lambda e: e.tensor_sub(
                sb_cvA[ia][:, 0:n], sb_cvA[ia][:, 0:n], sb_cvC[ic][:, 0:n]),
                   reads=(b_cvA[ia], b_cvC[ic]), writes=(b_cvA[ia],),
                   tag=f"cvs{tagn}")

        def s2aff():
            sch.op("vector", lambda e: e.tensor_scalar(
                sb_cvA[ia][:, 0:n], sb_cvA[ia][:, 0:n], 2.0, -1.0,
                op0=ALU.mult, op1=ALU.add),
                   reads=(b_cvA[ia],), writes=(b_cvA[ia],), tag=f"cvf{tagn}")

        def s3():
            dma("sync", dst_ap, sb_cvA[ia][:, 0:n], reads=(b_cvA[ia],),
                writes=(wbuf,), tag=f"cvw{tagn}")
        if src2_ap is not None:
            return [s1, s1b, s2sub, s3]
        return [s1, s2aff, s3]

    ci = 0
    for kc in range(8):
        for h in range(2):
            conv_steps.extend(conv_half(
                dk0a.ap()[kc, :, h * 2048:(h + 1) * 2048],
                dw0s.ap()[kc, :, h * 2048:(h + 1) * 2048],
                b_w0s[kc], ci,
                src2_ap=dk0b.ap()[kc, :, h * 2048:(h + 1) * 2048],
                tagn=f"w0_{kc}_{h}"))
            ci += 1
    for kc in range(32):
        for h in range(2):
            conv_steps.extend(conv_half(
                dk1.ap()[kc, :, h * 2048:(h + 1) * 2048],
                dw1s.ap()[kc, :, h * 2048:(h + 1) * 2048],
                b_w1s[kc], ci, tagn=f"w1_{kc}_{h}"))
            ci += 1
    for kc in range(32):
        conv_steps.extend(conv_half(
            dk2.ap()[kc], dw2s.ap()[kc], b_w2s[kc], ci, tagn=f"w2_{kc}", n=1024))
        ci += 1
    conv_iter = iter(conv_steps)

    def emit_conv(k):
        for _ in range(k):
            step = next(conv_iter, None)
            if step is None:
                return
            step()

    for t in range(NTT):
        b_at[t].absorb([b_cs])
    units = [(i, h) for i in range(NTT) for h in range(NH)]
    NU = len(units)

    def uctx(u):
        i, h = units[u]
        slot = u % N_SLOT
        jj0 = max(0, 2 - i)
        return dict(i=i, h=h, slot=slot, bank=psf[slot], bbank=b_psf[slot],
                    pbank=psb[u % 2], bpbank=b_psb[u % 2],
                    c=h // 2, off=(h % 2) * 64, jj0=jj0,
                    js=list(range(jj0, 3)))

    def st_mm(u):
        t = uctx(u)
        i, h, c, off, jj0 = t["i"], t["h"], t["c"], t["off"], t["jj0"]
        w = (3 - jj0) * 128
        k0c = (i - 2 + jj0) * 128
        mm(t["bank"][:, jj0 * 128:384],
           sb_q[off:off + 64, c * 2048 + i * 128: c * 2048 + i * 128 + 128],
           sb_k[off:off + 64, c * 2048 + k0c: c * 2048 + k0c + w],
           True, True, reads=(b_k[c], b_q[c]), writes=(t["bbank"],),
           tag=f"sq{i}_{h}")

    def st_mskmx(u):
        t = uctx(u)
        i, h, slot, jj0 = t["i"], t["h"], t["slot"], t["jj0"]
        t1s = sb_T1[:, slot * 384 + jj0 * 128: slot * 384 + 384]
        sch.op("vector", lambda e, bank=t["bank"], t1s=t1s, jj0=jj0: e.tensor_add(
            t1s, bank[:, jj0 * 128:384], sb_mask[:, jj0 * 128:384]),
               reads=(t["bbank"], b_const), writes=(b_T1[slot],), tag=f"msk{i}_{h}")
        sch.op("vector", lambda e, t1s=t1s, slot=slot: e.reduce_max(
            sb_mng[:, slot:slot + 1], t1s, axis=AX.X, negate=True),
               reads=(b_T1[slot],), writes=(b_mng[slot],), tag=f"mx{i}_{h}")

    def st_exp(u):
        t = uctx(u)
        i, h, slot, jj0 = t["i"], t["h"], t["slot"], t["jj0"]
        t1s = sb_T1[:, slot * 384 + jj0 * 128: slot * 384 + 384]
        pes = sb_Pe[:, slot * 384 + jj0 * 128: slot * 384 + 384]
        sch.op("scalar", lambda e, t1s=t1s, pes=pes, slot=slot: e.activation(
            pes, t1s, AF.Exp, bias=sb_mng[:, slot:slot + 1],
            accum_out=sb_smx[:, slot:slot + 1]),
               reads=(b_T1[slot], b_mng[slot]),
               writes=(b_Pe[slot], b_smx[slot]), tag=f"exp{i}_{h}")

    def st_tp(u):
        t = uctx(u)
        i, h, slot, jj0 = t["i"], t["h"], t["slot"], t["jj0"]
        for jj in t["js"]:
            transp(t["pbank"][:, jj * 128:(jj + 1) * 128],
                   sb_Pe[:, slot * 384 + jj * 128: slot * 384 + jj * 128 + 128],
                   sb_idb[:],
                   reads=(b_Pe[slot], b_const),
                   writes=(t["bpbank"],) if jj == jj0 else (),
                   pwrites=(t["bpbank"],) if jj != jj0 else (),
                   tag=f"pt{i}_{h}_{jj}")

    def st_rcp(u):
        t = uctx(u)
        i, h, slot = t["i"], t["h"], t["slot"]
        sch.op("vector", lambda e, slot=slot: e.reciprocal(
            sb_rec[:, slot:slot + 1], sb_smx[:, slot:slot + 1]),
               reads=(b_smx[slot],), writes=(b_recsb[slot],), tag=f"rcp{i}_{h}")

    def st_ptev(u):
        t = uctx(u)
        i, h, slot, jj0 = t["i"], t["h"], t["slot"], t["jj0"]
        pets = sb_PeT[:, slot * 384 + jj0 * 128: slot * 384 + 384]
        sch.op("scalar", lambda e, pets=pets, pbank=t["pbank"], jj0=jj0: e.copy(
            pets, pbank[:, jj0 * 128:384]),
               reads=(t["bpbank"],), writes=(b_PeT[slot],), tag=f"ptev{i}_{h}")

    def st_av(u):
        t = uctx(u)
        i, h, slot, jj0 = t["i"], t["h"], t["slot"], t["jj0"]
        for jj in t["js"]:
            jt = i - 2 + jj
            mm(t["bank"][:, 384:448],
               sb_PeT[:, slot * 384 + jj * 128: slot * 384 + jj * 128 + 128],
               sb_vt[:, jt * 1024 + h * 64: jt * 1024 + h * 64 + 64],
               jj == jj0, jj == 2,
               reads=(b_PeT[slot], b_vt[jt]), pwrites=(t["bbank"],),
               tag=f"av{i}_{h}_{jj}")

    def st_aev(u):
        t = uctx(u)
        i, h, slot = t["i"], t["h"], t["slot"]
        dst = sb_at[:, i * 1024 + h * 64: i * 1024 + h * 64 + 64]
        sch.op("scalar",
               lambda e, d=dst, bank=t["bank"], slot=slot: e.activation(
                   d, bank[:, 384:448], AF.Copy, scale=sb_rec[:, slot:slot + 1]),
               reads=(t["bbank"], b_recsb[slot]), pwrites=(b_at[i],),
               tag=f"aev{i}_{h}")

    stages = [(6, st_aev), (5, st_av), (4, st_ptev), (3, st_rcp), (3, st_tp),
              (2, st_exp), (1, st_mskmx), (0, st_mm)]
    per_it = len(conv_steps) // (NU + 7) + 1
    for it in range(NU + 7):
        emit_conv(per_it)
        for lag, fn in stages:
            u = it - lag
            if 0 <= u < NU:
                fn(u)
    emit_conv(10 ** 9)

    if "a_t" in ddbg:
        dma("sync", ddbg["a_t"].ap(), sb_at[:], reads=tuple(b_at),
            pwrites=(b_dbg,), tag="dbg_at")
    if stage < 3:
        sch.fin([b_dbg])
        return _emit(nc, sch)

    for c in range(8):
        b_af[c].absorb([b_vt[2 * c], b_vt[2 * c + 1]])
    for c in range(8):
        for half in range(2):
            bank, bb = psb[half], b_psb[half]
            for t8 in range(8):
                tt = half * 8 + t8
                transp(bank[:, t8 * 128:(t8 + 1) * 128],
                       sb_at[:, tt * 1024 + c * 128: tt * 1024 + c * 128 + 128],
                       sb_idb[:],
                       reads=(b_at[tt], b_const),
                       writes=(bb,) if t8 == 0 else (),
                       pwrites=(bb,) if t8 else (),
                       tag=f"atp{c}_{tt}")
            dst = sb_af[:, c * 2048 + half * 1024: c * 2048 + half * 1024 + 1024]
            sch.op("vector", lambda e, d=dst, bank=bank: e.tensor_copy(d, bank[:]),
                   reads=(bb,), pwrites=(b_af[c],), tag=f"atev{c}_{half}")

    if "a_f" in ddbg:
        dma("sync", ddbg["a_f"].ap(), sb_af[:], reads=tuple(b_af), pwrites=(b_dbg,), tag="dbg_af")

    for c8 in range(8):
        b_st[c8].absorb([b_q[c8], b_k[c8]])
    for mo in range(8):
        stg, bstg = sb_wqs[mo % 2], b_wqs[mo % 2]
        wb, bwb = sb_wqb[mo % 2], b_wqb[mo % 2]
        dma("sync", stg[:], dwo.ap()[mo], writes=(bstg,), tag=f"ldwo{mo}")
        sch.op("scalar", lambda e, wb=wb, stg=stg: e.copy(wb[:], stg[:]),
               reads=(bstg,), writes=(bwb,), tag=f"cwo{mo}")
        for nt in range(4):
            bank = psf[(mo * 4 + nt) % 4]
            bb = b_psf[(mo * 4 + nt) % 4]
            for kc in range(8):
                mm(bank[:, 0:512], wb[:, kc * 128:(kc + 1) * 128],
                   sb_af[:, kc * 2048 + nt * 512: kc * 2048 + nt * 512 + 512],
                   kc == 0, kc == 7,
                   reads=(bwb, b_af[kc]),
                   writes=(bb,) if kc == 0 else (), pwrites=(bb,) if kc else (),
                   tag=f"op{mo}_{nt}_{kc}")
            dst = sb_st[:, mo * 2048 + nt * 512: mo * 2048 + nt * 512 + 512]
            sch.op("vector", lambda e, d=dst, bank=bank: e.tensor_copy(
                d, bank[:, 0:512]),
                   reads=(bb,), pwrites=(b_st[mo],), tag=f"opev{mo}_{nt}")

    if "s_t" in ddbg:
        dma("sync", ddbg["s_t"].ap(), sb_st[:], reads=tuple(b_st[:8]), pwrites=(b_dbg,), tag="dbg_s")
    if "w0s" in ddbg:
        dma("sync", ddbg["w0s"].ap(), dw0s.ap(), reads=tuple(b_w0s), pwrites=(b_dbg,), tag="dbg_w0s")
    if "w1s" in ddbg:
        dma("sync", ddbg["w1s"].ap(), dw1s.ap(), reads=tuple(b_w1s), pwrites=(b_dbg,), tag="dbg_w1s")

    if stage < 4:
        sch.fin([b_dbg])
        return _emit(nc, sch)

    for m in range(32):
        b_t1[m].absorb([b_at[m // 2]])
        b_t2[m].absorb([b_af[m // 4]])
    for i2 in range(3):
        b_slab[i2].absorb([b_cvA[0], b_cvA[1], b_cvC[0], b_cvC[1],
                           b_wqs[0], b_wqs[1], b_wqb[0], b_wqb[1]] + b_T1)
    for i2 in range(2):
        b_gb[i2].absorb([b_T1[4], b_T1[5], b_wqb[0], b_wqb[1]])
        b_pb[i2].absorb(b_T1 + b_Pe)
    for i2 in range(4):
        b_ob[i2].absorb(b_Pe + b_PeT + [b_wvs, b_swp, b_tr1])

    def rounds(total, size):
        out = []
        m0 = 0
        while m0 < total:
            out.append(list(range(m0, min(m0 + size, total))))
            m0 += size
        return out

    slab_i = [0]

    def next_slab():
        slab_i[0] = (slab_i[0] + 1) % 3
        return sb_slab[slab_i[0]], b_slab[slab_i[0]]

    bank_set = [0]

    def next_banks(n):
        bank_set[0] ^= 1
        base = bank_set[0] * 3
        return [base + j for j in range(n)]

    def f0_round_emitters(nt):
        outs = []
        for rnd in rounds(32, 3):
            def em(rnd=rnd, nt=nt):
                w = len(rnd)
                banks = next_banks(w)
                slab, bslab = next_slab()
                src = dw0s.ap()[:, :, rnd[0] * 128: rnd[0] * 128 + w * 128]
                src = src.rearrange("c p f -> p c f")
                dst = slab[:, 0:8 * w * 128].rearrange("p (c f) -> p c f", c=8)
                dma("sync", dst, src, reads=tuple(b_w0s), writes=(bslab,),
                    tag=f"f0ld{nt}_{rnd[0]}")
                for kc in range(8):
                    for widx, m1 in enumerate(rnd):
                        bk = banks[widx]
                        mm(psf[bk][:, 0:512],
                           slab[:, kc * w * 128 + widx * 128:
                                kc * w * 128 + widx * 128 + 128],
                           sb_xn[:, kc * 2048 + nt * 512: kc * 2048 + nt * 512 + 512],
                           kc == 0, kc == 7,
                           reads=(bslab, b_xnf[kc][nt]),
                           writes=(b_psf[bk],) if kc == 0 else (),
                           pwrites=(b_psf[bk],) if kc else (),
                           tag=f"f0mm{nt}_{m1}_{kc}")
                for widx, m1 in enumerate(rnd):
                    bk = banks[widx]
                    dst2 = sb_t1[:, m1 * 512:(m1 + 1) * 512]
                    sch.op("vector", lambda e, d=dst2, bk=bk: e.tensor_scalar(
                        d, psf[bk][:, 0:512], sb_sb2[:, 0:1], 0.0,
                        op0=ALU.mult, op1=ALU.max),
                           reads=(b_psf[bk], b_sb2), writes=(b_t1[m1],),
                           tag=f"f0ev{nt}_{m1}")
            outs.append(em)
        return outs

    def f1_emit(nt):
        for rnd in rounds(32, 3):
            w = len(rnd)
            banks = next_banks(w)
            for g in range(4):
                slab, bslab = next_slab()
                src = dw1s.ap()[g * 8:(g + 1) * 8, :,
                                rnd[0] * 128: rnd[0] * 128 + w * 128]
                src = src.rearrange("c p f -> p c f")
                dst = slab[:, 0:8 * w * 128].rearrange("p (c f) -> p c f", c=8)
                dma("sync", dst, src,
                    reads=tuple(b_w1s[g * 8:(g + 1) * 8]), writes=(bslab,),
                    tag=f"f1ld{nt}_{rnd[0]}_{g}")
                for kci in range(8):
                    kc = g * 8 + kci
                    for widx, m2 in enumerate(rnd):
                        bk = banks[widx]
                        mm(psf[bk][:, 0:512],
                           slab[:, kci * w * 128 + widx * 128:
                                kci * w * 128 + widx * 128 + 128],
                           sb_t1[:, kc * 512:(kc + 1) * 512],
                           kc == 0, kc == 31,
                           reads=(bslab, b_t1[kc]),
                           writes=(b_psf[bk],) if kc == 0 else (),
                           pwrites=(b_psf[bk],) if kc else (),
                           tag=f"f1mm{nt}_{m2}_{kc}")
            for widx, m2 in enumerate(rnd):
                bk = banks[widx]
                dst = sb_t2[:, m2 * 512:(m2 + 1) * 512]
                sch.op("vector", lambda e, d=dst, bk=bk: e.tensor_scalar(
                    d, psf[bk][:, 0:512], sb_sb2[:, 1:2], 0.0,
                    op0=ALU.mult, op1=ALU.max),
                       reads=(b_psf[bk], b_sb2), writes=(b_t2[m2],),
                       tag=f"f1ev{nt}_{m2}")

    def f2_round_emitters(nt):
        outs = []

        def em_pre(nt=nt):
            for j in range(4):
                gt = nt * 4 + j
                dma("sync", sb_ob[j][:], dx.ap()[gt * 128:(gt + 1) * 128, :],
                    writes=(b_ob[j],), tag=f"ldxo{nt}_{j}")
        outs.append(em_pre)
        for rnd in rounds(8, 3):
            def em(rnd=rnd, nt=nt):
                w = len(rnd)
                banks = next_banks(w)
                for g in range(4):
                    slab, bslab = next_slab()
                    src = dw2s.ap()[g * 8:(g + 1) * 8, :,
                                    rnd[0] * 128: rnd[0] * 128 + w * 128]
                    src = src.rearrange("c p f -> p c f")
                    dst = slab[:, 0:8 * w * 128].rearrange("p (c f) -> p c f", c=8)
                    dma("sync", dst, src,
                        reads=tuple(b_w2s[g * 8:(g + 1) * 8]), writes=(bslab,),
                        tag=f"f2ld{nt}_{rnd[0]}_{g}")
                    for kci in range(8):
                        kc = g * 8 + kci
                        for widx, m3 in enumerate(rnd):
                            bk = banks[widx]
                            mm(psf[bk][:, 0:512],
                               slab[:, kci * w * 128 + widx * 128:
                                    kci * w * 128 + widx * 128 + 128],
                               sb_t2[:, kc * 512:(kc + 1) * 512],
                               kc == 0, kc == 31,
                               reads=(bslab, b_t2[kc]),
                               writes=(b_psf[bk],) if kc == 0 else (),
                               pwrites=(b_psf[bk],) if kc else (),
                               tag=f"f2mm{nt}_{m3}_{kc}")
                for widx, m3 in enumerate(rnd):
                    bk = banks[widx]
                    gb, bgb = sb_gb[m3 % 2], b_gb[m3 % 2]
                    pb, bpb = sb_pb[m3 % 2], b_pb[m3 % 2]
                    sch.op("scalar", lambda e, gb=gb, bk=bk: e.activation(
                        gb[:], psf[bk][:, 0:512], AF.Silu, scale=sb_sb2[:, 2:3]),
                           reads=(b_psf[bk], b_sb2), writes=(bgb,),
                           tag=f"silu{nt}_{m3}")
                    ssl = sb_st[:, m3 * 2048 + nt * 512: m3 * 2048 + nt * 512 + 512]
                    sch.op("vector", lambda e, gb=gb, pb=pb, ssl=ssl: e.tensor_mul(
                        pb[:], gb[:], ssl),
                           reads=(bgb, b_st[m3]), writes=(bpb,),
                           tag=f"prod{nt}_{m3}")
                    for j in range(4):
                        transp(psf[bk][:, j * 128:(j + 1) * 128],
                               pb[:, j * 128:(j + 1) * 128], sb_idf[:],
                               reads=(bpb, b_const),
                               writes=(b_psf[bk],) if j == 0 else (),
                               pwrites=(b_psf[bk],) if j else (),
                               tag=f"ptr{nt}_{m3}_{j}")
                    for j in range(4):
                        osl = sb_ob[j][:, m3 * 128:(m3 + 1) * 128]
                        sch.op("vector",
                               lambda e, osl=osl, bk=bk, j=j: e.tensor_add(
                                   osl, psf[bk][:, j * 128:(j + 1) * 128], osl),
                               reads=(b_psf[bk], b_ob[j]), pwrites=(b_ob[j],),
                               tag=f"radd{nt}_{m3}_{j}")
            outs.append(em)

        def em_post(nt=nt):
            for j in range(4):
                gt = nt * 4 + j
                dma("sync", dout.ap()[gt * 128:(gt + 1) * 128, :], sb_ob[j][:],
                    reads=(b_ob[j],), pwrites=(b_out,), tag=f"stout{nt}_{j}")
        outs.append(em_post)
        return outs

    pending_f2 = []
    for nt in range(4):
        for em in pending_f2:
            em()
        for em in f0_round_emitters(nt):
            em()
        f1_emit(nt)
        if nt == 0 and "t1" in ddbg:
            dma("sync", ddbg["t1"].ap(), sb_t1[:], reads=tuple(b_t1),
                pwrites=(b_dbg,), tag="dbg_t1")
        if nt == 0 and "t2" in ddbg:
            dma("sync", ddbg["t2"].ap(), sb_t2[:], reads=tuple(b_t2),
                pwrites=(b_dbg,), tag="dbg_t2")
        pending_f2 = f2_round_emitters(nt)
    for em in pending_f2:
        em()

    sch.fin([b_out, b_dbg])
    return _emit(nc, sch)


def _emit(nc, sch):
    sch.finalize()
    sems = {}
    for eng in _Sched.ENGINES:
        sems[(eng, None)] = nc.alloc_semaphore(f"sem_{eng}")
        for ln in range(_Sched.NLANE):
            sems[(eng, ln)] = nc.alloc_semaphore(f"sem_{eng}_d{ln}")

    with nc.Block() as block:
        @block.tensor
        def _(e):
            sch.run("tensor", e, sems)

        @block.vector
        def _(e):
            sch.run("vector", e, sems)

        @block.scalar
        def _(e):
            sch.run("scalar", e, sems)

        @block.gpsimd
        def _(e):
            sch.run("gpsimd", e, sems)

        @block.sync
        def _(e):
            sch.run("sync", e, sems)

    return nc


def _prep(L0_attn_query, L0_attn_in_proj_w, L0_attn_out_proj_w,
          L0_ffn_k0, L0_ffn_k1, L0_ffn_k2, L0_ffn_s0, L0_ffn_s1, L0_ffn_s2):
    x = np.ascontiguousarray(np.asarray(L0_attn_query, np.float32))
    win = np.ascontiguousarray(np.asarray(L0_attn_in_proj_w, np.float32))
    wout = np.ascontiguousarray(np.asarray(L0_attn_out_proj_w, np.float32))
    wqk = np.ascontiguousarray(
        win[:2048].T.reshape(8, 128, 16, 128).transpose(2, 1, 0, 3))
    wv = np.ascontiguousarray(win[2048:].T.reshape(8, 128, 1024))
    wo = np.ascontiguousarray(
        wout.T.reshape(8, 128, 8, 128).transpose(2, 1, 0, 3))
    k0 = np.ascontiguousarray(np.asarray(L0_ffn_k0)).view(np.uint8)
    k0a = np.ascontiguousarray(k0[:1024].reshape(8, 128, 4096))
    k0b = np.ascontiguousarray(k0[1024:].reshape(8, 128, 4096))
    k1 = np.ascontiguousarray(np.asarray(L0_ffn_k1)).view(np.uint8)
    k1 = k1.reshape(32, 128, 4096)
    k2 = np.ascontiguousarray(np.asarray(L0_ffn_k2)).view(np.uint8)
    k2 = k2.reshape(32, 128, 1024)
    shared = dict(wqk=wqk, wv=wv, wo=wo, k0a=k0a, k0b=k0b, k1=k1, k2=k2,
                  s0=np.asarray(L0_ffn_s0, np.float32).reshape(1, 1),
                  s1=np.asarray(L0_ffn_s1, np.float32).reshape(1, 1),
                  s2=np.asarray(L0_ffn_s2, np.float32).reshape(1, 1))
    for name, arr in _host_constants().items():
        shared["c_" + name] = np.ascontiguousarray(arr)
    return x, shared


def kernel(L0_attn_query, L0_attn_in_proj_w, L0_attn_out_proj_w,
           L0_ffn_k0, L0_ffn_k1, L0_ffn_k2, L0_ffn_s0, L0_ffn_s1, L0_ffn_s2):
    x, shared = _prep(L0_attn_query, L0_attn_in_proj_w, L0_attn_out_proj_w,
                      L0_ffn_k0, L0_ffn_k1, L0_ffn_k2,
                      L0_ffn_s0, L0_ffn_s1, L0_ffn_s2)
    nc = build_nc()
    in_maps = [dict(shared, x=np.ascontiguousarray(x[c])) for c in range(8)]
    res = run_bass_kernel_spmd(nc, in_maps, core_ids=list(range(8)))
    return np.stack([np.asarray(r["out"], np.float32) for r in res.results],
                    axis=0)


# version 14
# speedup vs baseline: 1.3692x; 1.0185x over previous
import numpy as np
import ml_dtypes
import concourse.bass as bass
import concourse.mybir as mybir
from concourse.bass_utils import run_bass_kernel_spmd

F32 = mybir.dt.float32
BF16 = mybir.dt.bfloat16
U8 = mybir.dt.uint8
AF = mybir.ActivationFunctionType
ALU = mybir.AluOpType
AX = mybir.AxisListType
BFD = ml_dtypes.bfloat16

T_SEQ = 2048
DM = 1024
NH = 16
NTT = 16
NEG_BIG = -1.0e30
N_SLOT = 6

class _Rec:
    __slots__ = ("eng", "dma", "thunk", "deps", "marked", "value", "waits",
                 "seq", "tag", "lane", "lane_pred")

    def __init__(self, eng, dma, thunk, seq, tag):
        self.eng = eng
        self.dma = dma
        self.thunk = thunk
        self.deps = []
        self.marked = False
        self.value = None
        self.waits = None
        self.seq = seq
        self.tag = tag
        self.lane = None
        self.lane_pred = None


class _Buf:
    __slots__ = ("name", "writers", "readers")

    def __init__(self, name):
        self.name = name
        self.writers = []
        self.readers = []

    def absorb(self, others):
        for o in others:
            self.writers = self.writers + o.writers + o.readers


class _Sched:
    ENGINES = ("tensor", "vector", "scalar", "gpsimd", "sync")
    NLANE = 8

    def __init__(self):
        self.recs = {e: [] for e in self.ENGINES}
        self.nseq = 0
        self.dma_count = {e: 0 for e in self.ENGINES}
        self.lane_last = {}

    def op(self, eng, thunk, reads=(), writes=(), pwrites=(), dma=False, tag=""):
        r = _Rec(eng, dma, thunk, self.nseq, tag)
        self.nseq += 1
        if dma:
            lane = self.dma_count[eng] % self.NLANE
            self.dma_count[eng] += 1
            r.lane = lane
            r.lane_pred = self.lane_last.get((eng, lane))
            self.lane_last[(eng, lane)] = r
        deps = {}
        for b in reads:
            for d in b.writers:
                deps[id(d)] = d
        for b in tuple(writes) + tuple(pwrites):
            for d in b.writers:
                deps[id(d)] = d
            for d in b.readers:
                deps[id(d)] = d
        if dma and r.lane_pred is not None:
            deps[id(r.lane_pred)] = r.lane_pred
        for d in deps.values():
            if d is r:
                continue
            if (not d.dma) and (not dma) and d.eng == eng == "tensor":
                continue
            assert d.seq < r.seq, (d.tag, r.tag)
            r.deps.append(d)
        for b in reads:
            b.readers.append(r)
        for b in writes:
            b.writers = [r]
            b.readers = []
        for b in pwrites:
            b.writers.append(r)
        self.recs[eng].append(r)
        return r

    def finalize(self):
        for lst in self.recs.values():
            for r in lst:
                for d in r.deps:
                    d.marked = True
                if r.dma:
                    r.marked = True
        for lst in self.recs.values():
            cnt = {}
            for r in lst:
                if r.marked:
                    key = ("d", r.lane) if r.dma else "c"
                    cnt[key] = cnt.get(key, 0) + 1
                    r.value = cnt[key] * (16 if r.dma else 1)
        for lst in self.recs.values():
            hw = {}
            for r in lst:
                ws = []
                for d in r.deps:
                    key = (d.eng, d.lane) if d.dma else (d.eng, None)
                    assert d.value is not None
                    if hw.get(key, 0) < d.value:
                        ws.append((key, d.value))
                        hw[key] = d.value
                ws.sort(key=lambda kv: kv[1])
                r.waits = ws

    def fin(self, bufs, tag="fin"):
        r = self.op("sync", lambda e: e.nop(), reads=tuple(bufs), tag=tag)
        have = {id(d) for d in r.deps}
        for rec in self.lane_last.values():
            if id(rec) not in have:
                r.deps.append(rec)
                have.add(id(rec))
        return r

    def run(self, eng_name, eng, sems):
        for r in self.recs[eng_name]:
            for key, val in r.waits:
                eng.wait_ge(sems[key], val)
            ins = r.thunk(eng)
            if r.marked:
                assert ins is not None, r.tag
                key = (eng_name, r.lane) if r.dma else (eng_name, None)
                ins.then_inc(sems[key], 16 if r.dma else 1)


def _host_constants():
    c = {}
    c["ident_f32"] = np.eye(128, dtype=np.float32)
    c["ident_bf"] = np.eye(128, dtype=np.float32).astype(BFD)
    i_of_r = (np.arange(128) % 64) // 2
    freqs = np.exp(-np.log(10000.0) * i_of_r / 32.0)
    t = np.arange(T_SEQ, dtype=np.float64)
    ang = freqs[:, None] * t[None, :]
    cos = np.cos(ang)
    sin = np.sin(ang)
    sgn = np.where(np.arange(128) % 2 == 0, -1.0, 1.0)[:, None]
    c["Cq"] = (0.125 * cos).astype(np.float32)
    c["Sq"] = (0.125 * sgn * sin).astype(np.float32)
    c["Ck"] = cos.astype(np.float32)
    c["Sk"] = (sgn * sin).astype(np.float32)
    tq = np.arange(128)[:, None]
    cw = np.arange(384)[None, :]
    valid = (cw > tq + 6) & (cw <= tq + 256)
    c["maskT"] = np.where(valid, 0.0, NEG_BIG).astype(np.float32)
    c["ones_bf"] = np.ones((128, 1), np.float32).astype(BFD)
    c["neg1"] = np.full((128, 1), -1.0, np.float32)
    c["onesc2"] = np.full((1, 128), 2.0, np.float32)
    c["one11"] = np.ones((1, 1), np.float32)
    c["onesc1"] = np.ones((1, 128), np.float32)
    return c


_CONST_SPECS = [
    ("ident_f32", (128, 128), F32), ("ident_bf", (128, 128), BF16),
    ("Cq", (128, 2048), F32), ("Sq", (128, 2048), F32),
    ("Ck", (128, 2048), F32), ("Sk", (128, 2048), F32),
    ("maskT", (128, 384), F32), ("ones_bf", (128, 1), BF16),
    ("neg1", (128, 1), F32),
    ("onesc2", (1, 128), F32), ("one11", (1, 1), F32),
    ("onesc1", (1, 128), F32),
]


def build_nc(debug=(), stage=4):
    nc = bass.Bass("TRN2", target_bir_lowering=False, debug=False)
    sch = _Sched()

    dx = nc.dram_tensor("x", [T_SEQ, DM], F32, kind="ExternalInput")
    dwqk = nc.dram_tensor("wqk", [16, 128, 8, 128], F32, kind="ExternalInput")
    dwv = nc.dram_tensor("wv", [8, 128, 1024], F32, kind="ExternalInput")
    dwo = nc.dram_tensor("wo", [8, 128, 8, 128], F32, kind="ExternalInput")
    dk0a = nc.dram_tensor("k0a", [8, 128, 4096], U8, kind="ExternalInput")
    dk0b = nc.dram_tensor("k0b", [8, 128, 4096], U8, kind="ExternalInput")
    dk1 = nc.dram_tensor("k1", [32, 128, 4096], U8, kind="ExternalInput")
    dk2 = nc.dram_tensor("k2", [32, 128, 1024], U8, kind="ExternalInput")
    dsc = [nc.dram_tensor(f"s{j}", [1, 1], F32, kind="ExternalInput")
           for j in range(3)]
    dconst = {name: nc.dram_tensor("c_" + name, list(shp), dt, kind="ExternalInput")
              for name, shp, dt in _CONST_SPECS}
    dout = nc.dram_tensor("out", [T_SEQ, DM], F32, kind="ExternalOutput")
    dw0s = nc.dram_tensor("w0s", [8, 128, 4096], BF16, kind="Internal")
    dw1s = nc.dram_tensor("w1s", [32, 128, 4096], BF16, kind="Internal")
    dw2s = nc.dram_tensor("w2s", [32, 128, 1024], BF16, kind="Internal")

    ddbg = {}
    for name, shp, dt in [
        ("xn_f", [128, 16384], BF16), ("q_f", [128, 16384], BF16),
        ("k_f", [128, 16384], BF16), ("v_t", [128, 16384], BF16),
        ("a_t", [128, 16384], BF16),
        ("a_f", [128, 16384], BF16), ("s_t", [128, 16384], F32),
        ("t1", [128, 16384], BF16), ("t2", [128, 16384], BF16),
        ("w0s", [8, 128, 4096], BF16), ("w1s", [32, 128, 4096], BF16),
    ]:
        if name in debug:
            ddbg[name] = nc.dram_tensor("dbg_" + name, shp, dt,
                                        kind="ExternalOutput")

    sb_xn = nc.alloc_sbuf_tensor("xn_f", [128, 16384], BF16)
    sb_mask = nc.alloc_sbuf_tensor("maskT", [128, 384], F32)
    sb_idf = nc.alloc_sbuf_tensor("ident_f32", [128, 128], F32)
    sb_idb = nc.alloc_sbuf_tensor("ident_bf", [128, 128], BF16)
    sb_1bf = nc.alloc_sbuf_tensor("ones_bf", [128, 1], BF16)
    sb_neg1 = nc.alloc_sbuf_tensor("neg1", [128, 1], F32)
    sb_oc2 = nc.alloc_sbuf_tensor("onesc2", [1, 128], F32)
    sb_one11 = nc.alloc_sbuf_tensor("one11", [1, 1], F32)
    sb_oc1 = nc.alloc_sbuf_tensor("onesc1", [1, 128], F32)
    sb_sca = nc.alloc_sbuf_tensor("scalars", [1, 4], F32)
    sb_sb2 = nc.alloc_sbuf_tensor("s012b2", [128, 4], F32)
    sb_lnst = nc.alloc_sbuf_tensor("lnst", [128, 12], F32)
    sb_rec = nc.alloc_sbuf_tensor("rec", [128, N_SLOT], F32)
    sb_smx = nc.alloc_sbuf_tensor("softmax_sums", [128, N_SLOT], F32)
    sb_mng = nc.alloc_sbuf_tensor("mneg", [128, N_SLOT], F32)

    sb_st = nc.alloc_sbuf_tensor("s_t", [128, 16384], F32)
    a1 = nc.lookup_mloc(sb_st).addr
    sb_q = nc.alloc_sbuf_tensor_at("q_f", [128, 16384], BF16, offset=a1)
    sb_k = nc.alloc_sbuf_tensor_at("k_f", [128, 16384], BF16, offset=a1 + 32768)
    sb_vt = nc.alloc_sbuf_tensor("v_t", [128, 16384], BF16)
    a2 = nc.lookup_mloc(sb_vt).addr
    sb_af = nc.alloc_sbuf_tensor_at("a_f", [128, 16384], BF16, offset=a2)
    sb_t2 = nc.alloc_sbuf_tensor_at("t2", [128, 16384], BF16, offset=a2)
    sb_at = nc.alloc_sbuf_tensor("a_t", [128, 16384], BF16)
    a3 = nc.lookup_mloc(sb_at).addr
    sb_Cq = nc.alloc_sbuf_tensor_at("Cq", [128, 2048], F32, offset=a3)
    sb_Sq = nc.alloc_sbuf_tensor_at("Sq", [128, 2048], F32, offset=a3 + 8192)
    sb_Ck = nc.alloc_sbuf_tensor_at("Ck", [128, 2048], F32, offset=a3 + 16384)
    sb_Sk = nc.alloc_sbuf_tensor_at("Sk", [128, 2048], F32, offset=a3 + 24576)
    sb_t1 = nc.alloc_sbuf_tensor_at("t1", [128, 16384], BF16, offset=a3)
    R4 = 45056
    sb_r4 = nc.alloc_sbuf_tensor("R4", [128, R4], U8)
    a4 = nc.lookup_mloc(sb_r4).addr

    def r4(name, shape, dtype, off):
        nbytes = int(np.prod(shape[1:])) * mybir.dt.size(dtype)
        assert off + nbytes <= R4, (name, off, nbytes)
        return nc.alloc_sbuf_tensor_at(name, list(shape), dtype, offset=a4 + off)

    sb_xb = [r4(f"xbuf{i}", [128, 1024], F32, i * 4096) for i in range(2)]
    sb_wqs = [r4(f"wqkst{i}", [128, 1024], F32, 8192 + i * 4096) for i in range(2)]
    sb_sqd = r4("sqdump", [128, 1024], F32, 12288)
    sb_wqb = [r4(f"wqkbf{i}", [128, 1024], BF16, 16384 + i * 2048) for i in range(2)]
    sb_wvb = r4("wv_bf", [128, 8192], BF16, 20480)
    sb_wvs = r4("wv_st", [128, 1024], F32, 36864)
    sb_swp = [r4("swp", [128, 512], F32, 40960),
              r4("swp1", [128, 512], F32, 36864)]
    sb_tr1 = [r4("tr1", [128, 512], F32, 43008),
              r4("tr11", [128, 512], F32, 38912)]
    sb_cvA = [r4(f"cvA{i}", [128, 2048], BF16, i * 4096) for i in range(2)]
    sb_cvC = [r4(f"cvC{i}", [128, 2048], BF16, 8192 + i * 4096) for i in range(2)]
    sb_T1 = r4("T1", [128, N_SLOT * 384], F32, 16384)
    sb_Pe = r4("Pe", [128, N_SLOT * 384], BF16, 25600)
    sb_PeT = r4("PeT", [128, N_SLOT * 384], BF16, 30208)
    sb_slab = [r4(f"slab{i}", [128, 6144], BF16, i * 6144) for i in range(3)]
    sb_gb = [r4(f"gbuf{i}", [128, 512], F32, 18432 + i * 2048) for i in range(2)]
    sb_pb = [r4(f"pbuf{i}", [128, 512], F32, 22528 + i * 2048) for i in range(2)]
    sb_ob = [r4(f"outb{i}", [128, 1024], F32, 26624 + i * 4096) for i in range(4)]

    psf = [nc.alloc_psum_tensor(f"psf{i}", [128, 512], F32) for i in range(6)]
    psb = [nc.alloc_psum_tensor(f"psb{i}", [128, 1024], BF16) for i in range(2)]

    B = _Buf
    b_const = B("const")
    b_cs = B("cs_tables")
    b_sca = B("sca")
    b_sb2 = B("sb2")
    b_xb = [B(f"xb{i}") for i in range(2)]
    b_lnst = [B(f"lnst{i}") for i in range(2)]
    b_sqd = B("sqd")
    b_xnf = [[B(f"xnf{c}_{q}") for q in range(4)] for c in range(8)]
    b_wqs = [B(f"wqs{i}") for i in range(2)]
    b_wqb = [B(f"wqb{i}") for i in range(2)]
    b_wvs = B("wvs")
    b_wvb = B("wvb")
    b_q = [B(f"q{c}") for c in range(8)]
    b_k = [B(f"k{c}") for c in range(8)]
    b_vt = [B(f"vt{t}") for t in range(NTT)]
    b_swp = [B("swp0"), B("swp1")]
    b_tr1 = [B("tr10"), B("tr11")]
    b_psf = [B(f"psf{i}") for i in range(6)]
    b_psb = [B(f"psb{i}") for i in range(2)]
    b_T1 = [B(f"T1_{s}") for s in range(N_SLOT)]
    b_Pe = [B(f"Pe_{s}") for s in range(N_SLOT)]
    b_PeT = [B(f"PeT_{s}") for s in range(N_SLOT)]
    b_smx = [B(f"smx{s}") for s in range(N_SLOT)]
    b_mng = [B(f"mng{s}") for s in range(N_SLOT)]
    b_recsb = [B(f"recsb{s}") for s in range(N_SLOT)]
    b_at = [B(f"at{t}") for t in range(NTT)]
    b_af = [B(f"af{c}") for c in range(8)]
    b_st = [B(f"st{t}") for t in range(NTT)]
    b_cvA = [B(f"cvA{i}") for i in range(2)]
    b_cvC = [B(f"cvC{i}") for i in range(2)]
    b_w0s = [B(f"w0s{c}") for c in range(8)]
    b_w1s = [B(f"w1s{c}") for c in range(32)]
    b_w2s = [B(f"w2s{c}") for c in range(32)]

    b_slab = [B(f"slab{i}") for i in range(3)]
    b_t1 = [B(f"t1_{m}") for m in range(32)]
    b_t2 = [B(f"t2_{m}") for m in range(32)]
    b_gb = [B(f"gb{i}") for i in range(2)]
    b_pb = [B(f"pb{i}") for i in range(2)]
    b_ob = [B(f"ob{i}") for i in range(4)]
    b_out = B("out_dram")
    b_dbg = B("dbg")

    def dma(q, dst, src, reads=(), writes=(), pwrites=(), tag=""):
        return sch.op(q, lambda e, d=dst, s=src: e.dma_start(d, s),
                      reads=reads, writes=writes, pwrites=pwrites, dma=True,
                      tag=tag)

    def mm(out, lhsT, rhs, start, stop, reads=(), writes=(), pwrites=(), tag=""):
        return sch.op("tensor",
                      lambda e, o=out, l=lhsT, r=rhs, st=start, sp=stop:
                      e.matmul(o, l, r, start=st, stop=sp),
                      reads=reads, writes=writes, pwrites=pwrites, tag=tag)

    def transp(out, inp, ident, reads=(), writes=(), pwrites=(), tag=""):
        return sch.op("tensor",
                      lambda e, o=out, i=inp, d=ident: e.transpose(o, i, d),
                      reads=reads, writes=writes, pwrites=pwrites, tag=tag)

    sb_konst = dict(ident_f32=sb_idf, ident_bf=sb_idb, Cq=sb_Cq, Sq=sb_Sq,
                    Ck=sb_Ck, Sk=sb_Sk, maskT=sb_mask, ones_bf=sb_1bf,
                    neg1=sb_neg1, onesc2=sb_oc2, one11=sb_one11, onesc1=sb_oc1)

    def ldconst(name):
        buf = b_cs if name[0] in "CS" and name != "maskT" else b_const
        dma("sync", sb_konst[name][:], dconst[name].ap(),
            pwrites=(buf,), tag=f"ld_{name}")

    for t in range(2):
        dma("sync", sb_xb[t][:], dx.ap()[t * 128:(t + 1) * 128, :],
            writes=(b_xb[t],), tag=f"ldx{t}")
    ldconst("ident_f32")
    for kc in range(8):
        dma("sync", sb_wvs[:], dwv.ap()[kc], writes=(b_wvs,), tag=f"ldwv{kc}")
        sch.op("scalar",
               lambda e, kc=kc: e.copy(sb_wvb[:, kc * 1024:(kc + 1) * 1024],
                                       sb_wvs[:]),
               reads=(b_wvs,), pwrites=(b_wvb,), tag=f"cwv{kc}")
    for m in range(2):
        dma("sync", sb_wqs[m][:], dwqk.ap()[m], writes=(b_wqs[m],),
            tag=f"ldwqk{m}")
        sch.op("scalar", lambda e, m=m: e.copy(sb_wqb[m][:], sb_wqs[m][:]),
               reads=(b_wqs[m],), writes=(b_wqb[m],), tag=f"cwqk{m}")
    for name, _, _dt in _CONST_SPECS:
        if name != "ident_f32":
            ldconst(name)
    for j in range(3):
        dma("sync", sb_sca[0:1, j:j + 1], dsc[j].ap(), pwrites=(b_sca,),
            tag=f"ld_s{j}")
    for j in range(3):
        mm(psf[5][:, j:j + 1], sb_oc2[:, :], sb_sca[0:1, j:j + 1], True, True,
           reads=(b_const, b_sca),
           writes=(b_psf[5],) if j == 0 else (), pwrites=(b_psf[5],) if j else (),
           tag=f"bc_s{j}")
    sch.op("vector", lambda e: e.tensor_copy(sb_sb2[:, 0:3], psf[5][:, 0:3]),
           reads=(b_psf[5],), writes=(b_sb2,), tag="ev_s012")
    b_sqd.absorb([b_wqs[1]])

    xnf_r = sb_xn[:].rearrange("p (c n) -> p c n", c=8)
    for t in range(NTT):
        xb = sb_xb[t % 2]
        bxb = b_xb[t % 2]
        sl = t % 2
        bst = b_lnst[sl]
        st_c = sb_lnst[:, sl * 6: sl * 6 + 6]
        xs, mu, vs, v1, ln1, inv = (st_c[:, i:i + 1] for i in range(6))
        if t >= 2:
            dma("sync", xb[:], dx.ap()[t * 128:(t + 1) * 128, :], writes=(bxb,),
                tag=f"ldx{t}")
        sch.op("vector", lambda e, xb=xb, xs=xs: e.reduce_sum(
            xs, xb[:], axis=AX.X),
               reads=(bxb,), writes=(bst,), tag=f"ln_sum{t}")
        sch.op("vector", lambda e, mu=mu, xs=xs: e.tensor_scalar_mul(
            mu, xs, 1.0 / DM),
               reads=(bst,), pwrites=(bst,), tag=f"ln_mu{t}")
        sch.op("vector", lambda e, xb=xb, mu=mu: e.tensor_scalar(
            xb[:], xb[:], mu, None, op0=ALU.subtract),
               reads=(bst,), writes=(bxb,), tag=f"ln_sub{t}")
        sch.op("scalar", lambda e, xb=xb, vs=vs: e.activation(
            sb_sqd[:], xb[:], AF.Square, accum_out=vs),
               reads=(bxb,), writes=(b_sqd,), pwrites=(bst,), tag=f"ln_sq{t}")
        sch.op("vector", lambda e, v1=v1, vs=vs: e.tensor_scalar(
            v1, vs, 1.0 / DM, 1e-5, op0=ALU.mult, op1=ALU.add),
               reads=(bst,), pwrites=(bst,), tag=f"ln_v1{t}")
        sch.op("scalar", lambda e, ln1=ln1, v1=v1: e.activation(
            ln1, v1, AF.Ln),
               reads=(bst,), pwrites=(bst,), tag=f"ln_ln{t}")
        sch.op("scalar", lambda e, inv=inv, ln1=ln1: e.activation(
            inv, ln1, AF.Exp, scale=-0.5),
               reads=(bst,), pwrites=(bst,), tag=f"ln_inv{t}")
        sch.op("vector", lambda e, xb=xb, inv=inv: e.tensor_scalar_mul(
            xb[:], xb[:], inv),
               reads=(bst,), writes=(bxb,), tag=f"ln_fin{t}")
        for g in range(2):
            bank = psf[4 + g]
            bb = b_psf[4 + g]
            for qq in range(4):
                cchunk = g * 4 + qq
                transp(bank[:, qq * 128:(qq + 1) * 128],
                       xb[:, cchunk * 128:(cchunk + 1) * 128], sb_idf[:],
                       reads=(bxb, b_const),
                       writes=(bb,) if qq == 0 else (),
                       pwrites=(bb,) if qq else (),
                       tag=f"tp{t}_{cchunk}")
            dst = xnf_r[:, g * 4:(g + 1) * 4, t * 128:(t + 1) * 128]
            src = bank[:].rearrange("p (c n) -> p c n", c=4)
            sch.op("scalar", lambda e, d=dst, s=src: e.copy(d, s),
                   reads=(bb,),
                   pwrites=tuple(b_xnf[g * 4 + c][t // 4] for c in range(4)),
                   tag=f"tpev{t}_{g}")
    b_wqs[1].absorb([b_sqd])

    swmask = []
    for i in range(16):
        swmask += [2 * i + 1, 2 * i]
    b_swp[1].absorb([b_wvs])
    b_tr1[1].absorb([b_wvs])
    for m in range(16):
        stg, bstg = sb_wqs[m % 2], b_wqs[m % 2]
        wb, bwb = sb_wqb[m % 2], b_wqb[m % 2]
        if m >= 2:
            dma("sync", stg[:], dwqk.ap()[m], writes=(bstg,), tag=f"ldwqk{m}")
            sch.op("scalar", lambda e, wb=wb, stg=stg: e.copy(wb[:], stg[:]),
                   reads=(bstg,), writes=(bwb,), tag=f"cwqk{m}")
        isq = m < 8
        ctab, stab = (sb_Cq, sb_Sq) if isq else (sb_Ck, sb_Sk)
        dstt = sb_q if isq else sb_k
        bdst = b_q[m] if isq else b_k[m - 8]
        chunk = m if isq else m - 8
        for nt in range(4):
            bank, bb = psf[nt], b_psf[nt]
            for kc in range(8):
                mm(bank[:, 0:512], wb[:, kc * 128:(kc + 1) * 128],
                   sb_xn[:, kc * 2048 + nt * 512: kc * 2048 + nt * 512 + 512],
                   kc == 0, kc == 7,
                   reads=(bwb, b_xnf[kc][nt]),
                   writes=(bb,) if kc == 0 else (), pwrites=(bb,) if kc else (),
                   tag=f"qk{m}_{nt}_{kc}")
            cs = slice(nt * 512, nt * 512 + 512)
            rsl = (m * 4 + nt) % 2
            swp, bswp = sb_swp[rsl], b_swp[rsl]
            tr1, btr1 = sb_tr1[rsl], b_tr1[rsl]
            sch.op("vector", lambda e, bank=bank, swp=swp: e.stream_shuffle(
                swp[:], bank[:, 0:512], swmask),
                   reads=(bb,), writes=(bswp,), tag=f"rp_sh{m}_{nt}")
            sch.op("vector", lambda e, bank=bank, ctab=ctab, cs=cs, tr1=tr1:
                   e.tensor_mul(tr1[:], bank[:, 0:512], ctab[:, cs]),
                   reads=(bb, b_cs), writes=(btr1,), tag=f"rp_c{m}_{nt}")
            sch.op("vector", lambda e, stab=stab, cs=cs, swp=swp: e.tensor_mul(
                swp[:], swp[:], stab[:, cs]),
                   reads=(bswp, b_cs), writes=(bswp,), tag=f"rp_s{m}_{nt}")
            dst = dstt[:, chunk * 2048 + nt * 512: chunk * 2048 + nt * 512 + 512]
            sch.op("vector", lambda e, d=dst, tr1=tr1, swp=swp: e.tensor_add(
                d, tr1[:], swp[:]),
                   reads=(btr1, bswp), pwrites=(bdst,), tag=f"rp_o{m}_{nt}")
    for tt in range(NTT):
        for og in range(2):
            bank = psf[4 + ((tt * 2 + og) % 2)]
            bb = b_psf[4 + ((tt * 2 + og) % 2)]
            for kc in range(8):
                mm(bank[:, 0:512],
                   sb_xn[:, kc * 2048 + tt * 128: kc * 2048 + tt * 128 + 128],
                   sb_wvb[:, kc * 1024 + og * 512: kc * 1024 + og * 512 + 512],
                   kc == 0, kc == 7,
                   reads=(b_xnf[kc][tt // 4], b_wvb),
                   writes=(bb,) if kc == 0 else (), pwrites=(bb,) if kc else (),
                   tag=f"v{tt}_{og}_{kc}")
            dst = sb_vt[:, tt * 1024 + og * 512: tt * 1024 + og * 512 + 512]
            sch.op("scalar", lambda e, d=dst, bank=bank: e.copy(d, bank[:, 0:512]),
                   reads=(bb,), pwrites=(b_vt[tt],), tag=f"vev{tt}_{og}")

    if "xn_f" in ddbg:
        dma("sync", ddbg["xn_f"].ap(), sb_xn[:], reads=tuple(b for row in b_xnf for b in row), pwrites=(b_dbg,), tag="dbg_xn")
    if "q_f" in ddbg:
        dma("sync", ddbg["q_f"].ap(), sb_q[:], reads=tuple(b_q), pwrites=(b_dbg,), tag="dbg_q")
    if "k_f" in ddbg:
        dma("sync", ddbg["k_f"].ap(), sb_k[:], reads=tuple(b_k), pwrites=(b_dbg,), tag="dbg_k")
    if "v_t" in ddbg:
        dma("sync", ddbg["v_t"].ap(), sb_vt[:], reads=tuple(b_vt), pwrites=(b_dbg,), tag="dbg_v")

    if stage < 2:
        sch.fin([b_dbg])
        return _emit(nc, sch)
    b_cvA[0].absorb([b_xb[0], b_xb[1]])
    b_cvA[1].absorb([b_xb[1], b_wqs[0]])
    b_cvC[0].absorb([b_wqs[0], b_wqs[1], b_sqd])
    b_cvC[1].absorb([b_wqs[1], b_sqd])
    for s in range(N_SLOT):
        b_T1[s].absorb([b_wqb[0], b_wqb[1], b_wvb])
        b_Pe[s].absorb([b_wvb])
        b_PeT[s].absorb([b_wvb])

    conv_steps = []

    def conv_half(src_ap, dst_ap, wbuf, idx, src2_ap=None, tagn="", n=2048):
        ia, ic = idx % 2, idx % 2

        def s1():
            dma("gpsimd", sb_cvA[ia][:, 0:n], src_ap, writes=(b_cvA[ia],),
                tag=f"cva{tagn}")

        def s1b():
            dma("gpsimd", sb_cvC[ic][:, 0:n], src2_ap, writes=(b_cvC[ic],),
                tag=f"cvb{tagn}")

        def s2sub():
            sch.op("vector", lambda e: e.tensor_sub(
                sb_cvA[ia][:, 0:n], sb_cvA[ia][:, 0:n], sb_cvC[ic][:, 0:n]),
                   reads=(b_cvA[ia], b_cvC[ic]), writes=(b_cvA[ia],),
                   tag=f"cvs{tagn}")

        def s2aff():
            sch.op("vector", lambda e: e.tensor_scalar(
                sb_cvA[ia][:, 0:n], sb_cvA[ia][:, 0:n], 2.0, -1.0,
                op0=ALU.mult, op1=ALU.add),
                   reads=(b_cvA[ia],), writes=(b_cvA[ia],), tag=f"cvf{tagn}")

        def s3():
            dma("sync", dst_ap, sb_cvA[ia][:, 0:n], reads=(b_cvA[ia],),
                writes=(wbuf,), tag=f"cvw{tagn}")
        if src2_ap is not None:
            return [s1, s1b, s2sub, s3]
        return [s1, s2aff, s3]

    ci = 0
    for kc in range(8):
        for h in range(2):
            conv_steps.extend(conv_half(
                dk0a.ap()[kc, :, h * 2048:(h + 1) * 2048],
                dw0s.ap()[kc, :, h * 2048:(h + 1) * 2048],
                b_w0s[kc], ci,
                src2_ap=dk0b.ap()[kc, :, h * 2048:(h + 1) * 2048],
                tagn=f"w0_{kc}_{h}"))
            ci += 1
    for kc in range(32):
        for h in range(2):
            conv_steps.extend(conv_half(
                dk1.ap()[kc, :, h * 2048:(h + 1) * 2048],
                dw1s.ap()[kc, :, h * 2048:(h + 1) * 2048],
                b_w1s[kc], ci, tagn=f"w1_{kc}_{h}"))
            ci += 1
    for kc in range(32):
        conv_steps.extend(conv_half(
            dk2.ap()[kc], dw2s.ap()[kc], b_w2s[kc], ci, tagn=f"w2_{kc}", n=1024))
        ci += 1
    conv_iter = iter(conv_steps)

    def emit_conv(k):
        for _ in range(k):
            step = next(conv_iter, None)
            if step is None:
                return
            step()

    for t in range(NTT):
        b_at[t].absorb([b_cs])
    units = [(i, h) for i in range(NTT) for h in range(NH)]
    NU = len(units)

    def uctx(u):
        i, h = units[u]
        slot = u % N_SLOT
        jj0 = max(0, 2 - i)
        return dict(i=i, h=h, slot=slot, bank=psf[slot], bbank=b_psf[slot],
                    pbank=psb[u % 2], bpbank=b_psb[u % 2],
                    c=h // 2, off=(h % 2) * 64, jj0=jj0,
                    js=list(range(jj0, 3)))

    def st_mm(u):
        t = uctx(u)
        i, h, c, off, jj0 = t["i"], t["h"], t["c"], t["off"], t["jj0"]
        w = (3 - jj0) * 128
        k0c = (i - 2 + jj0) * 128
        mm(t["bank"][:, jj0 * 128:384],
           sb_q[off:off + 64, c * 2048 + i * 128: c * 2048 + i * 128 + 128],
           sb_k[off:off + 64, c * 2048 + k0c: c * 2048 + k0c + w],
           True, True, reads=(b_k[c], b_q[c]), writes=(t["bbank"],),
           tag=f"sq{i}_{h}")

    def st_mskmx(u):
        t = uctx(u)
        i, h, slot, jj0 = t["i"], t["h"], t["slot"], t["jj0"]
        t1s = sb_T1[:, slot * 384 + jj0 * 128: slot * 384 + 384]
        sch.op("vector", lambda e, bank=t["bank"], t1s=t1s, jj0=jj0: e.tensor_add(
            t1s, bank[:, jj0 * 128:384], sb_mask[:, jj0 * 128:384]),
               reads=(t["bbank"], b_const), writes=(b_T1[slot],), tag=f"msk{i}_{h}")
        sch.op("vector", lambda e, t1s=t1s, slot=slot: e.reduce_max(
            sb_mng[:, slot:slot + 1], t1s, axis=AX.X, negate=True),
               reads=(b_T1[slot],), writes=(b_mng[slot],), tag=f"mx{i}_{h}")

    def st_exp(u):
        t = uctx(u)
        i, h, slot, jj0 = t["i"], t["h"], t["slot"], t["jj0"]
        t1s = sb_T1[:, slot * 384 + jj0 * 128: slot * 384 + 384]
        pes = sb_Pe[:, slot * 384 + jj0 * 128: slot * 384 + 384]
        sch.op("scalar", lambda e, t1s=t1s, pes=pes, slot=slot: e.activation(
            pes, t1s, AF.Exp, bias=sb_mng[:, slot:slot + 1],
            accum_out=sb_smx[:, slot:slot + 1]),
               reads=(b_T1[slot], b_mng[slot]),
               writes=(b_Pe[slot], b_smx[slot]), tag=f"exp{i}_{h}")

    def st_tp(u):
        t = uctx(u)
        i, h, slot, jj0 = t["i"], t["h"], t["slot"], t["jj0"]
        for jj in t["js"]:
            transp(t["pbank"][:, jj * 128:(jj + 1) * 128],
                   sb_Pe[:, slot * 384 + jj * 128: slot * 384 + jj * 128 + 128],
                   sb_idb[:],
                   reads=(b_Pe[slot], b_const),
                   writes=(t["bpbank"],) if jj == jj0 else (),
                   pwrites=(t["bpbank"],) if jj != jj0 else (),
                   tag=f"pt{i}_{h}_{jj}")

    def st_rcp(u):
        t = uctx(u)
        i, h, slot = t["i"], t["h"], t["slot"]
        sch.op("vector", lambda e, slot=slot: e.reciprocal(
            sb_rec[:, slot:slot + 1], sb_smx[:, slot:slot + 1]),
               reads=(b_smx[slot],), writes=(b_recsb[slot],), tag=f"rcp{i}_{h}")

    def st_ptev(u):
        t = uctx(u)
        i, h, slot, jj0 = t["i"], t["h"], t["slot"], t["jj0"]
        pets = sb_PeT[:, slot * 384 + jj0 * 128: slot * 384 + 384]
        sch.op("scalar", lambda e, pets=pets, pbank=t["pbank"], jj0=jj0: e.copy(
            pets, pbank[:, jj0 * 128:384]),
               reads=(t["bpbank"],), writes=(b_PeT[slot],), tag=f"ptev{i}_{h}")

    def st_av(u):
        t = uctx(u)
        i, h, slot, jj0 = t["i"], t["h"], t["slot"], t["jj0"]
        for jj in t["js"]:
            jt = i - 2 + jj
            mm(t["bank"][:, 384:448],
               sb_PeT[:, slot * 384 + jj * 128: slot * 384 + jj * 128 + 128],
               sb_vt[:, jt * 1024 + h * 64: jt * 1024 + h * 64 + 64],
               jj == jj0, jj == 2,
               reads=(b_PeT[slot], b_vt[jt]), pwrites=(t["bbank"],),
               tag=f"av{i}_{h}_{jj}")

    def st_aev(u):
        t = uctx(u)
        i, h, slot = t["i"], t["h"], t["slot"]
        dst = sb_at[:, i * 1024 + h * 64: i * 1024 + h * 64 + 64]
        sch.op("scalar",
               lambda e, d=dst, bank=t["bank"], slot=slot: e.activation(
                   d, bank[:, 384:448], AF.Copy, scale=sb_rec[:, slot:slot + 1]),
               reads=(t["bbank"], b_recsb[slot]), pwrites=(b_at[i],),
               tag=f"aev{i}_{h}")

    stages = [(6, st_aev), (5, st_av), (4, st_ptev), (3, st_rcp), (3, st_tp),
              (2, st_exp), (1, st_mskmx), (0, st_mm)]
    per_it = len(conv_steps) // (NU + 7) + 1
    for it in range(NU + 7):
        emit_conv(per_it)
        for lag, fn in stages:
            u = it - lag
            if 0 <= u < NU:
                fn(u)
    emit_conv(10 ** 9)

    if "a_t" in ddbg:
        dma("sync", ddbg["a_t"].ap(), sb_at[:], reads=tuple(b_at),
            pwrites=(b_dbg,), tag="dbg_at")
    if stage < 3:
        sch.fin([b_dbg])
        return _emit(nc, sch)

    for c in range(8):
        b_af[c].absorb([b_vt[2 * c], b_vt[2 * c + 1]])
    for c in range(8):
        for half in range(2):
            bank, bb = psb[half], b_psb[half]
            for t8 in range(8):
                tt = half * 8 + t8
                transp(bank[:, t8 * 128:(t8 + 1) * 128],
                       sb_at[:, tt * 1024 + c * 128: tt * 1024 + c * 128 + 128],
                       sb_idb[:],
                       reads=(b_at[tt], b_const),
                       writes=(bb,) if t8 == 0 else (),
                       pwrites=(bb,) if t8 else (),
                       tag=f"atp{c}_{tt}")
            dst = sb_af[:, c * 2048 + half * 1024: c * 2048 + half * 1024 + 1024]
            sch.op("vector", lambda e, d=dst, bank=bank: e.tensor_copy(d, bank[:]),
                   reads=(bb,), pwrites=(b_af[c],), tag=f"atev{c}_{half}")

    if "a_f" in ddbg:
        dma("sync", ddbg["a_f"].ap(), sb_af[:], reads=tuple(b_af), pwrites=(b_dbg,), tag="dbg_af")

    for c8 in range(8):
        b_st[c8].absorb([b_q[c8], b_k[c8]])
    for mo in range(8):
        stg, bstg = sb_wqs[mo % 2], b_wqs[mo % 2]
        wb, bwb = sb_wqb[mo % 2], b_wqb[mo % 2]
        dma("sync", stg[:], dwo.ap()[mo], writes=(bstg,), tag=f"ldwo{mo}")
        sch.op("scalar", lambda e, wb=wb, stg=stg: e.copy(wb[:], stg[:]),
               reads=(bstg,), writes=(bwb,), tag=f"cwo{mo}")
        for nt in range(4):
            bank = psf[(mo * 4 + nt) % 4]
            bb = b_psf[(mo * 4 + nt) % 4]
            for kc in range(8):
                mm(bank[:, 0:512], wb[:, kc * 128:(kc + 1) * 128],
                   sb_af[:, kc * 2048 + nt * 512: kc * 2048 + nt * 512 + 512],
                   kc == 0, kc == 7,
                   reads=(bwb, b_af[kc]),
                   writes=(bb,) if kc == 0 else (), pwrites=(bb,) if kc else (),
                   tag=f"op{mo}_{nt}_{kc}")
            dst = sb_st[:, mo * 2048 + nt * 512: mo * 2048 + nt * 512 + 512]
            sch.op("vector", lambda e, d=dst, bank=bank: e.tensor_copy(
                d, bank[:, 0:512]),
                   reads=(bb,), pwrites=(b_st[mo],), tag=f"opev{mo}_{nt}")

    if "s_t" in ddbg:
        dma("sync", ddbg["s_t"].ap(), sb_st[:], reads=tuple(b_st[:8]), pwrites=(b_dbg,), tag="dbg_s")
    if "w0s" in ddbg:
        dma("sync", ddbg["w0s"].ap(), dw0s.ap(), reads=tuple(b_w0s), pwrites=(b_dbg,), tag="dbg_w0s")
    if "w1s" in ddbg:
        dma("sync", ddbg["w1s"].ap(), dw1s.ap(), reads=tuple(b_w1s), pwrites=(b_dbg,), tag="dbg_w1s")

    if stage < 4:
        sch.fin([b_dbg])
        return _emit(nc, sch)

    for m in range(32):
        b_t1[m].absorb([b_at[m // 2]])
        b_t2[m].absorb([b_af[m // 4]])
    for i2 in range(3):
        b_slab[i2].absorb([b_cvA[0], b_cvA[1], b_cvC[0], b_cvC[1],
                           b_wqs[0], b_wqs[1], b_wqb[0], b_wqb[1]] + b_T1)
    for i2 in range(2):
        b_gb[i2].absorb([b_T1[4], b_T1[5], b_wqb[0], b_wqb[1]])
        b_pb[i2].absorb(b_T1 + b_Pe)
    for i2 in range(4):
        b_ob[i2].absorb(b_Pe + b_PeT + [b_wvs] + b_swp + b_tr1)

    def rounds(total, size):
        out = []
        m0 = 0
        while m0 < total:
            out.append(list(range(m0, min(m0 + size, total))))
            m0 += size
        return out

    slab_i = [0]

    def next_slab():
        slab_i[0] = (slab_i[0] + 1) % 3
        return sb_slab[slab_i[0]], b_slab[slab_i[0]]

    bank_set = [0]

    def next_banks(n):
        bank_set[0] ^= 1
        base = bank_set[0] * 3
        return [base + j for j in range(n)]

    def f0_round_emitters(nt):
        outs = []
        for rnd in rounds(32, 3):
            def em(rnd=rnd, nt=nt):
                w = len(rnd)
                banks = next_banks(w)
                slab, bslab = next_slab()
                src = dw0s.ap()[:, :, rnd[0] * 128: rnd[0] * 128 + w * 128]
                src = src.rearrange("c p f -> p c f")
                dst = slab[:, 0:8 * w * 128].rearrange("p (c f) -> p c f", c=8)
                dma("sync", dst, src, reads=tuple(b_w0s), writes=(bslab,),
                    tag=f"f0ld{nt}_{rnd[0]}")
                for kc in range(8):
                    for widx, m1 in enumerate(rnd):
                        bk = banks[widx]
                        mm(psf[bk][:, 0:512],
                           slab[:, kc * w * 128 + widx * 128:
                                kc * w * 128 + widx * 128 + 128],
                           sb_xn[:, kc * 2048 + nt * 512: kc * 2048 + nt * 512 + 512],
                           kc == 0, kc == 7,
                           reads=(bslab, b_xnf[kc][nt]),
                           writes=(b_psf[bk],) if kc == 0 else (),
                           pwrites=(b_psf[bk],) if kc else (),
                           tag=f"f0mm{nt}_{m1}_{kc}")
                for widx, m1 in enumerate(rnd):
                    bk = banks[widx]
                    dst2 = sb_t1[:, m1 * 512:(m1 + 1) * 512]
                    sch.op("vector", lambda e, d=dst2, bk=bk: e.tensor_scalar(
                        d, psf[bk][:, 0:512], sb_sb2[:, 0:1], 0.0,
                        op0=ALU.mult, op1=ALU.max),
                           reads=(b_psf[bk], b_sb2), writes=(b_t1[m1],),
                           tag=f"f0ev{nt}_{m1}")
            outs.append(em)
        return outs

    def f1_emit(nt):
        for rnd in rounds(32, 3):
            w = len(rnd)
            banks = next_banks(w)
            for g in range(4):
                slab, bslab = next_slab()
                src = dw1s.ap()[g * 8:(g + 1) * 8, :,
                                rnd[0] * 128: rnd[0] * 128 + w * 128]
                src = src.rearrange("c p f -> p c f")
                dst = slab[:, 0:8 * w * 128].rearrange("p (c f) -> p c f", c=8)
                dma("sync", dst, src,
                    reads=tuple(b_w1s[g * 8:(g + 1) * 8]), writes=(bslab,),
                    tag=f"f1ld{nt}_{rnd[0]}_{g}")
                for kci in range(8):
                    kc = g * 8 + kci
                    for widx, m2 in enumerate(rnd):
                        bk = banks[widx]
                        mm(psf[bk][:, 0:512],
                           slab[:, kci * w * 128 + widx * 128:
                                kci * w * 128 + widx * 128 + 128],
                           sb_t1[:, kc * 512:(kc + 1) * 512],
                           kc == 0, kc == 31,
                           reads=(bslab, b_t1[kc]),
                           writes=(b_psf[bk],) if kc == 0 else (),
                           pwrites=(b_psf[bk],) if kc else (),
                           tag=f"f1mm{nt}_{m2}_{kc}")
            for widx, m2 in enumerate(rnd):
                bk = banks[widx]
                dst = sb_t2[:, m2 * 512:(m2 + 1) * 512]
                sch.op("vector", lambda e, d=dst, bk=bk: e.tensor_scalar(
                    d, psf[bk][:, 0:512], sb_sb2[:, 1:2], 0.0,
                    op0=ALU.mult, op1=ALU.max),
                       reads=(b_psf[bk], b_sb2), writes=(b_t2[m2],),
                       tag=f"f1ev{nt}_{m2}")

    def f2_round_emitters(nt):
        outs = []

        def em_pre(nt=nt):
            for j in range(4):
                gt = nt * 4 + j
                dma("sync", sb_ob[j][:], dx.ap()[gt * 128:(gt + 1) * 128, :],
                    writes=(b_ob[j],), tag=f"ldxo{nt}_{j}")
        outs.append(em_pre)
        for rnd in rounds(8, 3):
            def em(rnd=rnd, nt=nt):
                w = len(rnd)
                banks = next_banks(w)
                for g in range(4):
                    slab, bslab = next_slab()
                    src = dw2s.ap()[g * 8:(g + 1) * 8, :,
                                    rnd[0] * 128: rnd[0] * 128 + w * 128]
                    src = src.rearrange("c p f -> p c f")
                    dst = slab[:, 0:8 * w * 128].rearrange("p (c f) -> p c f", c=8)
                    dma("sync", dst, src,
                        reads=tuple(b_w2s[g * 8:(g + 1) * 8]), writes=(bslab,),
                        tag=f"f2ld{nt}_{rnd[0]}_{g}")
                    for kci in range(8):
                        kc = g * 8 + kci
                        for widx, m3 in enumerate(rnd):
                            bk = banks[widx]
                            mm(psf[bk][:, 0:512],
                               slab[:, kci * w * 128 + widx * 128:
                                    kci * w * 128 + widx * 128 + 128],
                               sb_t2[:, kc * 512:(kc + 1) * 512],
                               kc == 0, kc == 31,
                               reads=(bslab, b_t2[kc]),
                               writes=(b_psf[bk],) if kc == 0 else (),
                               pwrites=(b_psf[bk],) if kc else (),
                               tag=f"f2mm{nt}_{m3}_{kc}")
                for widx, m3 in enumerate(rnd):
                    bk = banks[widx]
                    gb, bgb = sb_gb[m3 % 2], b_gb[m3 % 2]
                    pb, bpb = sb_pb[m3 % 2], b_pb[m3 % 2]
                    sch.op("scalar", lambda e, gb=gb, bk=bk: e.activation(
                        gb[:], psf[bk][:, 0:512], AF.Silu, scale=sb_sb2[:, 2:3]),
                           reads=(b_psf[bk], b_sb2), writes=(bgb,),
                           tag=f"silu{nt}_{m3}")
                    ssl = sb_st[:, m3 * 2048 + nt * 512: m3 * 2048 + nt * 512 + 512]
                    sch.op("vector", lambda e, gb=gb, pb=pb, ssl=ssl: e.tensor_mul(
                        pb[:], gb[:], ssl),
                           reads=(bgb, b_st[m3]), writes=(bpb,),
                           tag=f"prod{nt}_{m3}")
                    for j in range(4):
                        transp(psf[bk][:, j * 128:(j + 1) * 128],
                               pb[:, j * 128:(j + 1) * 128], sb_idf[:],
                               reads=(bpb, b_const),
                               writes=(b_psf[bk],) if j == 0 else (),
                               pwrites=(b_psf[bk],) if j else (),
                               tag=f"ptr{nt}_{m3}_{j}")
                    for j in range(4):
                        osl = sb_ob[j][:, m3 * 128:(m3 + 1) * 128]
                        sch.op("vector",
                               lambda e, osl=osl, bk=bk, j=j: e.tensor_add(
                                   osl, psf[bk][:, j * 128:(j + 1) * 128], osl),
                               reads=(b_psf[bk], b_ob[j]), pwrites=(b_ob[j],),
                               tag=f"radd{nt}_{m3}_{j}")
            outs.append(em)

        def em_post(nt=nt):
            for j in range(4):
                gt = nt * 4 + j
                dma("sync", dout.ap()[gt * 128:(gt + 1) * 128, :], sb_ob[j][:],
                    reads=(b_ob[j],), pwrites=(b_out,), tag=f"stout{nt}_{j}")
        outs.append(em_post)
        return outs

    pending_f2 = []
    for nt in range(4):
        f0s = f0_round_emitters(nt)
        fi = iter(f0s)
        for em in pending_f2:
            em()
            nxt = next(fi, None)
            if nxt is not None:
                nxt()
        for em in fi:
            em()
        f1_emit(nt)
        if nt == 0 and "t1" in ddbg:
            dma("sync", ddbg["t1"].ap(), sb_t1[:], reads=tuple(b_t1),
                pwrites=(b_dbg,), tag="dbg_t1")
        if nt == 0 and "t2" in ddbg:
            dma("sync", ddbg["t2"].ap(), sb_t2[:], reads=tuple(b_t2),
                pwrites=(b_dbg,), tag="dbg_t2")
        pending_f2 = f2_round_emitters(nt)
    for em in pending_f2:
        em()

    sch.fin([b_out, b_dbg])
    return _emit(nc, sch)


def _emit(nc, sch):
    sch.finalize()
    sems = {}
    for eng in _Sched.ENGINES:
        sems[(eng, None)] = nc.alloc_semaphore(f"sem_{eng}")
        for ln in range(_Sched.NLANE):
            sems[(eng, ln)] = nc.alloc_semaphore(f"sem_{eng}_d{ln}")

    with nc.Block() as block:
        @block.tensor
        def _(e):
            sch.run("tensor", e, sems)

        @block.vector
        def _(e):
            sch.run("vector", e, sems)

        @block.scalar
        def _(e):
            sch.run("scalar", e, sems)

        @block.gpsimd
        def _(e):
            sch.run("gpsimd", e, sems)

        @block.sync
        def _(e):
            sch.run("sync", e, sems)

    return nc


def _prep(L0_attn_query, L0_attn_in_proj_w, L0_attn_out_proj_w,
          L0_ffn_k0, L0_ffn_k1, L0_ffn_k2, L0_ffn_s0, L0_ffn_s1, L0_ffn_s2):
    x = np.ascontiguousarray(np.asarray(L0_attn_query, np.float32))
    win = np.ascontiguousarray(np.asarray(L0_attn_in_proj_w, np.float32))
    wout = np.ascontiguousarray(np.asarray(L0_attn_out_proj_w, np.float32))
    wqk = np.ascontiguousarray(
        win[:2048].T.reshape(8, 128, 16, 128).transpose(2, 1, 0, 3))
    wv = np.ascontiguousarray(win[2048:].T.reshape(8, 128, 1024))
    wo = np.ascontiguousarray(
        wout.T.reshape(8, 128, 8, 128).transpose(2, 1, 0, 3))
    k0 = np.ascontiguousarray(np.asarray(L0_ffn_k0)).view(np.uint8)
    k0a = np.ascontiguousarray(k0[:1024].reshape(8, 128, 4096))
    k0b = np.ascontiguousarray(k0[1024:].reshape(8, 128, 4096))
    k1 = np.ascontiguousarray(np.asarray(L0_ffn_k1)).view(np.uint8)
    k1 = k1.reshape(32, 128, 4096)
    k2 = np.ascontiguousarray(np.asarray(L0_ffn_k2)).view(np.uint8)
    k2 = k2.reshape(32, 128, 1024)
    shared = dict(wqk=wqk, wv=wv, wo=wo, k0a=k0a, k0b=k0b, k1=k1, k2=k2,
                  s0=np.asarray(L0_ffn_s0, np.float32).reshape(1, 1),
                  s1=np.asarray(L0_ffn_s1, np.float32).reshape(1, 1),
                  s2=np.asarray(L0_ffn_s2, np.float32).reshape(1, 1))
    for name, arr in _host_constants().items():
        shared["c_" + name] = np.ascontiguousarray(arr)
    return x, shared


def kernel(L0_attn_query, L0_attn_in_proj_w, L0_attn_out_proj_w,
           L0_ffn_k0, L0_ffn_k1, L0_ffn_k2, L0_ffn_s0, L0_ffn_s1, L0_ffn_s2):
    x, shared = _prep(L0_attn_query, L0_attn_in_proj_w, L0_attn_out_proj_w,
                      L0_ffn_k0, L0_ffn_k1, L0_ffn_k2,
                      L0_ffn_s0, L0_ffn_s1, L0_ffn_s2)
    nc = build_nc()
    in_maps = [dict(shared, x=np.ascontiguousarray(x[c])) for c in range(8)]
    res = run_bass_kernel_spmd(nc, in_maps, core_ids=list(range(8)))
    return np.stack([np.asarray(r["out"], np.float32) for r in res.results],
                    axis=0)
